# version 18
# baseline (speedup 1.0000x reference)
import numpy as np
from contextlib import ExitStack

import concourse.bass as bass
import concourse.mybir as mybir
from concourse.bass_utils import run_bass_kernel_spmd
from concourse.alu_op_type import AluOpType as OP

F32 = mybir.dt.float32
BF16 = mybir.dt.bfloat16
AF = mybir.ActivationFunctionType
AX = mybir.AxisListType.X

TOK = 4096
P = 512
NB = TOK // P
E = 1024
NFC = E // 128
H = 16
D = 64
NJC = P // 128
INV_E = 1.0 / E
LN_EPS = 1e-5
ATT_EPS = 1e-5
INV_SQRT_G = 1.0 / np.sqrt(128.0)


class Sem:
    def __init__(self, nc, ctx, name):
        self.h = ctx.enter_context(nc.semaphore(name))
        self.n = 0

    def reserve(self, k=1):
        self.n += k
        return self.n

    def attach(self, inst, k=1):
        inst.then_inc(self.h, k)

    def inc(self, inst, k=1):
        inst.then_inc(self.h, k)
        self.n += k
        return self.n


class Dfr:

    def __init__(self, name):
        self.name = name
        self.q = []

    def add(self, fn):
        self.q.append(fn)

    def flush(self, blk):
        if not self.q:
            return
        q = self.q
        self.q = []

        def body(eng):
            for f in q:
                f(eng)
        getattr(blk, self.name)(body)


def build_nc(stop_after=None, dbg=False, dbg2=False, a_parts='lkvts'):
    nc = bass.Bass("TRN2", target_bir_lowering=False)

    xT = nc.dram_tensor("xT", [E, TOK], F32, kind="ExternalInput")
    wq_h = nc.dram_tensor("wq_h", [128, 8192], F32, kind="ExternalInput")
    wk_h = nc.dram_tensor("wk_h", [128, 8192], F32, kind="ExternalInput")
    wv_h = nc.dram_tensor("wv_h", [128, 8192], F32, kind="ExternalInput")
    wo_h = nc.dram_tensor("wo_h", [128, 8192], F32, kind="ExternalInput")
    bo_h = nc.dram_tensor("bo_h", [128, 8], F32, kind="ExternalInput")
    wr_hh = nc.dram_tensor("wr_hh", [128, 1024], F32, kind="ExternalInput")
    cT_h = nc.dram_tensor("cT_h", [128, 16], F32, kind="ExternalInput")
    temp_h = nc.dram_tensor("temp_h", [1, 1], F32, kind="ExternalInput")
    win_h = nc.dram_tensor("win_h", [128, 65536], F32, kind="ExternalInput")
    wout_hh = nc.dram_tensor("wout_hh", [128, 4096], F32, kind="ExternalInput")
    et_h = nc.dram_tensor("et_h", [16, 1024], F32, kind="ExternalInput")
    ident_h = nc.dram_tensor("ident_h", [128, 128], F32, kind="ExternalInput")
    maskd_h = nc.dram_tensor("maskd_h", [128, 128], F32, kind="ExternalInput")
    xpT = nc.dram_tensor("xpT", [E, TOK], F32, kind="ExternalInput")
    gm_h = nc.dram_tensor("gm_h", [128, 1], F32, kind="ExternalInput")
    outT = nc.dram_tensor("outT", [E, TOK], F32, kind="ExternalOutput")

    _sk = "ExternalOutput" if dbg else "Internal"
    ln1_scr = nc.dram_tensor("ln1_scr", [NB, 128, 4096], BF16, kind=_sk)
    phk_scr = nc.dram_tensor("phk_scr", [NB, 128, 4096], BF16, kind=_sk)
    v_scr = nc.dram_tensor("v_scr", [NB, 128, 4160], BF16, kind=_sk)
    y_scr = nc.dram_tensor("y_scr", [NB, 128, 4096], F32, kind=_sk)
    wbf_scr = nc.dram_tensor("wbf_scr", [16, 128, 4096], BF16,
                             kind="ExternalOutput" if dbg2 else "Internal")
    if dbg2:
        dbg_ffn2 = nc.dram_tensor("dbg_ffn2", [NB, 128, 4096], BF16,
                                  kind="ExternalOutput")
        dbg_g2 = nc.dram_tensor("dbg_g2", [NB, 16, 512], BF16,
                                kind="ExternalOutput")
        dbg_wb = nc.dram_tensor("dbg_wb", [2, 128, 4096], BF16,
                                kind="ExternalOutput")
    if dbg:
        dbg_ao = nc.dram_tensor("dbg_ao", [NB, 128, 4096], F32,
                                kind="ExternalOutput")
        dbg_u = nc.dram_tensor("dbg_u", [NB, 128, 4096], BF16,
                               kind="ExternalOutput")
        dbg_ffn = nc.dram_tensor("dbg_ffn", [NB, 128, 4096], BF16,
                                 kind="ExternalOutput")
        dbg_gtm = nc.dram_tensor("dbg_gtm", [NB, 128, 64], F32,
                                 kind="ExternalOutput")
        dbg_hid = nc.dram_tensor("dbg_hid", [NB, 128, 2048], BF16,
                                 kind="ExternalOutput")

    ctx = ExitStack()
    with ctx:
        def sb(st, name, shape, dt):
            return st.enter_context(nc.sbuf_tensor(name, shape, dt))

        ident_f = sb(ctx, "ident_f", [128, 128], F32)
        ident_b = sb(ctx, "ident_b", [128, 128], BF16)
        ones_cf = sb(ctx, "ones_cf", [128, 1], F32)
        ones_cb = sb(ctx, "ones_cb", [128, 1], BF16)
        ones_rf = sb(ctx, "ones_rf", [1, 128], F32)
        ones_rb = sb(ctx, "ones_rb", [1, 128], BF16)
        ones_mb = sb(ctx, "ones_mb", [128, 1], BF16)
        cnT = sb(ctx, "cnT", [128, 16], F32)
        expT = sb(ctx, "expT", [1, 1], F32)
        r0 = sb(ctx, "r0", [1, 512], F32)
        r1 = sb(ctx, "r1", [1, 512], F32)
        r2 = sb(ctx, "r2", [1, 512], F32)
        r3 = sb(ctx, "r3", [1, 512], F32)
        rden_b = sb(ctx, "rden_b", [1, 512], BF16)
        rbc = sb(ctx, "rbc", [128, 512], F32)
        cbc = sb(ctx, "cbc", [128, 512], F32)
        sqring = sb(ctx, "sqring", [128, 1024], BF16)
        yring = sb(ctx, "yring", [128, 1024], F32)
        wr_b = sb(ctx, "wr_b", [128, 1024], BF16)
        et_b = sb(ctx, "et_b", [16, 1024], BF16)
        wout_b = sb(ctx, "wout_b", [128, 4096], BF16)
        castr = sb(ctx, "castr", [128, 1024], BF16)

        bank = [ctx.enter_context(nc.psum_tensor(f"bk{i}", [128, 512], F32))
                for i in range(8)]

        sPE = Sem(nc, ctx, "sPE")
        sV = Sem(nc, ctx, "sV")
        sS = Sem(nc, ctx, "sS")
        sPL = Sem(nc, ctx, "sPL")
        sPR = Sem(nc, ctx, "sPR")
        sSS = Sem(nc, ctx, "sSS")
        dX = Sem(nc, ctx, "dX")
        dX2 = Sem(nc, ctx, "dX2")
        dSL = Sem(nc, ctx, "dSL")
        dSC = Sem(nc, ctx, "dSC")
        dY = Sem(nc, ctx, "dY")
        dYL = Sem(nc, ctx, "dYL")
        dW = Sem(nc, ctx, "dW")
        dWS = Sem(nc, ctx, "dWS")
        dWB = Sem(nc, ctx, "dWB")
        dO = Sem(nc, ctx, "dO")

        DT = Dfr("tensor")
        DV = Dfr("vector")
        DS = Dfr("scalar")
        DP = Dfr("gpsimd")

        def flush(blk):
            DT.flush(blk)
            DV.flush(blk)
            DS.flush(blk)
            DP.flush(blk)

        stA = ExitStack()
        maskd = sb(stA, "maskd", [128, 128], F32)
        bo_sb = sb(stA, "bo_sb", [128, 8], F32)
        gm_sb = sb(stA, "gm_sb", [128, 1], F32)
        wq_b = sb(stA, "wq_b", [128, 8192], BF16)
        wo_b = sb(stA, "wo_b", [128, 8192], BF16)
        S_sb = sb(stA, "S_sb", [128, 512], F32)
        z_sb = sb(stA, "z_sb", [128, 8], F32)
        ckS = sb(stA, "ckS", [128, 4096], F32)
        ckz = sb(stA, "ckz", [128, 64], F32)
        Spre_sb = sb(stA, "Spre_sb", [128, 512], F32)
        zpre = sb(stA, "zpre", [128, 8], F32)
        x0 = sb(stA, "x0", [128, 4096], F32)
        ln1_b = sb(stA, "ln1_b", [128, 4096], BF16)
        phk_f = sb(stA, "phk_f", [128, 4096], BF16)
        V0 = sb(stA, "V0", [128, 4160], BF16)
        e1s = sb(stA, "e1s", [128, 1024], F32)
        rs = sb(stA, "rs", [128, 1024], F32)

        stAk = ExitStack()
        wk_b = sb(stAk, "wk_b", [128, 8192], BF16)
        wv_b = sb(stAk, "wv_b", [128, 8192], BF16)
        phk_t = sb(stAk, "phk_t", [128, 4096], BF16)
        wstg = sb(stAk, "wstg", [128, 4096], F32)
        wcast = sb(stAk, "wcast", [128, 4096], BF16)

        wjobs = []
        wjobs.append((wq_h.ap()[:, 0:4096], 4096, wq_b[:, 0:4096], None))
        wjobs.append((wq_h.ap()[:, 4096:8192], 4096, wq_b[:, 4096:8192], None))
        wjobs.append((wo_h.ap()[:, 0:4096], 4096, wo_b[:, 0:4096], None))
        wjobs.append((wo_h.ap()[:, 4096:8192], 4096, wo_b[:, 4096:8192], None))
        wjobs.append((wr_hh.ap()[:, :], 1024, wr_b[:, :], None))
        wjobs.append((wout_hh.ap()[:, :], 4096, wout_b[:, :], None))
        for hh in range(16):
            wjobs.append((win_h.ap()[:, hh * 4096:(hh + 1) * 4096], 4096,
                          None, hh))
        wstate = {"free_pl": 0, "wc_store": 0, "ji": 0}

        def emit_wjobs(blk, kmax):
            for _ in range(kmax):
                if wstate["ji"] >= len(wjobs):
                    return
                src, width, dst, scr = wjobs[wstate["ji"]]
                wstate["ji"] += 1
                fpl = wstate["free_pl"]
                wcs = wstate["wc_store"]

                def _(sy, src=src, width=width, fpl=fpl):
                    if fpl:
                        sy.wait_ge(sPL.h, fpl)
                    dW.inc(sy.dma_start(wstg[:, 0:width], src), 16)
                blk.sync(_)
                ld_mark = dW.n
                cm = sPL.reserve(1)
                if dst is not None:
                    def _(g, dst=dst, width=width, ld=ld_mark):
                        g.wait_ge(dW.h, ld)
                        sPL.attach(g.tensor_copy(dst, wstg[:, 0:width]))
                    DP.add(_)
                    wstate["free_pl"] = cm
                else:
                    def _(g, width=width, ld=ld_mark, wcs=wcs):
                        g.wait_ge(dW.h, ld)
                        if wcs:
                            g.wait_ge(dWS.h, wcs)
                        sPL.attach(g.tensor_copy(wcast[:, 0:width],
                                                 wstg[:, 0:width]))
                    DP.add(_)
                    wstate["free_pl"] = cm

                    def _(sy, scr=scr, cm=cm):
                        sy.wait_ge(sPL.h, cm)
                        dWS.inc(sy.dma_start(wbf_scr.ap()[scr], wcast[:, :]),
                                16)
                    blk.sync(_)
                    wstate["wc_store"] = dWS.n

        lnst = {"b67_free_v": 0, "stat_last": 0}

        def emit_ln_stats(src, srcbf, src_ready):
            free_v = lnst["b67_free_v"]
            stat_last = lnst["stat_last"]
            pool_guard = sPL.n
            sq_m = []
            summ_m = []
            sqm_m = []
            for fc in range(NFC):
                if srcbf:
                    m = sS.reserve(1)

                    def _(sc, fc=fc, stat_last=stat_last):
                        if fc == 0:
                            for s, n in src_ready:
                                sc.wait_ge(s.h, n)
                        if fc < 2 and stat_last:
                            sc.wait_ge(sPE.h, stat_last)
                        if fc >= 2:
                            sc.wait_ge(sPE.h, sqm_m[fc - 2])
                        sS.attach(sc.activation(
                            sqring[:, (fc % 2) * 512:(fc % 2) * 512 + 512],
                            src[:, fc * 512:(fc + 1) * 512], AF.Square))
                    DS.add(_)
                    sq_m.append(m)
                else:
                    m = sS.reserve(2)

                    def _(sc, fc=fc, stat_last=stat_last):
                        if fc == 0:
                            for s, n in src_ready:
                                sc.wait_ge(s.h, n)
                        if fc < 2 and stat_last:
                            sc.wait_ge(sPE.h, stat_last)
                        if fc >= 2:
                            sc.wait_ge(sPE.h, summ_m[fc - 2])
                            sc.wait_ge(sPE.h, sqm_m[fc - 2])
                        sS.attach(sc.copy(
                            castr[:, (fc % 2) * 512:(fc % 2) * 512 + 512],
                            src[:, fc * 512:(fc + 1) * 512]))
                        sS.attach(sc.activation(
                            sqring[:, (fc % 2) * 512:(fc % 2) * 512 + 512],
                            castr[:, (fc % 2) * 512:(fc % 2) * 512 + 512],
                            AF.Square))
                    DS.add(_)
                    sq_m.append(m)

            for fc in range(NFC):
                m1 = sPE.reserve(1)
                m2 = sPE.reserve(1)

                def _(te, fc=fc):
                    if fc == 0 and free_v:
                        te.wait_ge(sV.h, free_v)
                    te.wait_ge(sS.h, sq_m[fc])
                    if srcbf:
                        sPE.attach(te.matmul(bank[6][0:1, :], ones_mb[:, :],
                                             src[:, fc * 512:(fc + 1) * 512],
                                             start=(fc == 0),
                                             stop=(fc == NFC - 1),
                                             skip_group_check=True))
                    else:
                        sPE.attach(te.matmul(
                            bank[6][0:1, :], ones_mb[:, :],
                            castr[:, (fc % 2) * 512:(fc % 2) * 512 + 512],
                            start=(fc == 0), stop=(fc == NFC - 1),
                            skip_group_check=True))
                    sPE.attach(te.matmul(
                        bank[7][0:1, :], ones_mb[:, :],
                        sqring[:, (fc % 2) * 512:(fc % 2) * 512 + 512],
                        start=(fc == 0), stop=(fc == NFC - 1),
                        skip_group_check=True))
                DT.add(_)
                summ_m.append(m1)
                sqm_m.append(m2)
            stat_mark = sPE.n

            sqz = sS.reserve(1)

            def _(sc):
                sc.wait_ge(sPE.h, stat_mark)
                sS.attach(sc.activation(r2[:, :], bank[6][0:1, :],
                                        AF.Square))
            DS.add(_)

            fin = sV.reserve(1)

            def _(v, sqz=sqz):
                v.wait_ge(sS.h, sqz)
                sV.attach(v.scalar_tensor_tensor(r3[:, :], bank[7][0:1, :],
                                                 LN_EPS, r2[:, :],
                                                 OP.add, OP.subtract))
            DV.add(_)

            rstd = sS.reserve(3)

            def _(sc, fin=fin, base=rstd - 3):
                sc.wait_ge(sV.h, fin)
                sS.attach(sc.activation(r1[:, :], r3[:, :], AF.Ln))
                sc.wait_ge(sS.h, base + 1)
                sS.attach(sc.activation(r2[:, :], r1[:, :], AF.Exp,
                                        scale=-0.5))
                sc.wait_ge(sS.h, base + 2)
                sS.attach(sc.copy(sqring[0:1, 0:512], r2[:, :]))
            DS.add(_)

            cneg = sV.reserve(1)

            def _(v, rstd=rstd):
                v.wait_ge(sS.h, rstd - 1)
                sV.attach(v.scalar_tensor_tensor(
                    sqring[0:1, 512:1024], bank[6][0:1, :], -1.0,
                    r2[:, :], OP.mult, OP.mult))
            DV.add(_)

            bc = sPE.reserve(1)

            def _(te, cneg=cneg, rstd=rstd):
                te.wait_ge(sV.h, cneg)
                te.wait_ge(sS.h, rstd)
                te.matmul(bank[6][:, :], ones_rb[0:1, :],
                          sqring[0:1, 0:512], start=True, stop=True,
                          skip_group_check=True)
                sPE.attach(te.matmul(bank[7][:, :], ones_rb[0:1, :],
                                     sqring[0:1, 512:1024], start=True,
                                     stop=True, skip_group_check=True))
            DT.add(_)

            bcev = sV.reserve(1)

            def _(v, bc=bc, pg=pool_guard):
                v.wait_ge(sPE.h, bc)
                if pg:
                    v.wait_ge(sPL.h, pg)
                v.tensor_copy(rbc[:, :], bank[6][:, :])
                sV.attach(v.tensor_copy(cbc[:, :], bank[7][:, :]))
            DV.add(_)
            lnst["b67_free_v"] = bcev
            lnst["stat_last"] = bc
            return bcev

        def emit_elu_proj(w_sb, dst_bf, rhs_bf, rhs_ready, prev_free,
                          mask=None, parts='mse'):
            evac_s = []
            evac_v = []
            mmm = []
            for oc in range(NFC):
                bk = bank[oc % 2]
                s_ = (oc % 2) * 512
                o_ = 512 - s_
                m = sPE.reserve(1)

                def _(te, oc=oc, bk=bk):
                    if oc == 0:
                        for s, n in rhs_ready:
                            te.wait_ge(s.h, n)
                    if oc < 2:
                        for s, n in prev_free:
                            te.wait_ge(s.h, n)
                    if oc >= 2 and 's' in parts:
                        te.wait_ge(sS.h, evac_s[oc - 2])
                    if oc >= 2 and 'e' in parts:
                        te.wait_ge(sV.h, evac_v[oc - 2])
                    for ec in range(NFC):
                        inst = te.matmul(
                            bk[:, :],
                            w_sb[:, ec * 1024 + oc * 128: ec * 1024 + oc * 128 + 128],
                            rhs_bf[:, ec * 512:(ec + 1) * 512],
                            start=(ec == 0), stop=(ec == NFC - 1),
                            skip_group_check=True)
                    sPE.attach(inst)
                DT.add(_)
                mmm.append(m)

                if 's' in parts:
                    ms = sS.reserve(1)

                    def _(sc, oc=oc, bk=bk, s_=s_, m=m):
                        sc.wait_ge(sPE.h, m)
                        if 'e' in parts:
                            if mask is None:
                                if oc >= 2:
                                    sc.wait_ge(sV.h, evac_v[oc - 2])
                            else:
                                if oc >= 1:
                                    sc.wait_ge(sV.h, evac_v[oc - 1])
                        sS.attach(sc.activation(e1s[:, s_:s_ + 512], bk[:, :],
                                                AF.Exp))
                    DS.add(_)
                    evac_s.append(ms)
                else:
                    evac_s.append(0)

                if 'e' in parts:
                    mv = sV.reserve(1)

                    def _(v, oc=oc, bk=bk, s_=s_, o_=o_, m=m, ms=evac_s[-1]):
                        v.wait_ge(sPE.h, m)
                        if ms:
                            v.wait_ge(sS.h, ms)
                        if mask is None:
                            v.tensor_scalar_max(rs[:, s_:s_ + 512], bk[:, :],
                                                0.0)
                            sV.attach(v.scalar_tensor_tensor(
                                dst_bf[:, oc * 512:(oc + 1) * 512],
                                e1s[:, s_:s_ + 512], 1.0, rs[:, s_:s_ + 512],
                                OP.min, OP.add))
                        else:
                            v.tensor_scalar(rs[:, s_:s_ + 512], bk[:, :], 0.0,
                                            mask, OP.max, OP.mult)
                            v.tensor_scalar(rs[:, o_:o_ + 512],
                                            e1s[:, s_:s_ + 512], mask, mask,
                                            OP.mult, OP.min)
                            sV.attach(v.tensor_add(
                                dst_bf[:, oc * 512:(oc + 1) * 512],
                                rs[:, o_:o_ + 512], rs[:, s_:s_ + 512]))
                    DV.add(_)
                    evac_v.append(mv)
                else:
                    evac_v.append(0)
            return evac_v, mmm, evac_s

        with nc.Block() as blk:
            def _(sy):
                dW.inc(sy.dma_start(ident_f[:, :], ident_h.ap()[:, :]), 16)
                dW.inc(sy.dma_start(maskd[:, :], maskd_h.ap()[:, :]), 16)
                dW.inc(sy.dma_start(bo_sb[:, :], bo_h.ap()[:, :]), 16)
                dW.inc(sy.dma_start(gm_sb[:, :], gm_h.ap()[:, :]), 16)
                dW.inc(sy.dma_start(r0[0:1, 0:1], temp_h.ap()[:, :]), 16)
                dW.inc(sy.dma_start(e1s[:, 0:16], cT_h.ap()[:, :]), 16)
                dW.inc(sy.dma_start(wstg[0:16, 0:1024], et_h.ap()[:, :]), 16)
                src = xpT.ap().rearrange("(c p) t -> p c t", p=128)
                dst = x0[:, :].rearrange("p (c t) -> p c t", c=NFC)
                dX.inc(sy.dma_start(dst, src[:, :, 0:P]), 16)
            blk.sync(_)
            w0_ld = dW.n

            def _(v):
                v.memset(ones_cf[:, :], 1.0)
                v.memset(ones_cb[:, :], 1.0)
                v.memset(ones_rf[:, :], 1.0)
                v.memset(ones_rb[:, :], 1.0)
                v.memset(ones_mb[:, :], 1.0 / 1024.0)
                v.memset(S_sb[:, :], 0.0)
                v.memset(z_sb[:, :], 0.0)
                v.memset(Spre_sb[:, :], 0.0)
                v.memset(zpre[:, :], 0.0)
                v0r = V0[:, :].rearrange("p (j h c) -> p j h c", j=4, h=16)
                sV.inc(v.memset(v0r[:, :, :, 64:65], 1.0))
            blk.vector(_)

            def _(g):
                g.wait_ge(dW.h, w0_ld)
                g.tensor_copy(ident_b[:, :], ident_f[:, :])
                sPL.inc(g.tensor_copy(et_b[:, :], wstg[0:16, 0:1024]))
            blk.gpsimd(_)
            wstate["free_pl"] = sPL.n

            def _(sc):
                sc.wait_ge(dW.h, w0_ld)
                sS.inc(sc.activation(r1[0:1, 0:1], r0[0:1, 0:1], AF.Exp))
                sS.inc(sc.activation(rs[:, 0:16], e1s[:, 0:16], AF.Square))
            blk.scalar(_)
            m_expT = sS.n

            def _(v):
                v.wait_ge(sS.h, m_expT)
                sV.inc(v.tensor_scalar_mul(expT[:, :], r1[0:1, 0:1],
                                           INV_SQRT_G))
            blk.vector(_)

            def _(te):
                te.wait_ge(sS.h, m_expT)
                sPE.inc(te.matmul(bank[6][0:1, 0:16], ones_cf[:, :],
                                  rs[:, 0:16], start=True, stop=True,
                                  skip_group_check=True))
            blk.tensor(_)
            m_cssq = sPE.n

            def _(sc):
                sc.wait_ge(sPE.h, m_cssq)
                sS.inc(sc.activation(r1[0:1, 0:16], bank[6][0:1, 0:16],
                                     AF.Ln))
                sc.wait_ge(sS.h, sS.n)
                sS.inc(sc.activation(r2[0:1, 0:16], r1[0:1, 0:16], AF.Exp,
                                     scale=-0.5))
            blk.scalar(_)
            m_crn = sS.n

            def _(v):
                v.wait_ge(sS.h, m_crn)
                sV.inc(v.tensor_scalar_min(r3[0:1, 0:16], r2[0:1, 0:16],
                                           1e12))
            blk.vector(_)
            m_crn2 = sV.n

            def _(te):
                te.wait_ge(sV.h, m_crn2)
                sPE.inc(te.matmul(bank[6][:, 0:16], ones_rf[0:1, :],
                                  r3[0:1, 0:16], start=True, stop=True,
                                  skip_group_check=True))
            blk.tensor(_)
            m_cbc = sPE.n

            def _(v):
                v.wait_ge(sPE.h, m_cbc)
                sV.inc(v.tensor_copy(rs[:, 16:32], bank[6][:, 0:16]))
                v.wait_ge(sV.h, sV.n)
                sV.inc(v.tensor_mul(cnT[:, :], e1s[:, 0:16], rs[:, 16:32]))
            blk.vector(_)
            lnst["b67_free_v"] = sV.n

            wkv_v = 0
            for w_dst, w_src in ((wk_b, wk_h), (wv_b, wv_h)):
                for half in range(2):
                    fpl = wstate["free_pl"]
                    fv = wkv_v

                    def _(sy, w_src=w_src, half=half, fpl=fpl, fv=fv):
                        sy.wait_ge(sPL.h, fpl)
                        if fv:
                            sy.wait_ge(sV.h, fv)
                        dW.inc(sy.dma_start(
                            wstg[:, :],
                            w_src.ap()[:, half * 4096:half * 4096 + 4096]), 16)
                    blk.sync(_)
                    ldm = dW.n

                    def _(g, w_dst=w_dst, half=half, ldm=ldm):
                        g.wait_ge(dW.h, ldm)
                        sPL.inc(g.tensor_copy(
                            w_dst[:, half * 4096:half * 4096 + 2048],
                            wstg[:, 0:2048]))
                    blk.gpsimd(_)
                    wstate["free_pl"] = sPL.n

                    def _(v, w_dst=w_dst, half=half, ldm=ldm):
                        v.wait_ge(dW.h, ldm)
                        sV.inc(v.tensor_copy(
                            w_dst[:, half * 4096 + 2048:half * 4096 + 4096],
                            wstg[:, 2048:4096]))
                    blk.vector(_)
                    wkv_v = sV.n
        wk_ready = sPL.n
        wk_ready_v = wkv_v

        x_marks = {"pe": 0, "s": 0, "v": 0, "pl": 0}
        st_marks = {"ln1": 0, "phk": 0, "v": 0}
        hoist_ln = [None]
        proj01_free = []
        prev_supd = [0]
        prev_trev = [0]
        if stop_after == "W0":
            n_a_iters = 0
        elif isinstance(stop_after, str) and stop_after.startswith("A") and \
                len(stop_after) > 1 and stop_after[1:].isdigit():
            n_a_iters = int(stop_after[1:])
        else:
            n_a_iters = 2 * NB
        if n_a_iters:
            blkA = nc.Block()
            blkA.__enter__()
        for it in range(n_a_iters):
            prefix = it < NB
            b = it % NB
            if True:
                blk = blkA
                xr = [(dX, 16 * (it + 1))]
                ln_mark = emit_ln_stats(x0, False, xr)

                stg = dict(st_marks)
                ln1_v = sV.reserve(4)

                def _(v, stg=stg, ln_mark=ln_mark):
                    v.wait_ge(sV.h, ln_mark)
                    if stg["ln1"]:
                        v.wait_ge(dSC.h, stg["ln1"])
                    for fc in range(0, NFC, 2):
                        v.tensor_mul(e1s[:, 0:512],
                                     x0[:, fc * 512:(fc + 1) * 512], rbc[:, :])
                        sV.attach(v.tensor_add(
                            ln1_b[:, fc * 512:(fc + 1) * 512],
                            e1s[:, 0:512], cbc[:, :]))
                DV.add(_)

                ln1_pl = sPL.reserve(4)

                def _(g, stg=stg, ln_mark=ln_mark):
                    g.wait_ge(sV.h, ln_mark)
                    if stg["ln1"]:
                        g.wait_ge(dSC.h, stg["ln1"])
                    for fc in range(1, NFC, 2):
                        g.tensor_mul(rs[:, 0:512],
                                     x0[:, fc * 512:(fc + 1) * 512], rbc[:, :])
                        sPL.attach(g.tensor_add(
                            ln1_b[:, fc * 512:(fc + 1) * 512],
                            rs[:, 0:512], cbc[:, :]))
                DP.add(_)
                ln1_ready = [(sV, ln1_v), (sPL, ln1_pl)]
                x_marks = {"pe": sPE.n, "s": sS.n, "v": ln1_v, "pl": ln1_pl}

                if it + 1 < 2 * NB:
                    def _(sy, it=it, xm=dict(x_marks)):
                        sy.wait_ge(sPE.h, xm["pe"])
                        sy.wait_ge(sS.h, xm["s"])
                        sy.wait_ge(sV.h, xm["v"])
                        sy.wait_ge(sPL.h, xm["pl"])
                        srct = xpT if (it + 1) < NB else xT
                        nb = (it + 1) % NB
                        src = srct.ap().rearrange("(c p) t -> p c t", p=128)
                        dst = x0[:, :].rearrange("p (c t) -> p c t", c=NFC)
                        dX.inc(sy.dma_start(dst,
                                            src[:, :, nb * P:(nb + 1) * P]),
                               16)
                    blk.sync(_)

                if 'k' not in a_parts:
                    flush(blk)
                    continue
                prev_free = [(sPL, wk_ready), (sV, wk_ready_v)] + \
                    list(proj01_free)
                if st_marks["phk"]:
                    prev_free = prev_free + [(dSC, st_marks["phk"])]
                kp = 'm'
                if '1' not in a_parts:
                    kp += 's'
                if '1' not in a_parts and '2' not in a_parts:
                    kp += 'e'
                phk_marks, _km, phk_es = emit_elu_proj(
                    wk_b, phk_f, ln1_b, ln1_ready, prev_free,
                    mask=(gm_sb[:, :] if (prefix and 'M' not in a_parts)
                          else None), parts=kp)
                if phk_es and phk_es[-1]:
                    proj01_free[:] = [(sS, phk_es[-1]),
                                      (sV, phk_marks[-1])]

                if 'v' not in a_parts:
                    flush(blk)
                    continue
                vevac = []
                vmm = []
                stv = st_marks["v"]
                gi = 0
                for jc in range(NJC):
                    for half in range(2):
                        bk = bank[2 + gi % 2]
                        m = sPE.reserve(1)

                        def _(te, jc=jc, half=half, bk=bk, gi=gi, stv=stv,
                              psu=prev_supd[0]):
                            if gi == 0:
                                for s, n in ln1_ready:
                                    te.wait_ge(s.h, n)
                                te.wait_ge(sPL.h, wk_ready)
                                te.wait_ge(sV.h, wk_ready_v)
                                if stv:
                                    te.wait_ge(dSC.h, stv)
                            if gi < 2 and psu:
                                te.wait_ge(sV.h, psu)
                            if gi >= 2:
                                te.wait_ge(sV.h, vevac[gi - 2])
                            for ec in range(NFC):
                                inst = te.matmul(
                                    bk[:, :],
                                    ln1_b[:, ec * 512 + jc * 128: ec * 512 + jc * 128 + 128],
                                    wv_b[:, ec * 1024 + half * 512: ec * 1024 + half * 512 + 512],
                                    start=(ec == 0), stop=(ec == NFC - 1),
                                    skip_group_check=True)
                            sPE.attach(inst)
                        DT.add(_)
                        vmm.append(m)

                        mv = sV.reserve(1)

                        def _(v, jc=jc, half=half, bk=bk, m=m):
                            v.wait_ge(sPE.h, m)
                            v0r = V0[:, :].rearrange("p (j h c) -> p j h c",
                                                     j=4, h=16)
                            dst = v0r[:, jc, half * 8:half * 8 + 8, 0:64]
                            sV.attach(v.tensor_copy(
                                dst, bk[:, :].rearrange("p (h d) -> p h d",
                                                        h=8)))
                        DV.add(_)
                        vevac.append(mv)
                        gi += 1
                v_done = sV.n

                if 't' not in a_parts:
                    flush(blk)
                    continue
                tr_m = []
                tr_ev = []
                gi = 0
                for jc in range(NJC):
                    for half in range(2):
                        bk = bank[4 + gi % 2]
                        m = sPE.reserve(1)

                        def _(te, jc=jc, half=half, bk=bk, gi=gi,
                              ptr=prev_trev[0]):
                            if gi < 2 and ptr:
                                te.wait_ge(sS.h, ptr)
                            if gi >= 2:
                                te.wait_ge(sS.h, tr_ev[gi - 2])
                            for k in range(4):
                                oc = half * 4 + k
                                te.wait_ge(sV.h, phk_marks[oc])
                                inst = te.transpose(
                                    bk[:, :].bitcast(BF16)[:, k * 128:(k + 1) * 128],
                                    phk_f[:, oc * 512 + jc * 128: oc * 512 + jc * 128 + 128],
                                    ident_b[:, :])
                            sPE.attach(inst)
                        DT.add(_)
                        tr_m.append(m)

                        ms = sS.reserve(1)

                        def _(sc, jc=jc, half=half, bk=bk, gi=gi):
                            sc.wait_ge(sPE.h, tr_m[gi])
                            sS.attach(sc.copy(
                                phk_t[:, jc * 1024 + half * 512: jc * 1024 + half * 512 + 512],
                                bk[:, :].bitcast(BF16)[:, 0:512]))
                        DS.add(_)
                        tr_ev.append(ms)
                        gi += 1
                trdone = sS.n

                if 's' not in a_parts:
                    flush(blk)
                    continue
                ds_mark = sPE.reserve(1)

                def _(te, v_done=v_done, trdone=trdone):
                    te.wait_ge(sV.h, v_done)
                    te.wait_ge(sS.h, trdone)
                    for hh in range(H):
                        for jc in range(NJC):
                            inst = te.matmul(
                                bank[2][(hh % 2) * 64:(hh % 2) * 64 + 64,
                                        (hh // 2) * 64:(hh // 2) * 64 + 64],
                                phk_t[:, jc * 1024 + (hh // 2) * 128 + (hh % 2) * 64:
                                      jc * 1024 + (hh // 2) * 128 + (hh % 2) * 64 + 64],
                                V0[:, jc * 1040 + hh * 65: jc * 1040 + hh * 65 + 64],
                                start=(jc == 0), stop=(jc == NJC - 1),
                                skip_group_check=True)
                    sPE.attach(inst)
                DT.add(_)

                s_upd = sV.reserve(NFC + 1)

                def _(v, b=b, prefix=prefix, ds_mark=ds_mark,
                      base=s_upd - NFC - 1):
                    Sdst = Spre_sb if prefix else S_sb
                    zdst = zpre if prefix else z_sb
                    if not prefix:
                        v.tensor_copy(ckS[:, b * 512:(b + 1) * 512],
                                      S_sb[:, :])
                        v.tensor_copy(ckz[:, b * 8:(b + 1) * 8], z_sb[:, :])
                    for fc in range(NFC):
                        v.wait_ge(sV.h, phk_marks[fc])
                        sV.attach(v.reduce_sum(
                            rs[:, 0:1], phk_f[:, fc * 512:(fc + 1) * 512],
                            axis=AX))
                        v.wait_ge(sV.h, base + fc + 1)
                        v.tensor_add(zdst[:, fc:fc + 1], zdst[:, fc:fc + 1],
                                     rs[:, 0:1])
                    v.wait_ge(sPE.h, ds_mark)
                    sV.attach(v.tensor_add(Sdst[:, :], Sdst[:, :],
                                           bank[2][:, :]))
                DV.add(_)

                if not prefix:
                    def _(sy, b=b, lv=ln1_v, lpl=ln1_pl, pm=phk_marks[-1],
                          vd=v_done):
                        sy.wait_ge(sV.h, lv)
                        sy.wait_ge(sPL.h, lpl)
                        dSC.inc(sy.dma_start(ln1_scr.ap()[b], ln1_b[:, :]),
                                16)
                        sy.wait_ge(sV.h, pm)
                        dSC.inc(sy.dma_start(phk_scr.ap()[b], phk_f[:, :]),
                                16)
                        sy.wait_ge(sV.h, vd)
                        dSC.inc(sy.dma_start(v_scr.ap()[b], V0[:, :]), 16)
                        st_marks["ln1"] = dSC.n
                        st_marks["phk"] = dSC.n
                        st_marks["v"] = dSC.n
                    blk.sync(_)

                emit_wjobs(blk, 2 if it > 0 else 1)
                prev_supd[0] = s_upd
                prev_trev[0] = trdone
                flush(blk)
        if n_a_iters:
            blkA.__exit__(None, None, None)

        spre_mark = sV.n
        with nc.Block() as blk:
            emit_wjobs(blk, 99)

            def _(sy):
                sy.wait_ge(dWS.h, dWS.n)
                sy.wait_ge(dW.h, dW.n)
                src = xT.ap().rearrange("(c p) t -> p c t", p=128)
                dst = x0[:, :].rearrange("p (c t) -> p c t", c=NFC)
                dX.inc(sy.dma_start(dst, src[:, :, 0:P]), 16)
                sy.wait_ge(dSC.h, dSC.n)
                dSL.inc(sy.dma_start(ln1_b[:, :], ln1_scr.ap()[0]), 16)
                dSL.inc(sy.dma_start(phk_f[:, :], phk_scr.ap()[0]), 16)
                dSL.inc(sy.dma_start(V0[:, :], v_scr.ap()[0]), 16)
            blk.sync(_)
            flush(blk)
        wcast_all = sPL.n

        stAk.close()
        stAq = ExitStack()
        x1 = sb(stAq, "x1", [128, 4096], F32)
        phq = sb(stAq, "phq", [128, 4096], BF16)
        A_sb0 = sb(stAq, "A_sb0", [128, 2048], BF16)
        A_sb1 = sb(stAq, "A_sb1", [128, 2048], BF16)
        u_sb = sb(stAq, "u_sb", [128, 4096], BF16)
        rden_c = sb(stAq, "rden_c", [1, 512], BF16)
        yring2 = sb(stAq, "yring2", [128, 1024], F32)
        ao = sb(stAq, "ao", [128, 4096], F32)
        rdbc = sb(stAq, "rdbc", [64, 512], F32)
        S65 = sb(stAq, "S65", [128, 520], BF16)

        aq_x = {0: dX, 1: dX2}
        aq_prev_pairs = [0, 0]
        aq_x_cnt = {0: dX.n, 1: dX2.n}
        yst_marks = []
        prev_u15 = [0]
        prev_u14 = [0]
        prev_rdbc15 = [0]
        prev_rd15 = [0]
        prev_num15 = [0]
        prev_wo = [0]
        prev_ypl = [0]
        abank_hist = {2: [], 3: []}
        if stop_after in ("W0", "A") or (
                isinstance(stop_after, str) and stop_after.startswith("A")
                and len(stop_after) > 1):
            n_q_iters = 0
        else:
            n_q_iters = NB
        if n_q_iters:
            blkQ = nc.Block()
            blkQ.__enter__()
        for b in range(n_q_iters):
            if True:
                blk = blkQ
                xbuf = x0 if b % 2 == 0 else x1
                xw_sem = aq_x[b % 2]
                xw_n = aq_x_cnt[b % 2]

                sl_base = 48 * b + 48

                if b + 1 < NB:
                    nsem = aq_x[(b + 1) % 2]
                    nsem.n += 16
                    aq_x_cnt[(b + 1) % 2] = nsem.n

                    def _(sy, b=b, nsem=nsem, gv=sV.n, gp=sPL.n):
                        if gv:
                            sy.wait_ge(sV.h, gv)
                        if gp:
                            sy.wait_ge(sPL.h, gp)
                        src = xT.ap().rearrange("(c p) t -> p c t", p=128)
                        dstb = (x1 if (b + 1) % 2 else x0)
                        dst = dstb[:, :].rearrange("p (c t) -> p c t", c=NFC)
                        nsem.attach(sy.dma_start(
                            dst, src[:, :, (b + 1) * P:(b + 2) * P]), 16)
                    blk.sync(_)

                s65_mark = sV.reserve(3)

                def _(v, b=b, base=s65_mark - 3, pn=prev_num15[0]):
                    v.wait_ge(sV.h, spre_mark)
                    if pn:
                        v.wait_ge(sPE.h, pn)
                    s65r = S65[:, :].rearrange("p (h c) -> p h c", h=8)
                    sV.attach(v.tensor_add(e1s[:, 0:512],
                                           ckS[:, b * 512:(b + 1) * 512],
                                           Spre_sb[:, :]))
                    v.wait_ge(sV.h, base + 1)
                    v.tensor_copy(s65r[:, :, 0:64],
                                  e1s[:, 0:512].rearrange("p (h d) -> p h d",
                                                          h=8))
                    sV.attach(v.tensor_add(rs[:, 0:8],
                                           ckz[:, b * 8:(b + 1) * 8],
                                           zpre[:, :]))
                    v.wait_ge(sV.h, base + 2)
                    sV.attach(v.tensor_copy(
                        s65r[:, :, 64:65],
                        rs[:, 0:8].rearrange("p (h o) -> p h o", o=1)))
                DV.add(_)

                phq_marks, _qm, phq_es = emit_elu_proj(
                    wq_b, phq, ln1_b, [(dSL, sl_base), (sPL, sPL.n)],
                    [(sV, s65_mark)] + proj01_free)
                proj01_free[:] = [(sS, phq_es[-1]), (sV, phq_marks[-1])]
                qproj_pe = sPE.n

                if b + 1 < NB:
                    def _(sy, qp=qproj_pe, b=b):
                        sy.wait_ge(sPE.h, qp)
                        dSL.inc(sy.dma_start(ln1_b[:, :],
                                             ln1_scr.ap()[b + 1]), 16)
                    blk.sync(_)

                b67g = lnst["b67_free_v"]
                amm = {}
                aev_v = {}
                aev_s = {}
                num_mark = {}
                mda_mark = {}
                bc_mark = {}
                rd_mark = {}
                rdbc_ev = {}
                u_mark = {}

                def emit_mda(g):
                    m = sV.reserve(1)

                    def _(v, g=g, pr=prev_rd15[0]):
                        nbk = bank[4] if g % 2 == 0 else bank[6]
                        v.wait_ge(sPE.h, num_mark[g])
                        if g >= 1:
                            v.wait_ge(sS.h, rd_mark[g - 1])
                        elif pr:
                            v.wait_ge(sS.h, pr)
                        sV.attach(v.tensor_scalar_add(
                            r0[:, :], nbk[64:65, 0:512], ATT_EPS))
                    DV.add(_)
                    mda_mark[g] = m

                def emit_den_s(g):
                    md = sS.reserve(2)

                    def _(sc, g=g, base=md - 2):
                        sc.wait_ge(sV.h, mda_mark[g])
                        sS.attach(sc.activation(r1[:, :], r0[:, :], AF.Ln))
                        sc.wait_ge(sS.h, base + 1)
                        if g >= 2:
                            sc.wait_ge(sPE.h, bc_mark[g - 2])
                        sS.attach(sc.activation(
                            (rden_b if g % 2 == 0 else rden_c)[:, :],
                            r1[:, :], AF.Exp, scale=-1.0))
                    DS.add(_)
                    rd_mark[g] = md

                def emit_stage2a(g):
                    mb = sPE.reserve(1)

                    def _(te, g=g, prb=prev_rdbc15[0]):
                        te.wait_ge(sS.h, rd_mark[g])
                        if g >= 1:
                            te.wait_ge(sS.h, rdbc_ev[g - 1])
                        elif prb:
                            te.wait_ge(sS.h, prb)
                        rdn = rden_b if g % 2 == 0 else rden_c
                        sPE.attach(te.matmul(
                            bank[5][0:64, :], ones_rb[0:1, 0:64],
                            rdn[0:1, :],
                            start=True, stop=True, skip_group_check=True))
                    DT.add(_)
                    bc_mark[g] = mb

                    ms = sS.reserve(1)

                    def _(sc, g=g, mb=mb):
                        sc.wait_ge(sPE.h, mb)
                        if g >= 1:
                            sc.wait_ge(sV.h, u_mark[g - 1])
                        sS.attach(sc.copy(rdbc[:, :], bank[5][0:64, :]))
                    DS.add(_)
                    rdbc_ev[g] = ms

                def emit_stage2b(g):
                    mu = sV.reserve(1)

                    def _(v, g=g, ms=rdbc_ev[g]):
                        nbk = bank[4] if g % 2 == 0 else bank[6]
                        v.wait_ge(sS.h, ms)
                        sV.attach(v.tensor_mul(
                            u_sb[(g % 2) * 64:(g % 2) * 64 + 64,
                                 (g // 2) * 512:(g // 2) * 512 + 512],
                            nbk[0:64, 0:512], rdbc[:, :]))
                    DV.add(_)
                    u_mark[g] = mu

                for h in range(H):
                    if h >= 2:
                        emit_stage2a(h - 2)
                    for jc in range(NJC):
                        bki = 2 + jc % 2
                        hist = abank_hist[bki][-1] if abank_hist[bki] else None
                        m = sPE.reserve(1)

                        def _(te, h=h, jc=jc, bki=bki, hist=hist, b=b):
                            qoff = (h % 2) * 64
                            coff = (h // 2) * 512
                            if h == 0 and jc == 0:
                                te.wait_ge(dSL.h, 48 * b + 48)
                            if hist:
                                te.wait_ge(sV.h, hist[0])
                                te.wait_ge(sS.h, hist[1])
                            te.wait_ge(sV.h, phq_marks[h // 2])
                            sPE.attach(te.matmul(
                                bank[bki][:, jc * 128:512],
                                phk_f[qoff:qoff + 64, coff + jc * 128: coff + jc * 128 + 128],
                                phq[qoff:qoff + 64, coff + jc * 128: coff + 512],
                                start=True, stop=True, skip_group_check=True))
                        DT.add(_)
                        amm[(h, jc)] = m

                        if jc < NJC - 1:
                            ms = sS.reserve(1)

                            def _(sc, h=h, jc=jc, bki=bki, m=m):
                                Ab = A_sb0 if h % 2 == 0 else A_sb1
                                sc.wait_ge(sPE.h, m)
                                if h >= 2:
                                    sc.wait_ge(sPE.h, num_mark[h - 2])
                                sS.attach(sc.copy(
                                    Ab[:, jc * 512 + (jc + 1) * 128: jc * 512 + 512],
                                    bank[bki][:, (jc + 1) * 128:512]))
                            DS.add(_)
                            aev_s[(h, jc)] = ms
                        else:
                            aev_s[(h, jc)] = 0

                        mv = sV.reserve(1)

                        def _(v, h=h, jc=jc, bki=bki, m=m,
                              ms=aev_s[(h, jc)]):
                            Ab = A_sb0 if h % 2 == 0 else A_sb1
                            v.wait_ge(sPE.h, m)
                            if ms:
                                v.wait_ge(sS.h, ms)
                            if h >= 2:
                                v.wait_ge(sPE.h, num_mark[h - 2])
                            sV.attach(v.tensor_mul(
                                Ab[:, jc * 512 + jc * 128: jc * 512 + jc * 128 + 128],
                                bank[bki][:, jc * 128:jc * 128 + 128],
                                maskd[:, :]))
                        DV.add(_)
                        aev_v[(h, jc)] = mv
                        abank_hist[bki].append((aev_v[(h, jc)],
                                                aev_s[(h, jc)]))
                        if jc == 0 and h >= 2:
                            emit_stage2b(h - 2)
                        if jc == 1 and h >= 1:
                            emit_mda(h - 1)

                    mn = sPE.reserve(1)

                    def _(te, h=h, b=b, pu14=prev_u14[0],
                          pu15=prev_u15[0], b67g=b67g):
                        qoff = (h % 2) * 64
                        nbk = bank[4] if h % 2 == 0 else bank[6]
                        Ab = A_sb0 if h % 2 == 0 else A_sb1
                        if h == 0:
                            te.wait_ge(dSL.h, 48 * b + 48)
                            if pu14:
                                te.wait_ge(sV.h, pu14)
                        if h == 1:
                            if pu15:
                                te.wait_ge(sV.h, pu15)
                            if b67g:
                                te.wait_ge(sV.h, b67g)
                        if h >= 2:
                            te.wait_ge(sV.h, u_mark[h - 2])
                        te.wait_ge(sV.h, s65_mark)
                        for jc in range(NJC):
                            te.wait_ge(sV.h, aev_v[(h, jc)])
                            if jc < NJC - 1:
                                te.wait_ge(sS.h, aev_s[(h, jc)])
                            te.matmul(
                                nbk[0:65, jc * 128:512],
                                V0[:, jc * 1040 + h * 65: jc * 1040 + h * 65 + 65],
                                Ab[:, jc * 512 + jc * 128: jc * 512 + 512],
                                start=(jc == 0), stop=False,
                                skip_group_check=True)
                        sPE.attach(te.matmul(
                            nbk[0:65, 0:512],
                            S65[qoff:qoff + 64, (h // 2) * 65:(h // 2) * 65 + 65],
                            phq[qoff:qoff + 64, (h // 2) * 512:(h // 2) * 512 + 512],
                            start=False, stop=True, skip_group_check=True))
                    DT.add(_)
                    num_mark[h] = mn
                    if h >= 1:
                        emit_den_s(h - 1)

                emit_mda(H - 1)
                emit_den_s(H - 1)
                emit_stage2a(H - 2)
                emit_stage2b(H - 2)
                emit_stage2a(H - 1)
                emit_stage2b(H - 1)
                u_last = sV.n
                last_A_pe = sPE.n

                if b + 1 < NB:
                    def _(sy, b=b, lastA=last_A_pe):
                        sy.wait_ge(sPE.h, lastA)
                        dSL.inc(sy.dma_start(phk_f[:, :],
                                             phk_scr.ap()[b + 1]), 16)
                        dSL.inc(sy.dma_start(V0[:, :], v_scr.ap()[b + 1]), 16)
                    blk.sync(_)

                wo_ev = []
                for oc in range(NFC):
                    bk = bank[oc % 2]
                    m = sPE.reserve(1)

                    def _(te, oc=oc, bk=bk, u_last=u_last,
                          pyl=prev_ypl[0]):
                        if oc == 0:
                            te.wait_ge(sV.h, u_last)
                            te.wait_ge(sS.h, rdbc_ev[H - 1])
                            te.wait_ge(sPL.h, wcast_all)
                            if pyl:
                                te.wait_ge(sPL.h, pyl)
                        if oc >= 2:
                            te.wait_ge(sS.h, wo_ev[oc - 2])
                        for fc in range(NFC):
                            inst = te.matmul(
                                bk[:, :],
                                wo_b[:, fc * 1024 + oc * 128: fc * 1024 + oc * 128 + 128],
                                u_sb[:, fc * 512:(fc + 1) * 512],
                                start=(fc == 0), stop=(fc == NFC - 1),
                                skip_group_check=True)
                        sPE.attach(inst)
                    DT.add(_)

                    ms = sS.reserve(1)

                    def _(sc, oc=oc, bk=bk, m=m):
                        sc.wait_ge(sPE.h, m)
                        sS.attach(sc.activation(
                            ao[:, oc * 512:(oc + 1) * 512], bk[:, :],
                            AF.Identity, bias=bo_sb[:, oc:oc + 1]))
                    DS.add(_)
                    wo_ev.append(ms)
                proj01_free[:] = [(sS, wo_ev[-1])]
                ao_ready = [(sS, sS.n)]

                if dbg:
                    def _(sy, b=b, am=sS.n, um=u_last):
                        sy.wait_ge(sS.h, am)
                        dW.inc(sy.dma_start(dbg_ao.ap()[b], ao[:, :]), 16)
                        sy.wait_ge(sV.h, um)
                        dW.inc(sy.dma_start(dbg_u.ap()[b], u_sb[:, :]), 16)
                        sy.wait_ge(dW.h, dW.n)
                    blk.sync(_)

                lnao_mark = emit_ln_stats(ao, False, ao_ready)
                loc_pairs = []
                for fc in range(NFC):
                    p = fc // 2
                    rguard = (loc_pairs[p - 2] if p >= 2 else
                              aq_prev_pairs[p])
                    ring = yring if p % 2 == 0 else yring2
                    if fc % 2 == 0:
                        mleft = sV.reserve(1)

                        def _(v, fc=fc, rg=rguard, lnm=lnao_mark, xw=xw_n,
                              xbuf=xbuf, xs=xw_sem, ring=ring):
                            if fc == 0:
                                v.wait_ge(sV.h, lnm)
                                v.wait_ge(xs.h, xw)
                            if rg:
                                v.wait_ge(dY.h, rg)
                            v.tensor_mul(e1s[:, 0:512],
                                         ao[:, fc * 512:(fc + 1) * 512],
                                         rbc[:, :])
                            v.tensor_add(e1s[:, 512:1024], e1s[:, 0:512],
                                         cbc[:, :])
                            sV.attach(v.tensor_add(
                                ring[:, 0:512], e1s[:, 512:1024],
                                xbuf[:, fc * 512:(fc + 1) * 512]))
                        DV.add(_)
                    else:
                        mright = sPL.reserve(1)

                        def _(g, fc=fc, rg=rguard, lnm=lnao_mark, xw=xw_n,
                              xbuf=xbuf, xs=xw_sem, ring=ring):
                            if fc == 1:
                                g.wait_ge(sV.h, lnm)
                                g.wait_ge(xs.h, xw)
                            if rg:
                                g.wait_ge(dY.h, rg)
                            g.tensor_mul(rs[:, 0:512],
                                         ao[:, fc * 512:(fc + 1) * 512],
                                         rbc[:, :])
                            g.tensor_add(rs[:, 512:1024], rs[:, 0:512],
                                         cbc[:, :])
                            sPL.attach(g.tensor_add(
                                ring[:, 512:1024], rs[:, 512:1024],
                                xbuf[:, fc * 512:(fc + 1) * 512]))
                        DP.add(_)

                        def _(sy, fc=fc, b=b, ml=mleft, mr=mright,
                              ring=ring):
                            sy.wait_ge(sV.h, ml)
                            sy.wait_ge(sPL.h, mr)
                            dst = y_scr.ap()[b].rearrange(
                                "p (c t) -> p c t", c=NFC)[:, fc - 1:fc + 1, :]
                            src = ring[:, :].rearrange("p (c t) -> p c t",
                                                       c=2)
                            dY.inc(sy.dma_start(dst, src), 16)
                        blk.sync(_)
                        loc_pairs.append(dY.n)
                aq_prev_pairs[:] = [loc_pairs[2], loc_pairs[3]]
                yst_marks.append(dY.n)
                prev_u15[0] = u_mark[H - 1]
                prev_u14[0] = u_mark[H - 2]
                prev_rdbc15[0] = rdbc_ev[H - 1]
                prev_rd15[0] = rd_mark[H - 1]
                prev_num15[0] = num_mark[H - 1]
                prev_ypl[0] = sPL.n
                flush(blk)
        if n_q_iters:
            blkQ.__exit__(None, None, None)

        stAq.close()
        stA.close()
        stB = ExitStack()
        win_b = sb(stB, "win_b", [128, 65536], BF16)
        y0 = sb(stB, "y0", [128, 4096], F32)
        ln2_b = sb(stB, "ln2_b", [128, 4096], BF16)
        hid0 = sb(stB, "hid0", [128, 2048], BF16)
        hid1 = sb(stB, "hid1", [128, 2048], BF16)
        G_ch = sb(stB, "G_ch", [128, 512], BF16)
        ffn_b = sb(stB, "ffn_b", [128, 4096], BF16)
        z_fp = sb(stB, "z_fp", [128, 512], F32)
        lgtm = sb(stB, "lgtm", [128, 64], F32)
        gtm_b = sb(stB, "gtm_b", [128, 64], BF16)
        gw0 = sb(stB, "gw0", [128, 16], F32)
        gw1 = sb(stB, "gw1", [128, 16], F32)
        gw2 = sb(stB, "gw2", [128, 16], F32)
        gc0 = sb(stB, "gc0", [128, 1], F32)
        gc1 = sb(stB, "gc1", [128, 1], F32)
        gfm_t = sb(stB, "gfm", [16, 512], BF16)
        gfm = gfm_t[:, :]
        lg0 = z_fp[0:16, :]
        lg1 = z_fp[32:48, :]
        bsc0 = hid0[:, :].bitcast(F32)
        bsc1 = hid1[:, :].bitcast(F32)

        with nc.Block() as blk:
            def _(sy):
                sy.wait_ge(dY.h, dY.n)
                dYL.inc(sy.dma_start(y0[:, :], y_scr.ap()[0]), 16)
                for hh in range(16):
                    dWB.inc(sy.dma_start(
                        win_b[:, hh * 4096:(hh + 1) * 4096],
                        wbf_scr.ap()[hh]), 16)
            blk.sync(_)
        yl_marks = [16]
        ost_marks = []

        if stop_after in ("W0", "A", "AQ") or (
                isinstance(stop_after, str) and stop_after.startswith("A")
                and len(stop_after) > 1):
            n_b_iters = 0
        elif isinstance(stop_after, str) and stop_after.startswith("B"):
            n_b_iters = int(stop_after[1:])
        else:
            n_b_iters = NB
        prev_gel63 = [0]
        prev_ffn7 = [0]
        prev_gtev3 = [0]
        prev_zsq = [0]
        prev_ypl2 = [0]
        if n_b_iters:
            blkB = nc.Block()
            blkB.__enter__()
        for b in range(n_b_iters):
            if True:
                blk = blkB
                ylw = [(dYL, yl_marks[b])]
                ln_mark = emit_ln_stats(y0, False, ylw)

                ln2_v = sV.reserve(4)

                def _(v, lnm=ln_mark):
                    v.wait_ge(sV.h, lnm)
                    for fc in range(0, NFC, 2):
                        v.tensor_mul(bsc0[:, 0:512],
                                     y0[:, fc * 512:(fc + 1) * 512], rbc[:, :])
                        sV.attach(v.tensor_add(
                            ln2_b[:, fc * 512:(fc + 1) * 512],
                            bsc0[:, 0:512], cbc[:, :]))
                DV.add(_)

                ln2_pl = sPL.reserve(4)

                def _(g, lnm=ln_mark):
                    g.wait_ge(sV.h, lnm)
                    for fc in range(1, NFC, 2):
                        g.tensor_mul(bsc1[:, 0:512],
                                     y0[:, fc * 512:(fc + 1) * 512], rbc[:, :])
                        sPL.attach(g.tensor_add(
                            ln2_b[:, fc * 512:(fc + 1) * 512],
                            bsc1[:, 0:512], cbc[:, :]))
                DP.add(_)
                ln2_ready = [(sV, ln2_v), (sPL, ln2_pl)]

                RT = []
                rt_i = [0]
                SRT = []
                srt_i = [0]

                def rt_drain(k):
                    while k > 0 and rt_i[0] < len(RT):
                        DT.add(RT[rt_i[0]])
                        rt_i[0] += 1
                        k -= 1

                def srt_drain(k):
                    while k > 0 and srt_i[0] < len(SRT):
                        DS.add(SRT[srt_i[0]])
                        srt_i[0] += 1
                        k -= 1

                z_mm = sPR.reserve(1)

                def _(te):
                    for s, n in ln2_ready:
                        te.wait_ge(s.h, n)
                    for ec in range(NFC):
                        inst = te.matmul(
                            bank[4][:, :], wr_b[:, ec * 128:(ec + 1) * 128],
                            ln2_b[:, ec * 512:(ec + 1) * 512],
                            start=(ec == 0), stop=(ec == NFC - 1),
                            skip_group_check=True)
                    sPR.attach(inst)
                RT.append(_)

                z_ev = sV.reserve(1)

                def _(v, m=z_mm, pz=prev_zsq[0]):
                    v.wait_ge(sPR.h, m)
                    if pz:
                        v.wait_ge(sS.h, pz)
                    sV.attach(v.tensor_copy(z_fp[:, :], bank[4][:, :]))
                DV.add(_)

                zsq_m = sS.reserve(1)

                def _(sc, m=z_ev):
                    sc.wait_ge(sV.h, m)
                    sS.attach(sc.activation(sqring[:, 0:512], z_fp[:, :],
                                            AF.Square))
                DS.add(_)

                ssq_m = sPR.reserve(1)

                def _(te, m=zsq_m, pg=prev_gtev3[0]):
                    te.wait_ge(sS.h, m)
                    if pg:
                        te.wait_ge(sV.h, pg)
                    sPR.attach(te.matmul(bank[5][0:1, :], ones_cb[:, :],
                                         sqring[:, 0:512], start=True,
                                         stop=True, skip_group_check=True))
                RT.append(_)

                rn_m = sS.reserve(2)

                def _(sc, m=ssq_m, base=rn_m - 2):
                    sc.wait_ge(sPR.h, m)
                    sS.attach(sc.activation(r0[:, :], bank[5][0:1, :], AF.Ln))
                    sc.wait_ge(sS.h, base + 1)
                    sS.attach(sc.activation(r1[:, :], r0[:, :], AF.Exp,
                                            scale=-0.5))
                DS.add(_)

                rn2_m = sV.reserve(2)

                def _(v, m=rn_m, base=rn2_m - 2):
                    v.wait_ge(sS.h, m)
                    sV.attach(v.tensor_scalar_min(r3[:, :], r1[:, :], 1e12))
                    v.wait_ge(sV.h, base + 1)
                    sV.attach(v.tensor_scalar_mul(r2[:, :], r3[:, :],
                                                  expT[0:1, 0:1]))
                DV.add(_)

                lg_mm = sPR.reserve(1)

                def _(te, m=z_ev):
                    te.wait_ge(sV.h, m)
                    sPR.attach(te.matmul(bank[4][0:16, :], cnT[:, :],
                                         z_fp[:, :], start=True, stop=True,
                                         skip_group_check=True))
                RT.append(_)

                lg_ev = sV.reserve(2)

                def _(v, m=lg_mm, m2=rn2_m, base=lg_ev - 2, mz=zsq_m):
                    v.wait_ge(sPR.h, m)
                    v.wait_ge(sS.h, mz)
                    sV.attach(v.tensor_copy(lg0, bank[4][0:16, :]))
                    v.wait_ge(sV.h, m2)
                    sV.attach(v.tensor_copy(rden_b[:, 0:512], r2[:, :]))
                DV.add(_)

                rnbc_m = sPR.reserve(1)

                def _(te, m=lg_ev, mr=rn_m):
                    te.wait_ge(sV.h, m)
                    te.wait_ge(sS.h, mr)
                    sPR.attach(te.matmul(bank[5][0:16, :], ones_rb[0:1, 0:16],
                                         rden_b[0:1, 0:512], start=True,
                                         stop=True, skip_group_check=True))
                RT.append(_)

                lg1_m = sV.reserve(1)

                def _(v, m=rnbc_m):
                    v.wait_ge(sPR.h, m)
                    sV.attach(v.tensor_mul(lg1, lg0, bank[5][0:16, :]))
                DV.add(_)

                lt_ev = []
                for c in range(NJC):
                    tbk = bank[3 + 2 * (c % 2)]
                    m = sPR.reserve(1)

                    def _(te, c=c, tbk=tbk, lg1_m=lg1_m):
                        te.wait_ge(sV.h, lg1_m)
                        if c >= 2:
                            te.wait_ge(sV.h, lt_ev[c - 2])
                        sPR.attach(te.transpose(tbk[:, c * 16:(c + 1) * 16],
                                                lg1[:, c * 128:(c + 1) * 128],
                                                ident_f[32:48, 32:48]))
                    RT.append(_)

                    mv = sV.reserve(1)

                    def _(v, c=c, tbk=tbk, m=m):
                        v.wait_ge(sPR.h, m)
                        sV.attach(v.tensor_copy(lgtm[:, c * 16:(c + 1) * 16],
                                                tbk[:, c * 16:(c + 1) * 16]))
                    DV.add(_)
                    lt_ev.append(mv)

                sm_exp = []
                g_done = []
                for c in range(NJC):
                    sub_m = sV.reserve(2)

                    def _(v, c=c, base=sub_m - 2):
                        v.wait_ge(sV.h, lt_ev[c])
                        sV.attach(v.reduce_max(gc0[:, :],
                                               lgtm[:, c * 16:(c + 1) * 16],
                                               axis=AX))
                        v.wait_ge(sV.h, base + 1)
                        sV.attach(v.tensor_scalar(
                            gw0[:, :], lgtm[:, c * 16:(c + 1) * 16],
                            gc0[:, :], None, OP.subtract))
                    DV.add(_)

                    me = sSS.reserve(1)

                    def _(sc, c=c, m=sub_m):
                        sc.wait_ge(sV.h, m)
                        sSS.attach(sc.activation(gw1[:, :], gw0[:, :],
                                                 AF.Exp))
                    SRT.append(_)
                    sm_exp.append(me)

                    mg = sV.reserve(9)

                    def _(v, c=c, me=me, base=mg - 9):
                        k = [base]

                        def step(inst):
                            k[0] += 1
                            sV.attach(inst)
                            v.wait_ge(sV.h, k[0])
                        v.wait_ge(sSS.h, me)
                        step(v.reduce_sum(gc1[:, :], gw1[:, :], axis=AX))
                        step(v.reciprocal(gc0[:, :], gc1[:, :]))
                        step(v.tensor_scalar(gw0[:, :], gw1[:, :], gc0[:, :],
                                             None, OP.mult))
                        step(v.reduce_max(gc1[:, :], gw0[:, :], axis=AX))
                        step(v.tensor_scalar(gw1[:, :], gw0[:, :], gc1[:, :],
                                             None, OP.is_equal))
                        step(v.scalar_tensor_tensor(gw2[:, :], gw1[:, :],
                                                    -1e30, gw0[:, :],
                                                    OP.mult, OP.add))
                        step(v.reduce_max(gc1[:, :], gw2[:, :], axis=AX))
                        step(v.tensor_scalar(gw2[:, :], gw0[:, :], gc1[:, :],
                                             None, OP.is_ge))
                        sV.attach(v.tensor_mul(gtm_b[:, c * 16:(c + 1) * 16],
                                               gw0[:, :], gw2[:, :]))
                    DV.add(_)
                    g_done.append(mg)

                gt_ev = []
                for c in range(NJC):
                    tbk = bank[3 + 2 * (c % 2)]
                    m = sPR.reserve(1)

                    def _(te, c=c, tbk=tbk):
                        te.wait_ge(sV.h, g_done[c])
                        if c >= 2:
                            te.wait_ge(sV.h, gt_ev[c - 2])
                        sPR.attach(te.transpose(
                            tbk[:, :].bitcast(BF16)[0:16, c * 128:(c + 1) * 128],
                            gtm_b[:, c * 16:(c + 1) * 16], ident_b[:, :]))
                    RT.append(_)

                    mv = sV.reserve(1)

                    def _(v, c=c, tbk=tbk, m=m):
                        v.wait_ge(sPR.h, m)
                        sV.attach(v.tensor_copy(
                            gfm[:, c * 128:(c + 1) * 128],
                            tbk[:, :].bitcast(BF16)[0:16, c * 128:(c + 1) * 128]))
                    DV.add(_)
                    gt_ev.append(mv)
                gfm_m = sV.n

                gel_m = {}
                o_mm = {}
                G_ev = {}
                ffn_ev = {}

                for h in range(H):
                    if h % 2 == 0:
                        c = h // 2
                        gsem = sPR if c == 0 else sPE
                        m = gsem.reserve(1)

                        def _(te, c=c, gfm_m=gfm_m, gsem=gsem):
                            te.wait_ge(sV.h, gfm_m)
                            if c >= 1:
                                te.wait_ge(sV.h, G_ev[c - 1])
                            gsem.attach(te.matmul(
                                bank[3][:, :],
                                et_b[0:16, c * 128:(c + 1) * 128],
                                gfm[:, :], start=True, stop=True,
                                skip_group_check=True))
                        (RT.append if c == 0 else DT.add)(_)

                        mv = sV.reserve(1)

                        def _(v, c=c, m=m, gsem=gsem):
                            v.wait_ge(gsem.h, m)
                            sV.attach(v.tensor_copy(G_ch[:, :],
                                                    bank[3][:, :]))
                        DV.add(_)
                        G_ev[c] = mv

                    for hc in range(4):
                        gi = h * 4 + hc
                        m = sPE.reserve(1)

                        def _(te, h=h, hc=hc, gi=gi, pg=prev_gel63[0]):
                            bk = bank[gi % 2]
                            if h == 0 and hc == 0:
                                for s, n in ln2_ready:
                                    te.wait_ge(s.h, n)
                            if gi < 2 and pg:
                                te.wait_ge(sS.h, pg)
                            if gi >= 2:
                                te.wait_ge(sS.h, gel_m[gi - 2])
                            if h == 0 and hc == 0:
                                te.wait_ge(dWB.h, 256)
                            for ec in range(NFC):
                                inst = te.matmul(
                                    bk[:, :],
                                    win_b[:, h * 4096 + ec * 512 + hc * 128:
                                          h * 4096 + ec * 512 + hc * 128 + 128],
                                    ln2_b[:, ec * 512:(ec + 1) * 512],
                                    start=(ec == 0), stop=(ec == NFC - 1),
                                    skip_group_check=True)
                            sPE.attach(inst)
                        if h == 0:
                            rt_drain([1, 3, 4, 0][hc])
                        elif h == 1:
                            rt_drain([2, 2, 1, 0][hc])
                        DT.add(_)

                        ms = sS.reserve(1)

                        def _(sc, h=h, hc=hc, gi=gi, m=m):
                            hbuf = hid0 if h % 2 == 0 else hid1
                            sc.wait_ge(sPE.h, m)
                            if h >= 2:
                                sc.wait_ge(sPE.h, o_mm[h - 2])
                            bk = bank[gi % 2]
                            sS.attach(sc.activation(
                                hbuf[:, hc * 512:(hc + 1) * 512], bk[:, :],
                                AF.Gelu))
                        DS.add(_)
                        gel_m[gi] = ms

                    if h == 0:
                        srt_drain(99)
                    if h == 1:
                        rt_drain(99)
                    mo = sPE.reserve(1)

                    def _(te, h=h, pf=prev_ffn7[0]):
                        hbuf = hid0 if h % 2 == 0 else hid1
                        for hc in range(4):
                            te.wait_ge(sS.h, gel_m[h * 4 + hc])
                        if h < 2 and pf:
                            te.wait_ge(sV.h, pf)
                        if h >= 2:
                            te.wait_ge(sV.h, ffn_ev[h // 2 - 1])
                        for hc in range(4):
                            inst = te.matmul(
                                bank[2][(h % 2) * 64:(h % 2) * 64 + 64, :],
                                wout_b[:, h * 256 + hc * 64: h * 256 + hc * 64 + 64],
                                hbuf[:, hc * 512:(hc + 1) * 512],
                                start=(hc == 0), stop=(hc == 3),
                                skip_group_check=True)
                        sPE.attach(inst)
                    DT.add(_)
                    o_mm[h] = mo

                    if h % 2 == 1:
                        c = h // 2
                        mv = sV.reserve(1)

                        def _(v, c=c, mo=mo, pyl=prev_ypl2[0]):
                            v.wait_ge(sPE.h, mo)
                            v.wait_ge(sV.h, G_ev[c])
                            if c == 0 and pyl:
                                v.wait_ge(sPL.h, pyl)
                            sV.attach(v.tensor_mul(
                                ffn_b[:, c * 512:(c + 1) * 512],
                                bank[2][:, :], G_ch[:, :]))
                        DV.add(_)
                        ffn_ev[c] = mv
                ffn_done = sV.n

                if dbg:
                    def _(sy, b=b, fd=ffn_done, gm=gfm_m):
                        sy.wait_ge(sV.h, fd)
                        dW.inc(sy.dma_start(dbg_ffn.ap()[b], ffn_b[:, :]), 16)
                        dW.inc(sy.dma_start(dbg_hid.ap()[b], hid0[:, :]), 16)
                        sy.wait_ge(sV.h, gm)
                        dW.inc(sy.dma_start(dbg_gtm.ap()[b],
                                            lgtm[:, :]), 16)
                        sy.wait_ge(dW.h, dW.n)
                    blk.sync(_)
                ln3_mark = emit_ln_stats(ffn_b, True, [(sV, ffn_done)])
                oloc = []
                for fc in range(NFC):
                    rguard = (oloc[-1] if oloc else
                              (ost_marks[-1] if ost_marks else 0))
                    if fc % 2 == 0:
                        mleft = sV.reserve(1)

                        def _(v, fc=fc, rg=rguard, lnm=ln3_mark):
                            if fc == 0:
                                v.wait_ge(sV.h, lnm)
                            if rg:
                                v.wait_ge(dO.h, rg)
                            v.tensor_mul(bsc0[:, 0:512],
                                         ffn_b[:, fc * 512:(fc + 1) * 512],
                                         rbc[:, :])
                            v.tensor_add(bsc0[:, 512:1024], bsc0[:, 0:512],
                                         cbc[:, :])
                            sV.attach(v.tensor_add(
                                yring[:, 0:512], bsc0[:, 512:1024],
                                y0[:, fc * 512:(fc + 1) * 512]))
                        DV.add(_)
                    else:
                        mright = sPL.reserve(1)

                        def _(g, fc=fc, rg=rguard, lnm=ln3_mark):
                            if fc == 1:
                                g.wait_ge(sV.h, lnm)
                            if rg:
                                g.wait_ge(dO.h, rg)
                            g.tensor_mul(bsc1[:, 0:512],
                                         ffn_b[:, fc * 512:(fc + 1) * 512],
                                         rbc[:, :])
                            g.tensor_add(bsc1[:, 512:1024], bsc1[:, 0:512],
                                         cbc[:, :])
                            sPL.attach(g.tensor_add(
                                yring[:, 512:1024], bsc1[:, 512:1024],
                                y0[:, fc * 512:(fc + 1) * 512]))
                        DP.add(_)

                        def _(sy, fc=fc, b=b, ml=mleft, mr=mright):
                            sy.wait_ge(sV.h, ml)
                            sy.wait_ge(sPL.h, mr)
                            dstt = outT.ap().rearrange("(c p) t -> p c t",
                                                       p=128)
                            dst = dstt[:, fc - 1:fc + 1, b * P:(b + 1) * P]
                            src = yring[:, :].rearrange("p (c t) -> p c t",
                                                        c=2)
                            dO.inc(sy.dma_start(dst, src), 16)
                        blk.sync(_)
                        oloc.append(dO.n)
                ost_marks.append(dO.n)

                if dbg2:
                    def _(sy, b=b, mo=dO.n):
                        sy.wait_ge(dO.h, mo)
                        dW.inc(sy.dma_start(dbg_ffn2.ap()[b], ffn_b[:, :]),
                               16)
                        dW.inc(sy.dma_start(dbg_g2.ap()[b], gfm), 16)
                        sy.wait_ge(dW.h, dW.n)
                    blk.sync(_)
                if b + 1 < NB:
                    def _(sy, b=b, mv=sV.n, mpl=sPL.n):
                        sy.wait_ge(sV.h, mv)
                        sy.wait_ge(sPL.h, mpl)
                        dYL.inc(sy.dma_start(y0[:, :], y_scr.ap()[b + 1]),
                                16)
                    blk.sync(_)
                    yl_marks.append(dYL.n)
                prev_gel63[0] = gel_m[4 * H - 1]
                prev_ffn7[0] = ffn_ev[H // 2 - 1]
                prev_gtev3[0] = gt_ev[NJC - 1]
                prev_zsq[0] = zsq_m
                prev_ypl2[0] = sPL.n
                flush(blk)
        if n_b_iters:
            blkB.__exit__(None, None, None)

        with nc.Block() as blk:
            def _(sy):
                if dbg2 and n_b_iters:
                    dW.inc(sy.dma_start(dbg_wb.ap()[0],
                                        win_b[:, 11 * 4096:12 * 4096]), 16)
                    dW.inc(sy.dma_start(dbg_wb.ap()[1],
                                        win_b[:, 12 * 4096:13 * 4096]), 16)
                    sy.wait_ge(dW.h, dW.n)
                sy.wait_ge(dO.h, dO.n)
            blk.sync(_)

        stB.close()
    return nc


def _prep_weights(inputs):
    Wq = inputs["L0_attn_Wq"]
    Wk = inputs["L0_attn_Wk"]
    Wv = inputs["L0_attn_Wv"]
    Wo = inputs["L0_attn_Wo"]

    def wtile(W):
        return np.ascontiguousarray(
            W.T.reshape(8, 128, 1024).transpose(1, 0, 2).reshape(128, 8192)
        ).astype(np.float32)

    wr = inputs["L0_ffn_w_router"]
    wr_h = np.ascontiguousarray(
        wr.T.reshape(8, 128, 128).transpose(1, 0, 2).reshape(128, 1024)
    ).astype(np.float32)
    ex_in = inputs["L0_ffn_input_experts"]
    win_h = np.ascontiguousarray(
        ex_in.reshape(16, 8, 128, 4, 128).transpose(2, 0, 1, 3, 4)
        .reshape(128, 65536)).astype(np.float32)
    ex_out = inputs["L0_ffn_output_experts"]
    wout_h = np.ascontiguousarray(
        ex_out.reshape(16, 4, 128, 64).transpose(2, 0, 1, 3)
        .reshape(128, 4096)).astype(np.float32)
    bo = inputs["L0_attn_bo"]
    bo_h = np.ascontiguousarray(bo.reshape(8, 128).T).astype(np.float32)
    cT = np.ascontiguousarray(
        inputs["L0_ffn_head_centroids"].T).astype(np.float32)
    temp = np.asarray(inputs["L0_ffn_temperature"],
                      dtype=np.float32).reshape(1, 1)
    et = np.zeros((16, 1024), dtype=np.float32)
    for n in range(16):
        et[n, n * 64:(n + 1) * 64] = 1.0
    ident = np.eye(128, dtype=np.float32)
    maskd = np.triu(np.ones((128, 128), dtype=np.float32))
    return {
        "wq_h": wtile(Wq), "wk_h": wtile(Wk), "wv_h": wtile(Wv),
        "wo_h": wtile(Wo), "wr_hh": wr_h, "win_h": win_h, "wout_hh": wout_h,
        "bo_h": bo_h, "cT_h": cT, "temp_h": temp, "et_h": et,
        "ident_h": ident, "maskd_h": maskd,
    }


def kernel(**inputs):
    x = np.asarray(inputs["L0_attn_x"], dtype=np.float32)
    B, T, _ = x.shape
    shared = _prep_weights(inputs)

    nc = build_nc()

    in_maps = []
    xp_cache = {}
    for c in range(8):
        bi, half = c // 2, c % 2
        xs = np.ascontiguousarray(
            x[bi, half * TOK:(half + 1) * TOK, :].T).astype(np.float32)
        if bi not in xp_cache:
            xp_cache[bi] = np.ascontiguousarray(
                x[bi, 0:TOK, :].T).astype(np.float32)
        m = dict(shared)
        m["xT"] = xs
        m["xpT"] = xp_cache[bi]
        m["gm_h"] = np.full((128, 1), 1.0 if half == 1 else 0.0,
                            dtype=np.float32)
        in_maps.append(m)

    res = run_bass_kernel_spmd(nc, in_maps, core_ids=list(range(8)))

    out = np.empty((B, T, E), dtype=np.float32)
    for c in range(8):
        bi, half = c // 2, c % 2
        out[bi, half * TOK:(half + 1) * TOK, :] = res.results[c]["outT"].T
    return out


# version 20 (final)
# speedup vs baseline: 1.0229x; 1.0229x over previous
import numpy as np
from contextlib import ExitStack

import concourse.bass as bass
import concourse.mybir as mybir
from concourse.bass_utils import run_bass_kernel_spmd
from concourse.alu_op_type import AluOpType as OP

F32 = mybir.dt.float32
BF16 = mybir.dt.bfloat16
AF = mybir.ActivationFunctionType
AX = mybir.AxisListType.X

TOK = 4096
P = 512
NB = TOK // P
E = 1024
NFC = E // 128
H = 16
D = 64
NJC = P // 128
INV_E = 1.0 / E
LN_EPS = 1e-5
ATT_EPS = 1e-5
INV_SQRT_G = 1.0 / np.sqrt(128.0)


class Sem:
    def __init__(self, nc, ctx, name):
        self.h = ctx.enter_context(nc.semaphore(name))
        self.n = 0

    def reserve(self, k=1):
        self.n += k
        return self.n

    def attach(self, inst, k=1):
        inst.then_inc(self.h, k)

    def inc(self, inst, k=1):
        inst.then_inc(self.h, k)
        self.n += k
        return self.n


class Dfr:

    def __init__(self, name):
        self.name = name
        self.q = []

    def add(self, fn):
        self.q.append(fn)

    def flush(self, blk):
        if not self.q:
            return
        q = self.q
        self.q = []

        def body(eng):
            for f in q:
                f(eng)
        getattr(blk, self.name)(body)


def build_nc(stop_after=None, dbg=False, dbg2=False, a_parts='lkvts'):
    nc = bass.Bass("TRN2", target_bir_lowering=False)

    xT = nc.dram_tensor("xT", [E, TOK], F32, kind="ExternalInput")
    wq_h = nc.dram_tensor("wq_h", [128, 8192], F32, kind="ExternalInput")
    wk_h = nc.dram_tensor("wk_h", [128, 8192], F32, kind="ExternalInput")
    wv_h = nc.dram_tensor("wv_h", [128, 8192], F32, kind="ExternalInput")
    wo_h = nc.dram_tensor("wo_h", [128, 8192], F32, kind="ExternalInput")
    bo_h = nc.dram_tensor("bo_h", [128, 8], F32, kind="ExternalInput")
    wr_hh = nc.dram_tensor("wr_hh", [128, 1024], F32, kind="ExternalInput")
    cT_h = nc.dram_tensor("cT_h", [128, 16], F32, kind="ExternalInput")
    temp_h = nc.dram_tensor("temp_h", [1, 1], F32, kind="ExternalInput")
    win_h = nc.dram_tensor("win_h", [128, 65536], F32, kind="ExternalInput")
    wout_hh = nc.dram_tensor("wout_hh", [128, 4096], F32, kind="ExternalInput")
    et_h = nc.dram_tensor("et_h", [16, 1024], F32, kind="ExternalInput")
    ident_h = nc.dram_tensor("ident_h", [128, 128], F32, kind="ExternalInput")
    maskd_h = nc.dram_tensor("maskd_h", [128, 128], F32, kind="ExternalInput")
    xpT = nc.dram_tensor("xpT", [E, TOK], F32, kind="ExternalInput")
    gm_h = nc.dram_tensor("gm_h", [128, 1], F32, kind="ExternalInput")
    outT = nc.dram_tensor("outT", [E, TOK], F32, kind="ExternalOutput")

    _sk = "ExternalOutput" if dbg else "Internal"
    ln1_scr = nc.dram_tensor("ln1_scr", [NB, 128, 4096], BF16, kind=_sk)
    phk_scr = nc.dram_tensor("phk_scr", [NB, 128, 4096], BF16, kind=_sk)
    v_scr = nc.dram_tensor("v_scr", [NB, 128, 4160], BF16, kind=_sk)
    y_scr = nc.dram_tensor("y_scr", [NB, 128, 4096], F32, kind=_sk)
    wbf_scr = nc.dram_tensor("wbf_scr", [16, 128, 4096], BF16,
                             kind="ExternalOutput" if dbg2 else "Internal")
    if dbg2:
        dbg_ffn2 = nc.dram_tensor("dbg_ffn2", [NB, 128, 4096], BF16,
                                  kind="ExternalOutput")
        dbg_g2 = nc.dram_tensor("dbg_g2", [NB, 16, 512], BF16,
                                kind="ExternalOutput")
        dbg_wb = nc.dram_tensor("dbg_wb", [2, 128, 4096], BF16,
                                kind="ExternalOutput")
    if dbg:
        dbg_ao = nc.dram_tensor("dbg_ao", [NB, 128, 4096], F32,
                                kind="ExternalOutput")
        dbg_u = nc.dram_tensor("dbg_u", [NB, 128, 4096], BF16,
                               kind="ExternalOutput")
        dbg_ffn = nc.dram_tensor("dbg_ffn", [NB, 128, 4096], BF16,
                                 kind="ExternalOutput")
        dbg_gtm = nc.dram_tensor("dbg_gtm", [NB, 128, 64], F32,
                                 kind="ExternalOutput")
        dbg_hid = nc.dram_tensor("dbg_hid", [NB, 128, 2048], BF16,
                                 kind="ExternalOutput")

    ctx = ExitStack()
    with ctx:
        def sb(st, name, shape, dt):
            return st.enter_context(nc.sbuf_tensor(name, shape, dt))

        ident_f = sb(ctx, "ident_f", [128, 128], F32)
        ident_b = sb(ctx, "ident_b", [128, 128], BF16)
        ones_cf = sb(ctx, "ones_cf", [128, 1], F32)
        ones_cb = sb(ctx, "ones_cb", [128, 1], BF16)
        ones_rf = sb(ctx, "ones_rf", [1, 128], F32)
        ones_rb = sb(ctx, "ones_rb", [1, 128], BF16)
        ones_mb = sb(ctx, "ones_mb", [128, 1], BF16)
        cnT = sb(ctx, "cnT", [128, 16], F32)
        expT = sb(ctx, "expT", [1, 1], F32)
        r0 = sb(ctx, "r0", [1, 512], F32)
        r1 = sb(ctx, "r1", [1, 512], F32)
        r2 = sb(ctx, "r2", [1, 512], F32)
        r3 = sb(ctx, "r3", [1, 512], F32)
        rden_b = sb(ctx, "rden_b", [1, 512], BF16)
        rbc = sb(ctx, "rbc", [128, 512], F32)
        cbc = sb(ctx, "cbc", [128, 512], F32)
        sqring = sb(ctx, "sqring", [128, 1024], BF16)
        yring = sb(ctx, "yring", [128, 1024], F32)
        wr_b = sb(ctx, "wr_b", [128, 1024], BF16)
        et_b = sb(ctx, "et_b", [16, 1024], BF16)
        wout_b = sb(ctx, "wout_b", [128, 4096], BF16)
        castr = sb(ctx, "castr", [128, 1024], BF16)

        bank = [ctx.enter_context(nc.psum_tensor(f"bk{i}", [128, 512], F32))
                for i in range(8)]

        sPE = Sem(nc, ctx, "sPE")
        sV = Sem(nc, ctx, "sV")
        sS = Sem(nc, ctx, "sS")
        sPL = Sem(nc, ctx, "sPL")
        sPR = Sem(nc, ctx, "sPR")
        sSS = Sem(nc, ctx, "sSS")
        dX = Sem(nc, ctx, "dX")
        dX2 = Sem(nc, ctx, "dX2")
        dSL = Sem(nc, ctx, "dSL")
        dSC = Sem(nc, ctx, "dSC")
        dY = Sem(nc, ctx, "dY")
        dYL = Sem(nc, ctx, "dYL")
        dW = Sem(nc, ctx, "dW")
        dWS = Sem(nc, ctx, "dWS")
        dWB = Sem(nc, ctx, "dWB")
        dO = Sem(nc, ctx, "dO")

        DT = Dfr("tensor")
        DV = Dfr("vector")
        DS = Dfr("scalar")
        DP = Dfr("gpsimd")

        def flush(blk):
            DT.flush(blk)
            DV.flush(blk)
            DS.flush(blk)
            DP.flush(blk)

        stA = ExitStack()
        maskd = sb(stA, "maskd", [128, 128], F32)
        bo_sb = sb(stA, "bo_sb", [128, 8], F32)
        gm_sb = sb(stA, "gm_sb", [128, 1], F32)
        wq_b = sb(stA, "wq_b", [128, 8192], BF16)
        wo_b = sb(stA, "wo_b", [128, 8192], BF16)
        S_sb = sb(stA, "S_sb", [128, 512], F32)
        z_sb = sb(stA, "z_sb", [128, 8], F32)
        ckS = sb(stA, "ckS", [128, 4096], F32)
        ckz = sb(stA, "ckz", [128, 64], F32)
        Spre_sb = sb(stA, "Spre_sb", [128, 512], F32)
        zpre = sb(stA, "zpre", [128, 8], F32)
        x0 = sb(stA, "x0", [128, 4096], F32)
        ln1_b = sb(stA, "ln1_b", [128, 4096], BF16)
        phk_f = sb(stA, "phk_f", [128, 4096], BF16)
        V0 = sb(stA, "V0", [128, 4160], BF16)
        e1s = sb(stA, "e1s", [128, 1024], F32)
        rs = sb(stA, "rs", [128, 1024], F32)

        stAk = ExitStack()
        wk_b = sb(stAk, "wk_b", [128, 8192], BF16)
        wv_b = sb(stAk, "wv_b", [128, 8192], BF16)
        phk_t = sb(stAk, "phk_t", [128, 4096], BF16)
        wstg = sb(stAk, "wstg", [128, 4096], F32)
        wcast = sb(stAk, "wcast", [128, 4096], BF16)

        wjobs = []
        wjobs.append((wq_h.ap()[:, 0:4096], 4096, wq_b[:, 0:4096], None))
        wjobs.append((wq_h.ap()[:, 4096:8192], 4096, wq_b[:, 4096:8192], None))
        wjobs.append((wo_h.ap()[:, 0:4096], 4096, wo_b[:, 0:4096], None))
        wjobs.append((wo_h.ap()[:, 4096:8192], 4096, wo_b[:, 4096:8192], None))
        wjobs.append((wr_hh.ap()[:, :], 1024, wr_b[:, :], None))
        wjobs.append((wout_hh.ap()[:, :], 4096, wout_b[:, :], None))
        for hh in range(16):
            wjobs.append((win_h.ap()[:, hh * 4096:(hh + 1) * 4096], 4096,
                          None, hh))
        wstate = {"free_pl": 0, "wc_store": 0, "ji": 0}

        def emit_wjobs(blk, kmax):
            for _ in range(kmax):
                if wstate["ji"] >= len(wjobs):
                    return
                src, width, dst, scr = wjobs[wstate["ji"]]
                wstate["ji"] += 1
                fpl = wstate["free_pl"]
                wcs = wstate["wc_store"]

                def _(sy, src=src, width=width, fpl=fpl):
                    if fpl:
                        sy.wait_ge(sPL.h, fpl)
                    dW.inc(sy.dma_start(wstg[:, 0:width], src), 16)
                blk.sync(_)
                ld_mark = dW.n
                cm = sPL.reserve(1)
                if dst is not None:
                    def _(g, dst=dst, width=width, ld=ld_mark):
                        g.wait_ge(dW.h, ld)
                        sPL.attach(g.tensor_copy(dst, wstg[:, 0:width]))
                    DP.add(_)
                    wstate["free_pl"] = cm
                else:
                    def _(g, width=width, ld=ld_mark, wcs=wcs):
                        g.wait_ge(dW.h, ld)
                        if wcs:
                            g.wait_ge(dWS.h, wcs)
                        sPL.attach(g.tensor_copy(wcast[:, 0:width],
                                                 wstg[:, 0:width]))
                    DP.add(_)
                    wstate["free_pl"] = cm

                    def _(sy, scr=scr, cm=cm):
                        sy.wait_ge(sPL.h, cm)
                        dWS.inc(sy.dma_start(wbf_scr.ap()[scr], wcast[:, :]),
                                16)
                    blk.sync(_)
                    wstate["wc_store"] = dWS.n

        lnst = {"b67_free_v": 0, "stat_last": 0}

        def emit_ln_stats(src, srcbf, src_ready):
            free_v = lnst["b67_free_v"]
            stat_last = lnst["stat_last"]
            pool_guard = sPL.n
            sq_m = []
            summ_m = []
            sqm_m = []
            for fc in range(NFC):
                if srcbf:
                    m = sS.reserve(1)

                    def _(sc, fc=fc, stat_last=stat_last):
                        if fc == 0:
                            for s, n in src_ready:
                                sc.wait_ge(s.h, n)
                        if fc < 2 and stat_last:
                            sc.wait_ge(sPE.h, stat_last)
                        if fc >= 2:
                            sc.wait_ge(sPE.h, sqm_m[fc - 2])
                        sS.attach(sc.activation(
                            sqring[:, (fc % 2) * 512:(fc % 2) * 512 + 512],
                            src[:, fc * 512:(fc + 1) * 512], AF.Square))
                    DS.add(_)
                    sq_m.append(m)
                else:
                    m = sS.reserve(2)

                    def _(sc, fc=fc, stat_last=stat_last):
                        if fc == 0:
                            for s, n in src_ready:
                                sc.wait_ge(s.h, n)
                        if fc < 2 and stat_last:
                            sc.wait_ge(sPE.h, stat_last)
                        if fc >= 2:
                            sc.wait_ge(sPE.h, summ_m[fc - 2])
                            sc.wait_ge(sPE.h, sqm_m[fc - 2])
                        sS.attach(sc.copy(
                            castr[:, (fc % 2) * 512:(fc % 2) * 512 + 512],
                            src[:, fc * 512:(fc + 1) * 512]))
                        sS.attach(sc.activation(
                            sqring[:, (fc % 2) * 512:(fc % 2) * 512 + 512],
                            castr[:, (fc % 2) * 512:(fc % 2) * 512 + 512],
                            AF.Square))
                    DS.add(_)
                    sq_m.append(m)

            for fc in range(NFC):
                m1 = sPE.reserve(1)
                m2 = sPE.reserve(1)

                def _(te, fc=fc):
                    if fc == 0 and free_v:
                        te.wait_ge(sV.h, free_v)
                    te.wait_ge(sS.h, sq_m[fc])
                    if srcbf:
                        sPE.attach(te.matmul(bank[6][0:1, :], ones_mb[:, :],
                                             src[:, fc * 512:(fc + 1) * 512],
                                             start=(fc == 0),
                                             stop=(fc == NFC - 1),
                                             skip_group_check=True))
                    else:
                        sPE.attach(te.matmul(
                            bank[6][0:1, :], ones_mb[:, :],
                            castr[:, (fc % 2) * 512:(fc % 2) * 512 + 512],
                            start=(fc == 0), stop=(fc == NFC - 1),
                            skip_group_check=True))
                    sPE.attach(te.matmul(
                        bank[7][0:1, :], ones_mb[:, :],
                        sqring[:, (fc % 2) * 512:(fc % 2) * 512 + 512],
                        start=(fc == 0), stop=(fc == NFC - 1),
                        skip_group_check=True))
                DT.add(_)
                summ_m.append(m1)
                sqm_m.append(m2)
            stat_mark = sPE.n

            sqz = sS.reserve(1)

            def _(sc):
                sc.wait_ge(sPE.h, stat_mark)
                sS.attach(sc.activation(r2[:, :], bank[6][0:1, :],
                                        AF.Square))
            DS.add(_)

            fin = sV.reserve(1)

            def _(v, sqz=sqz):
                v.wait_ge(sS.h, sqz)
                sV.attach(v.scalar_tensor_tensor(r3[:, :], bank[7][0:1, :],
                                                 LN_EPS, r2[:, :],
                                                 OP.add, OP.subtract))
            DV.add(_)

            rstd = sS.reserve(3)

            def _(sc, fin=fin, base=rstd - 3):
                sc.wait_ge(sV.h, fin)
                sS.attach(sc.activation(r1[:, :], r3[:, :], AF.Ln))
                sc.wait_ge(sS.h, base + 1)
                sS.attach(sc.activation(r2[:, :], r1[:, :], AF.Exp,
                                        scale=-0.5))
                sc.wait_ge(sS.h, base + 2)
                sS.attach(sc.copy(sqring[0:1, 0:512], r2[:, :]))
            DS.add(_)

            cneg = sV.reserve(1)

            def _(v, rstd=rstd):
                v.wait_ge(sS.h, rstd - 1)
                sV.attach(v.scalar_tensor_tensor(
                    sqring[0:1, 512:1024], bank[6][0:1, :], -1.0,
                    r2[:, :], OP.mult, OP.mult))
            DV.add(_)

            bc = sPE.reserve(1)

            def _(te, cneg=cneg, rstd=rstd):
                te.wait_ge(sV.h, cneg)
                te.wait_ge(sS.h, rstd)
                te.matmul(bank[6][:, :], ones_rb[0:1, :],
                          sqring[0:1, 0:512], start=True, stop=True,
                          skip_group_check=True)
                sPE.attach(te.matmul(bank[7][:, :], ones_rb[0:1, :],
                                     sqring[0:1, 512:1024], start=True,
                                     stop=True, skip_group_check=True))
            DT.add(_)

            bcev = sV.reserve(1)

            def _(v, bc=bc, pg=pool_guard):
                v.wait_ge(sPE.h, bc)
                if pg:
                    v.wait_ge(sPL.h, pg)
                v.tensor_copy(rbc[:, :], bank[6][:, :])
                sV.attach(v.tensor_copy(cbc[:, :], bank[7][:, :]))
            DV.add(_)
            lnst["b67_free_v"] = bcev
            lnst["stat_last"] = bc
            return bcev

        def emit_elu_proj(w_sb, dst_bf, rhs_bf, rhs_ready, prev_free,
                          mask=None, parts='mse'):
            evac_s = []
            evac_v = []
            mmm = []
            for oc in range(NFC):
                bk = bank[oc % 2]
                s_ = (oc % 2) * 512
                o_ = 512 - s_
                m = sPE.reserve(1)

                def _(te, oc=oc, bk=bk):
                    if oc == 0:
                        for s, n in rhs_ready:
                            te.wait_ge(s.h, n)
                    if oc < 2:
                        for s, n in prev_free:
                            te.wait_ge(s.h, n)
                    if oc >= 2 and 's' in parts:
                        te.wait_ge(sS.h, evac_s[oc - 2])
                    if oc >= 2 and 'e' in parts:
                        te.wait_ge(sV.h, evac_v[oc - 2])
                    for ec in range(NFC):
                        inst = te.matmul(
                            bk[:, :],
                            w_sb[:, ec * 1024 + oc * 128: ec * 1024 + oc * 128 + 128],
                            rhs_bf[:, ec * 512:(ec + 1) * 512],
                            start=(ec == 0), stop=(ec == NFC - 1),
                            skip_group_check=True)
                    sPE.attach(inst)
                DT.add(_)
                mmm.append(m)

                if 's' in parts:
                    ms = sS.reserve(1)

                    def _(sc, oc=oc, bk=bk, s_=s_, m=m):
                        sc.wait_ge(sPE.h, m)
                        if 'e' in parts:
                            if mask is None:
                                if oc >= 2:
                                    sc.wait_ge(sV.h, evac_v[oc - 2])
                            else:
                                if oc >= 1:
                                    sc.wait_ge(sV.h, evac_v[oc - 1])
                        sS.attach(sc.activation(e1s[:, s_:s_ + 512], bk[:, :],
                                                AF.Exp))
                    DS.add(_)
                    evac_s.append(ms)
                else:
                    evac_s.append(0)

                if 'e' in parts:
                    mv = sV.reserve(1)

                    def _(v, oc=oc, bk=bk, s_=s_, o_=o_, m=m, ms=evac_s[-1]):
                        v.wait_ge(sPE.h, m)
                        if ms:
                            v.wait_ge(sS.h, ms)
                        if mask is None:
                            v.tensor_scalar_max(rs[:, s_:s_ + 512], bk[:, :],
                                                0.0)
                            sV.attach(v.scalar_tensor_tensor(
                                dst_bf[:, oc * 512:(oc + 1) * 512],
                                e1s[:, s_:s_ + 512], 1.0, rs[:, s_:s_ + 512],
                                OP.min, OP.add))
                        else:
                            v.tensor_scalar(rs[:, s_:s_ + 512], bk[:, :], 0.0,
                                            mask, OP.max, OP.mult)
                            v.tensor_scalar(rs[:, o_:o_ + 512],
                                            e1s[:, s_:s_ + 512], mask, mask,
                                            OP.mult, OP.min)
                            sV.attach(v.tensor_add(
                                dst_bf[:, oc * 512:(oc + 1) * 512],
                                rs[:, o_:o_ + 512], rs[:, s_:s_ + 512]))
                    DV.add(_)
                    evac_v.append(mv)
                else:
                    evac_v.append(0)
            return evac_v, mmm, evac_s

        with nc.Block() as blk:
            def _(sy):
                dW.inc(sy.dma_start(ident_f[:, :], ident_h.ap()[:, :]), 16)
                dW.inc(sy.dma_start(maskd[:, :], maskd_h.ap()[:, :]), 16)
                dW.inc(sy.dma_start(bo_sb[:, :], bo_h.ap()[:, :]), 16)
                dW.inc(sy.dma_start(gm_sb[:, :], gm_h.ap()[:, :]), 16)
                dW.inc(sy.dma_start(r0[0:1, 0:1], temp_h.ap()[:, :]), 16)
                dW.inc(sy.dma_start(e1s[:, 0:16], cT_h.ap()[:, :]), 16)
                dW.inc(sy.dma_start(wstg[0:16, 0:1024], et_h.ap()[:, :]), 16)
                src = xpT.ap().rearrange("(c p) t -> p c t", p=128)
                dst = x0[:, :].rearrange("p (c t) -> p c t", c=NFC)
                dX.inc(sy.dma_start(dst, src[:, :, 0:P]), 16)
            blk.sync(_)
            w0_ld = dW.n

            def _(v):
                v.memset(ones_cf[:, :], 1.0)
                v.memset(ones_cb[:, :], 1.0)
                v.memset(ones_rf[:, :], 1.0)
                v.memset(ones_rb[:, :], 1.0)
                v.memset(ones_mb[:, :], 1.0 / 1024.0)
                v.memset(S_sb[:, :], 0.0)
                v.memset(z_sb[:, :], 0.0)
                v.memset(Spre_sb[:, :], 0.0)
                v.memset(zpre[:, :], 0.0)
                v0r = V0[:, :].rearrange("p (j h c) -> p j h c", j=4, h=16)
                sV.inc(v.memset(v0r[:, :, :, 64:65], 1.0))
            blk.vector(_)

            def _(g):
                g.wait_ge(dW.h, w0_ld)
                g.tensor_copy(ident_b[:, :], ident_f[:, :])
                sPL.inc(g.tensor_copy(et_b[:, :], wstg[0:16, 0:1024]))
            blk.gpsimd(_)
            wstate["free_pl"] = sPL.n

            def _(sc):
                sc.wait_ge(dW.h, w0_ld)
                sS.inc(sc.activation(r1[0:1, 0:1], r0[0:1, 0:1], AF.Exp))
                sS.inc(sc.activation(rs[:, 0:16], e1s[:, 0:16], AF.Square))
            blk.scalar(_)
            m_expT = sS.n

            def _(v):
                v.wait_ge(sS.h, m_expT)
                sV.inc(v.tensor_scalar_mul(expT[:, :], r1[0:1, 0:1],
                                           INV_SQRT_G))
            blk.vector(_)

            def _(te):
                te.wait_ge(sS.h, m_expT)
                sPE.inc(te.matmul(bank[6][0:1, 0:16], ones_cf[:, :],
                                  rs[:, 0:16], start=True, stop=True,
                                  skip_group_check=True))
            blk.tensor(_)
            m_cssq = sPE.n

            def _(sc):
                sc.wait_ge(sPE.h, m_cssq)
                sS.inc(sc.activation(r1[0:1, 0:16], bank[6][0:1, 0:16],
                                     AF.Ln))
                sc.wait_ge(sS.h, sS.n)
                sS.inc(sc.activation(r2[0:1, 0:16], r1[0:1, 0:16], AF.Exp,
                                     scale=-0.5))
            blk.scalar(_)
            m_crn = sS.n

            def _(v):
                v.wait_ge(sS.h, m_crn)
                sV.inc(v.tensor_scalar_min(r3[0:1, 0:16], r2[0:1, 0:16],
                                           1e12))
            blk.vector(_)
            m_crn2 = sV.n

            def _(te):
                te.wait_ge(sV.h, m_crn2)
                sPE.inc(te.matmul(bank[6][:, 0:16], ones_rf[0:1, :],
                                  r3[0:1, 0:16], start=True, stop=True,
                                  skip_group_check=True))
            blk.tensor(_)
            m_cbc = sPE.n

            def _(v):
                v.wait_ge(sPE.h, m_cbc)
                sV.inc(v.tensor_copy(rs[:, 16:32], bank[6][:, 0:16]))
                v.wait_ge(sV.h, sV.n)
                sV.inc(v.tensor_mul(cnT[:, :], e1s[:, 0:16], rs[:, 16:32]))
            blk.vector(_)
            lnst["b67_free_v"] = sV.n

            wkv_v = 0
            for w_dst, w_src in ((wk_b, wk_h), (wv_b, wv_h)):
                for half in range(2):
                    fpl = wstate["free_pl"]
                    fv = wkv_v

                    def _(sy, w_src=w_src, half=half, fpl=fpl, fv=fv):
                        sy.wait_ge(sPL.h, fpl)
                        if fv:
                            sy.wait_ge(sV.h, fv)
                        dW.inc(sy.dma_start(
                            wstg[:, :],
                            w_src.ap()[:, half * 4096:half * 4096 + 4096]), 16)
                    blk.sync(_)
                    ldm = dW.n

                    def _(g, w_dst=w_dst, half=half, ldm=ldm):
                        g.wait_ge(dW.h, ldm)
                        sPL.inc(g.tensor_copy(
                            w_dst[:, half * 4096:half * 4096 + 2048],
                            wstg[:, 0:2048]))
                    blk.gpsimd(_)
                    wstate["free_pl"] = sPL.n

                    def _(v, w_dst=w_dst, half=half, ldm=ldm):
                        v.wait_ge(dW.h, ldm)
                        sV.inc(v.tensor_copy(
                            w_dst[:, half * 4096 + 2048:half * 4096 + 4096],
                            wstg[:, 2048:4096]))
                    blk.vector(_)
                    wkv_v = sV.n
        wk_ready = sPL.n
        wk_ready_v = wkv_v

        x_marks = {"pe": 0, "s": 0, "v": 0, "pl": 0}
        st_marks = {"ln1": 0, "phk": 0, "v": 0}
        hoist_ln = [None]
        proj01_free = []
        prev_supd = [0]
        prev_trev = [0]
        if stop_after == "W0":
            n_a_iters = 0
        elif isinstance(stop_after, str) and stop_after.startswith("A") and \
                len(stop_after) > 1 and stop_after[1:].isdigit():
            n_a_iters = int(stop_after[1:])
        else:
            n_a_iters = 2 * NB
        if n_a_iters:
            blkA = nc.Block()
            blkA.__enter__()
        for it in range(n_a_iters):
            prefix = it < NB
            b = it % NB
            if True:
                blk = blkA
                xr = [(dX, 16 * (it + 1))]
                ln_mark = emit_ln_stats(x0, False, xr)

                stg = dict(st_marks)
                ln1_v = sV.reserve(4)

                def _(v, stg=stg, ln_mark=ln_mark):
                    v.wait_ge(sV.h, ln_mark)
                    if stg["ln1"]:
                        v.wait_ge(dSC.h, stg["ln1"])
                    for fc in range(0, NFC, 2):
                        v.tensor_mul(e1s[:, 0:512],
                                     x0[:, fc * 512:(fc + 1) * 512], rbc[:, :])
                        sV.attach(v.tensor_add(
                            ln1_b[:, fc * 512:(fc + 1) * 512],
                            e1s[:, 0:512], cbc[:, :]))
                DV.add(_)

                ln1_pl = sPL.reserve(4)

                def _(g, stg=stg, ln_mark=ln_mark):
                    g.wait_ge(sV.h, ln_mark)
                    if stg["ln1"]:
                        g.wait_ge(dSC.h, stg["ln1"])
                    for fc in range(1, NFC, 2):
                        g.tensor_mul(rs[:, 0:512],
                                     x0[:, fc * 512:(fc + 1) * 512], rbc[:, :])
                        sPL.attach(g.tensor_add(
                            ln1_b[:, fc * 512:(fc + 1) * 512],
                            rs[:, 0:512], cbc[:, :]))
                DP.add(_)
                ln1_ready = [(sV, ln1_v), (sPL, ln1_pl)]
                x_marks = {"pe": sPE.n, "s": sS.n, "v": ln1_v, "pl": ln1_pl}

                if it + 1 < 2 * NB:
                    def _(sy, it=it, xm=dict(x_marks)):
                        sy.wait_ge(sPE.h, xm["pe"])
                        sy.wait_ge(sS.h, xm["s"])
                        sy.wait_ge(sV.h, xm["v"])
                        sy.wait_ge(sPL.h, xm["pl"])
                        srct = xpT if (it + 1) < NB else xT
                        nb = (it + 1) % NB
                        src = srct.ap().rearrange("(c p) t -> p c t", p=128)
                        dst = x0[:, :].rearrange("p (c t) -> p c t", c=NFC)
                        dX.inc(sy.dma_start(dst,
                                            src[:, :, nb * P:(nb + 1) * P]),
                               16)
                    blk.sync(_)

                if 'k' not in a_parts:
                    flush(blk)
                    continue
                prev_free = [(sPL, wk_ready), (sV, wk_ready_v)] + \
                    list(proj01_free)
                if st_marks["phk"]:
                    prev_free = prev_free + [(dSC, st_marks["phk"])]
                kp = 'm'
                if '1' not in a_parts:
                    kp += 's'
                if '1' not in a_parts and '2' not in a_parts:
                    kp += 'e'
                phk_marks, _km, phk_es = emit_elu_proj(
                    wk_b, phk_f, ln1_b, ln1_ready, prev_free,
                    mask=(gm_sb[:, :] if (prefix and 'M' not in a_parts)
                          else None), parts=kp)
                if phk_es and phk_es[-1]:
                    proj01_free[:] = [(sS, phk_es[-1]),
                                      (sV, phk_marks[-1])]

                if 'v' not in a_parts:
                    flush(blk)
                    continue
                vevac = []
                vmm = []
                stv = st_marks["v"]
                gi = 0
                for jc in range(NJC):
                    for half in range(2):
                        bk = bank[2 + gi % 2]
                        m = sPE.reserve(1)

                        def _(te, jc=jc, half=half, bk=bk, gi=gi, stv=stv,
                              psu=prev_supd[0]):
                            if gi == 0:
                                for s, n in ln1_ready:
                                    te.wait_ge(s.h, n)
                                te.wait_ge(sPL.h, wk_ready)
                                te.wait_ge(sV.h, wk_ready_v)
                                if stv:
                                    te.wait_ge(dSC.h, stv)
                            if gi < 2 and psu:
                                te.wait_ge(sV.h, psu)
                            if gi >= 2:
                                te.wait_ge(sV.h, vevac[gi - 2])
                            for ec in range(NFC):
                                inst = te.matmul(
                                    bk[:, :],
                                    ln1_b[:, ec * 512 + jc * 128: ec * 512 + jc * 128 + 128],
                                    wv_b[:, ec * 1024 + half * 512: ec * 1024 + half * 512 + 512],
                                    start=(ec == 0), stop=(ec == NFC - 1),
                                    skip_group_check=True)
                            sPE.attach(inst)
                        DT.add(_)
                        vmm.append(m)

                        mv = sV.reserve(1)

                        def _(v, jc=jc, half=half, bk=bk, m=m):
                            v.wait_ge(sPE.h, m)
                            v0r = V0[:, :].rearrange("p (j h c) -> p j h c",
                                                     j=4, h=16)
                            dst = v0r[:, jc, half * 8:half * 8 + 8, 0:64]
                            sV.attach(v.tensor_copy(
                                dst, bk[:, :].rearrange("p (h d) -> p h d",
                                                        h=8)))
                        DV.add(_)
                        vevac.append(mv)
                        gi += 1
                v_done = sV.n

                if 't' not in a_parts:
                    flush(blk)
                    continue
                tr_m = []
                tr_ev = []
                gi = 0
                for jc in range(NJC):
                    for half in range(2):
                        bk = bank[4 + gi % 2]
                        m = sPE.reserve(1)

                        def _(te, jc=jc, half=half, bk=bk, gi=gi,
                              ptr=prev_trev[0]):
                            if gi < 2 and ptr:
                                te.wait_ge(sS.h, ptr)
                            if gi >= 2:
                                te.wait_ge(sS.h, tr_ev[gi - 2])
                            for k in range(4):
                                oc = half * 4 + k
                                te.wait_ge(sV.h, phk_marks[oc])
                                inst = te.transpose(
                                    bk[:, :].bitcast(BF16)[:, k * 128:(k + 1) * 128],
                                    phk_f[:, oc * 512 + jc * 128: oc * 512 + jc * 128 + 128],
                                    ident_b[:, :])
                            sPE.attach(inst)
                        DT.add(_)
                        tr_m.append(m)

                        ms = sS.reserve(1)

                        def _(sc, jc=jc, half=half, bk=bk, gi=gi):
                            sc.wait_ge(sPE.h, tr_m[gi])
                            sS.attach(sc.copy(
                                phk_t[:, jc * 1024 + half * 512: jc * 1024 + half * 512 + 512],
                                bk[:, :].bitcast(BF16)[:, 0:512]))
                        DS.add(_)
                        tr_ev.append(ms)
                        gi += 1
                trdone = sS.n

                if 's' not in a_parts:
                    flush(blk)
                    continue
                ds_mark = sPE.reserve(1)

                def _(te, v_done=v_done, trdone=trdone):
                    te.wait_ge(sV.h, v_done)
                    te.wait_ge(sS.h, trdone)
                    for hh in range(H):
                        for jc in range(NJC):
                            inst = te.matmul(
                                bank[2][(hh % 2) * 64:(hh % 2) * 64 + 64,
                                        (hh // 2) * 64:(hh // 2) * 64 + 64],
                                phk_t[:, jc * 1024 + (hh // 2) * 128 + (hh % 2) * 64:
                                      jc * 1024 + (hh // 2) * 128 + (hh % 2) * 64 + 64],
                                V0[:, jc * 1040 + hh * 65: jc * 1040 + hh * 65 + 64],
                                start=(jc == 0), stop=(jc == NJC - 1),
                                skip_group_check=True)
                    sPE.attach(inst)
                DT.add(_)

                s_upd = sV.reserve(NFC + 1)

                def _(v, b=b, prefix=prefix, ds_mark=ds_mark,
                      base=s_upd - NFC - 1):
                    Sdst = Spre_sb if prefix else S_sb
                    zdst = zpre if prefix else z_sb
                    if not prefix:
                        v.tensor_copy(ckS[:, b * 512:(b + 1) * 512],
                                      S_sb[:, :])
                        v.tensor_copy(ckz[:, b * 8:(b + 1) * 8], z_sb[:, :])
                    for fc in range(NFC):
                        v.wait_ge(sV.h, phk_marks[fc])
                        sV.attach(v.reduce_sum(
                            rs[:, 0:1], phk_f[:, fc * 512:(fc + 1) * 512],
                            axis=AX))
                        v.wait_ge(sV.h, base + fc + 1)
                        v.tensor_add(zdst[:, fc:fc + 1], zdst[:, fc:fc + 1],
                                     rs[:, 0:1])
                    v.wait_ge(sPE.h, ds_mark)
                    sV.attach(v.tensor_add(Sdst[:, :], Sdst[:, :],
                                           bank[2][:, :]))
                DV.add(_)

                if not prefix:
                    def _(sy, b=b, lv=ln1_v, lpl=ln1_pl, pm=phk_marks[-1],
                          vd=v_done):
                        sy.wait_ge(sV.h, lv)
                        sy.wait_ge(sPL.h, lpl)
                        dSC.inc(sy.dma_start(ln1_scr.ap()[b], ln1_b[:, :]),
                                16)
                        sy.wait_ge(sV.h, pm)
                        dSC.inc(sy.dma_start(phk_scr.ap()[b], phk_f[:, :]),
                                16)
                        sy.wait_ge(sV.h, vd)
                        dSC.inc(sy.dma_start(v_scr.ap()[b], V0[:, :]), 16)
                        st_marks["ln1"] = dSC.n
                        st_marks["phk"] = dSC.n
                        st_marks["v"] = dSC.n
                    blk.sync(_)

                emit_wjobs(blk, 2 if it > 0 else 1)
                prev_supd[0] = s_upd
                prev_trev[0] = trdone
                flush(blk)
        if n_a_iters:
            blkA.__exit__(None, None, None)

        spre_mark = sV.n
        with nc.Block() as blk:
            emit_wjobs(blk, 99)

            def _(sy):
                sy.wait_ge(dWS.h, dWS.n)
                sy.wait_ge(dW.h, dW.n)
                src = xT.ap().rearrange("(c p) t -> p c t", p=128)
                dst = x0[:, :].rearrange("p (c t) -> p c t", c=NFC)
                dX.inc(sy.dma_start(dst, src[:, :, 0:P]), 16)
                sy.wait_ge(dSC.h, dSC.n)
                dSL.inc(sy.dma_start(ln1_b[:, :], ln1_scr.ap()[0]), 16)
                dSL.inc(sy.dma_start(phk_f[:, :], phk_scr.ap()[0]), 16)
                dSL.inc(sy.dma_start(V0[:, :], v_scr.ap()[0]), 16)
            blk.sync(_)
            flush(blk)
        wcast_all = sPL.n

        stAk.close()
        stAq = ExitStack()
        x1 = sb(stAq, "x1", [128, 4096], F32)
        phq = sb(stAq, "phq", [128, 4096], BF16)
        A_sb0 = sb(stAq, "A_sb0", [128, 2048], BF16)
        A_sb1 = sb(stAq, "A_sb1", [128, 2048], BF16)
        u_sb = sb(stAq, "u_sb", [128, 4096], BF16)
        rden_c = sb(stAq, "rden_c", [1, 512], BF16)
        yring2 = sb(stAq, "yring2", [128, 1024], F32)
        ao = sb(stAq, "ao", [128, 4096], F32)
        rdbc = sb(stAq, "rdbc", [64, 512], F32)
        S65 = sb(stAq, "S65", [128, 520], BF16)

        aq_x = {0: dX, 1: dX2}
        aq_prev_pairs = [0, 0]
        aq_x_cnt = {0: dX.n, 1: dX2.n}
        yst_marks = []
        prev_u15 = [0]
        prev_u14 = [0]
        prev_rdbc15 = [0]
        prev_rd15 = [0]
        prev_num15 = [0]
        prev_wo = [0]
        prev_ypl = [0]
        abank_hist = {2: [], 3: []}
        if stop_after in ("W0", "A") or (
                isinstance(stop_after, str) and stop_after.startswith("A")
                and len(stop_after) > 1):
            n_q_iters = 0
        else:
            n_q_iters = NB
        if n_q_iters:
            blkQ = nc.Block()
            blkQ.__enter__()
        for b in range(n_q_iters):
            if True:
                blk = blkQ
                xbuf = x0 if b % 2 == 0 else x1
                xw_sem = aq_x[b % 2]
                xw_n = aq_x_cnt[b % 2]

                sl_base = 48 * b + 48

                if b + 1 < NB:
                    nsem = aq_x[(b + 1) % 2]
                    nsem.n += 16
                    aq_x_cnt[(b + 1) % 2] = nsem.n

                    def _(sy, b=b, nsem=nsem, gv=sV.n, gp=sPL.n):
                        if gv:
                            sy.wait_ge(sV.h, gv)
                        if gp:
                            sy.wait_ge(sPL.h, gp)
                        src = xT.ap().rearrange("(c p) t -> p c t", p=128)
                        dstb = (x1 if (b + 1) % 2 else x0)
                        dst = dstb[:, :].rearrange("p (c t) -> p c t", c=NFC)
                        nsem.attach(sy.dma_start(
                            dst, src[:, :, (b + 1) * P:(b + 2) * P]), 16)
                    blk.sync(_)

                s65_mark = sV.reserve(3)

                def _(v, b=b, base=s65_mark - 3, pn=prev_num15[0]):
                    v.wait_ge(sV.h, spre_mark)
                    if pn:
                        v.wait_ge(sPE.h, pn)
                    s65r = S65[:, :].rearrange("p (h c) -> p h c", h=8)
                    sV.attach(v.tensor_add(e1s[:, 0:512],
                                           ckS[:, b * 512:(b + 1) * 512],
                                           Spre_sb[:, :]))
                    v.wait_ge(sV.h, base + 1)
                    v.tensor_copy(s65r[:, :, 0:64],
                                  e1s[:, 0:512].rearrange("p (h d) -> p h d",
                                                          h=8))
                    sV.attach(v.tensor_add(rs[:, 0:8],
                                           ckz[:, b * 8:(b + 1) * 8],
                                           zpre[:, :]))
                    v.wait_ge(sV.h, base + 2)
                    sV.attach(v.tensor_copy(
                        s65r[:, :, 64:65],
                        rs[:, 0:8].rearrange("p (h o) -> p h o", o=1)))
                DV.add(_)

                phq_marks, _qm, phq_es = emit_elu_proj(
                    wq_b, phq, ln1_b, [(dSL, sl_base), (sPL, sPL.n)],
                    [(sV, s65_mark)] + proj01_free)
                proj01_free[:] = [(sS, phq_es[-1]), (sV, phq_marks[-1])]
                qproj_pe = sPE.n

                if b + 1 < NB:
                    def _(sy, qp=qproj_pe, b=b):
                        sy.wait_ge(sPE.h, qp)
                        dSL.inc(sy.dma_start(ln1_b[:, :],
                                             ln1_scr.ap()[b + 1]), 16)
                    blk.sync(_)

                b67g = lnst["b67_free_v"]
                amm = {}
                aev_v = {}
                aev_s = {}
                num_mark = {}
                mda_mark = {}
                bc_mark = {}
                rd_mark = {}
                rdbc_ev = {}
                u_mark = {}

                def emit_mda(g):
                    m = sV.reserve(1)

                    def _(v, g=g, pr=prev_rd15[0]):
                        nbk = bank[4] if g % 2 == 0 else bank[6]
                        v.wait_ge(sPE.h, num_mark[g])
                        if g >= 1:
                            v.wait_ge(sS.h, rd_mark[g - 1])
                        elif pr:
                            v.wait_ge(sS.h, pr)
                        sV.attach(v.tensor_scalar_add(
                            r0[:, :], nbk[64:65, 0:512], ATT_EPS))
                    DV.add(_)
                    mda_mark[g] = m

                def emit_den_s(g):
                    md = sS.reserve(2)

                    def _(sc, g=g, base=md - 2):
                        sc.wait_ge(sV.h, mda_mark[g])
                        sS.attach(sc.activation(r1[:, :], r0[:, :], AF.Ln))
                        sc.wait_ge(sS.h, base + 1)
                        if g >= 2:
                            sc.wait_ge(sPE.h, bc_mark[g - 2])
                        sS.attach(sc.activation(
                            (rden_b if g % 2 == 0 else rden_c)[:, :],
                            r1[:, :], AF.Exp, scale=-1.0))
                    DS.add(_)
                    rd_mark[g] = md

                def emit_stage2a(g):
                    mb = sPE.reserve(1)

                    def _(te, g=g, prb=prev_rdbc15[0]):
                        te.wait_ge(sS.h, rd_mark[g])
                        if g >= 1:
                            te.wait_ge(sS.h, rdbc_ev[g - 1])
                        elif prb:
                            te.wait_ge(sS.h, prb)
                        rdn = rden_b if g % 2 == 0 else rden_c
                        sPE.attach(te.matmul(
                            bank[5][0:64, :], ones_rb[0:1, 0:64],
                            rdn[0:1, :],
                            start=True, stop=True, skip_group_check=True))
                    DT.add(_)
                    bc_mark[g] = mb

                    ms = sS.reserve(1)

                    def _(sc, g=g, mb=mb):
                        sc.wait_ge(sPE.h, mb)
                        if g >= 1:
                            sc.wait_ge(sV.h, u_mark[g - 1])
                        sS.attach(sc.copy(rdbc[:, :], bank[5][0:64, :]))
                    DS.add(_)
                    rdbc_ev[g] = ms

                def emit_stage2b(g):
                    mu = sV.reserve(1)

                    def _(v, g=g, ms=rdbc_ev[g]):
                        nbk = bank[4] if g % 2 == 0 else bank[6]
                        v.wait_ge(sS.h, ms)
                        sV.attach(v.tensor_mul(
                            u_sb[(g % 2) * 64:(g % 2) * 64 + 64,
                                 (g // 2) * 512:(g // 2) * 512 + 512],
                            nbk[0:64, 0:512], rdbc[:, :]))
                    DV.add(_)
                    u_mark[g] = mu

                for h in range(H):
                    if h >= 2:
                        emit_stage2a(h - 2)
                    for jc in range(NJC):
                        bki = 2 + jc % 2
                        hist = abank_hist[bki][-1] if abank_hist[bki] else None
                        m = sPE.reserve(1)

                        def _(te, h=h, jc=jc, bki=bki, hist=hist, b=b):
                            qoff = (h % 2) * 64
                            coff = (h // 2) * 512
                            if h == 0 and jc == 0:
                                te.wait_ge(dSL.h, 48 * b + 48)
                            if hist:
                                te.wait_ge(sV.h, hist[0])
                                te.wait_ge(sS.h, hist[1])
                            te.wait_ge(sV.h, phq_marks[h // 2])
                            sPE.attach(te.matmul(
                                bank[bki][:, jc * 128:512],
                                phk_f[qoff:qoff + 64, coff + jc * 128: coff + jc * 128 + 128],
                                phq[qoff:qoff + 64, coff + jc * 128: coff + 512],
                                start=True, stop=True, skip_group_check=True))
                        DT.add(_)
                        amm[(h, jc)] = m

                        if jc < NJC - 1:
                            ms = sS.reserve(1)

                            def _(sc, h=h, jc=jc, bki=bki, m=m):
                                Ab = A_sb0 if h % 2 == 0 else A_sb1
                                sc.wait_ge(sPE.h, m)
                                if h >= 2:
                                    sc.wait_ge(sPE.h, num_mark[h - 2])
                                sS.attach(sc.copy(
                                    Ab[:, jc * 512 + (jc + 1) * 128: jc * 512 + 512],
                                    bank[bki][:, (jc + 1) * 128:512]))
                            DS.add(_)
                            aev_s[(h, jc)] = ms
                        else:
                            aev_s[(h, jc)] = 0

                        mv = sV.reserve(1)

                        def _(v, h=h, jc=jc, bki=bki, m=m,
                              ms=aev_s[(h, jc)]):
                            Ab = A_sb0 if h % 2 == 0 else A_sb1
                            v.wait_ge(sPE.h, m)
                            if ms:
                                v.wait_ge(sS.h, ms)
                            if h >= 2:
                                v.wait_ge(sPE.h, num_mark[h - 2])
                            sV.attach(v.tensor_mul(
                                Ab[:, jc * 512 + jc * 128: jc * 512 + jc * 128 + 128],
                                bank[bki][:, jc * 128:jc * 128 + 128],
                                maskd[:, :]))
                        DV.add(_)
                        aev_v[(h, jc)] = mv
                        abank_hist[bki].append((aev_v[(h, jc)],
                                                aev_s[(h, jc)]))
                        if jc == 0 and h >= 2:
                            emit_stage2b(h - 2)
                        if jc == 1 and h >= 1:
                            emit_mda(h - 1)

                    mn = sPE.reserve(1)

                    def _(te, h=h, b=b, pu14=prev_u14[0],
                          pu15=prev_u15[0], b67g=b67g):
                        qoff = (h % 2) * 64
                        nbk = bank[4] if h % 2 == 0 else bank[6]
                        Ab = A_sb0 if h % 2 == 0 else A_sb1
                        if h == 0:
                            te.wait_ge(dSL.h, 48 * b + 48)
                            if pu14:
                                te.wait_ge(sV.h, pu14)
                        if h == 1:
                            if pu15:
                                te.wait_ge(sV.h, pu15)
                            if b67g:
                                te.wait_ge(sV.h, b67g)
                        if h >= 2:
                            te.wait_ge(sV.h, u_mark[h - 2])
                        te.wait_ge(sV.h, s65_mark)
                        for jc in range(NJC):
                            te.wait_ge(sV.h, aev_v[(h, jc)])
                            if jc < NJC - 1:
                                te.wait_ge(sS.h, aev_s[(h, jc)])
                            te.matmul(
                                nbk[0:65, jc * 128:512],
                                V0[:, jc * 1040 + h * 65: jc * 1040 + h * 65 + 65],
                                Ab[:, jc * 512 + jc * 128: jc * 512 + 512],
                                start=(jc == 0), stop=False,
                                skip_group_check=True)
                        sPE.attach(te.matmul(
                            nbk[0:65, 0:512],
                            S65[qoff:qoff + 64, (h // 2) * 65:(h // 2) * 65 + 65],
                            phq[qoff:qoff + 64, (h // 2) * 512:(h // 2) * 512 + 512],
                            start=False, stop=True, skip_group_check=True))
                    DT.add(_)
                    num_mark[h] = mn
                    if h >= 1:
                        emit_den_s(h - 1)

                emit_mda(H - 1)
                emit_den_s(H - 1)
                emit_stage2a(H - 2)
                emit_stage2b(H - 2)
                emit_stage2a(H - 1)
                emit_stage2b(H - 1)
                u_last = sV.n
                last_A_pe = sPE.n

                if b + 1 < NB:
                    def _(sy, b=b, lastA=last_A_pe):
                        sy.wait_ge(sPE.h, lastA)
                        dSL.inc(sy.dma_start(phk_f[:, :],
                                             phk_scr.ap()[b + 1]), 16)
                        dSL.inc(sy.dma_start(V0[:, :], v_scr.ap()[b + 1]), 16)
                    blk.sync(_)

                wo_ev = []
                for oc in range(NFC):
                    bk = bank[oc % 2]
                    m = sPE.reserve(1)

                    def _(te, oc=oc, bk=bk, u_last=u_last,
                          pyl=prev_ypl[0]):
                        if oc == 0:
                            te.wait_ge(sV.h, u_last)
                            te.wait_ge(sS.h, rdbc_ev[H - 1])
                            te.wait_ge(sPL.h, wcast_all)
                            if pyl:
                                te.wait_ge(sPL.h, pyl)
                        if oc >= 2:
                            te.wait_ge(sS.h, wo_ev[oc - 2])
                        for fc in range(NFC):
                            inst = te.matmul(
                                bk[:, :],
                                wo_b[:, fc * 1024 + oc * 128: fc * 1024 + oc * 128 + 128],
                                u_sb[:, fc * 512:(fc + 1) * 512],
                                start=(fc == 0), stop=(fc == NFC - 1),
                                skip_group_check=True)
                        sPE.attach(inst)
                    DT.add(_)

                    ms = sS.reserve(1)

                    def _(sc, oc=oc, bk=bk, m=m):
                        sc.wait_ge(sPE.h, m)
                        sS.attach(sc.activation(
                            ao[:, oc * 512:(oc + 1) * 512], bk[:, :],
                            AF.Identity, bias=bo_sb[:, oc:oc + 1]))
                    DS.add(_)
                    wo_ev.append(ms)
                proj01_free[:] = [(sS, wo_ev[-1])]
                ao_ready = [(sS, sS.n)]

                if dbg:
                    def _(sy, b=b, am=sS.n, um=u_last):
                        sy.wait_ge(sS.h, am)
                        dW.inc(sy.dma_start(dbg_ao.ap()[b], ao[:, :]), 16)
                        sy.wait_ge(sV.h, um)
                        dW.inc(sy.dma_start(dbg_u.ap()[b], u_sb[:, :]), 16)
                        sy.wait_ge(dW.h, dW.n)
                    blk.sync(_)

                lnao_mark = emit_ln_stats(ao, False, ao_ready)
                loc_pairs = []
                for fc in range(NFC):
                    p = fc // 2
                    rguard = (loc_pairs[p - 2] if p >= 2 else
                              aq_prev_pairs[p])
                    ring = yring if p % 2 == 0 else yring2
                    if fc % 2 == 0:
                        mleft = sV.reserve(1)

                        def _(v, fc=fc, rg=rguard, lnm=lnao_mark, xw=xw_n,
                              xbuf=xbuf, xs=xw_sem, ring=ring):
                            if fc == 0:
                                v.wait_ge(sV.h, lnm)
                                v.wait_ge(xs.h, xw)
                            if rg:
                                v.wait_ge(dY.h, rg)
                            v.tensor_mul(e1s[:, 0:512],
                                         ao[:, fc * 512:(fc + 1) * 512],
                                         rbc[:, :])
                            v.tensor_add(e1s[:, 512:1024], e1s[:, 0:512],
                                         cbc[:, :])
                            sV.attach(v.tensor_add(
                                ring[:, 0:512], e1s[:, 512:1024],
                                xbuf[:, fc * 512:(fc + 1) * 512]))
                        DV.add(_)
                    else:
                        mright = sPL.reserve(1)

                        def _(g, fc=fc, rg=rguard, lnm=lnao_mark, xw=xw_n,
                              xbuf=xbuf, xs=xw_sem, ring=ring):
                            if fc == 1:
                                g.wait_ge(sV.h, lnm)
                                g.wait_ge(xs.h, xw)
                            if rg:
                                g.wait_ge(dY.h, rg)
                            g.tensor_mul(rs[:, 0:512],
                                         ao[:, fc * 512:(fc + 1) * 512],
                                         rbc[:, :])
                            g.tensor_add(rs[:, 512:1024], rs[:, 0:512],
                                         cbc[:, :])
                            sPL.attach(g.tensor_add(
                                ring[:, 512:1024], rs[:, 512:1024],
                                xbuf[:, fc * 512:(fc + 1) * 512]))
                        DP.add(_)

                        def _(sy, fc=fc, b=b, ml=mleft, mr=mright,
                              ring=ring):
                            sy.wait_ge(sV.h, ml)
                            sy.wait_ge(sPL.h, mr)
                            dst = y_scr.ap()[b].rearrange(
                                "p (c t) -> p c t", c=NFC)[:, fc - 1:fc + 1, :]
                            src = ring[:, :].rearrange("p (c t) -> p c t",
                                                       c=2)
                            dY.inc(sy.dma_start(dst, src), 16)
                        blk.sync(_)
                        loc_pairs.append(dY.n)
                aq_prev_pairs[:] = [loc_pairs[2], loc_pairs[3]]
                yst_marks.append(dY.n)
                prev_u15[0] = u_mark[H - 1]
                prev_u14[0] = u_mark[H - 2]
                prev_rdbc15[0] = rdbc_ev[H - 1]
                prev_rd15[0] = rd_mark[H - 1]
                prev_num15[0] = num_mark[H - 1]
                prev_ypl[0] = sPL.n
                flush(blk)
        if n_q_iters:
            blkQ.__exit__(None, None, None)

        stAq.close()
        stA.close()
        stB = ExitStack()
        win_b = sb(stB, "win_b", [128, 65536], BF16)
        y0 = sb(stB, "y0", [128, 4096], F32)
        ln2_b = sb(stB, "ln2_b", [128, 4096], BF16)
        hid0 = sb(stB, "hid0", [128, 2048], BF16)
        hid1 = sb(stB, "hid1", [128, 2048], BF16)
        G_ch = sb(stB, "G_ch", [128, 512], BF16)
        ffn_b = sb(stB, "ffn_b", [128, 4096], BF16)
        z_fp = sb(stB, "z_fp", [128, 512], F32)
        lgtm = sb(stB, "lgtm", [128, 64], F32)
        gtm_b = sb(stB, "gtm_b", [128, 64], BF16)
        gw0 = sb(stB, "gw0", [128, 16], F32)
        gw1 = sb(stB, "gw1", [128, 16], F32)
        gw2 = sb(stB, "gw2", [128, 16], F32)
        gc0 = sb(stB, "gc0", [128, 1], F32)
        gc1 = sb(stB, "gc1", [128, 1], F32)
        gfm_t = sb(stB, "gfm", [16, 512], BF16)
        gfm = gfm_t[:, :]
        lg0 = z_fp[0:16, :]
        lg1 = z_fp[32:48, :]
        bsc0 = hid0[:, :].bitcast(F32)
        bsc1 = hid1[:, :].bitcast(F32)

        with nc.Block() as blk:
            def _(sy):
                sy.wait_ge(dY.h, dY.n)
                dYL.inc(sy.dma_start(y0[:, :], y_scr.ap()[0]), 16)
                for hh in range(16):
                    dWB.inc(sy.dma_start(
                        win_b[:, hh * 4096:(hh + 1) * 4096],
                        wbf_scr.ap()[hh]), 16)
            blk.sync(_)
        yl_marks = [16]
        ost_marks = []
        wb_hist = []

        if stop_after in ("W0", "A", "AQ") or (
                isinstance(stop_after, str) and stop_after.startswith("A")
                and len(stop_after) > 1):
            n_b_iters = 0
        elif isinstance(stop_after, str) and stop_after.startswith("B"):
            n_b_iters = int(stop_after[1:])
        else:
            n_b_iters = NB
        prev_gel63 = [0]
        prev_ffn7 = [0]
        prev_gtev3 = [0]
        prev_zsq = [0]
        prev_ypl2 = [0]
        if n_b_iters:
            blkB = nc.Block()
            blkB.__enter__()
        for b in range(n_b_iters):
            if True:
                blk = blkB
                ylw = [(dYL, yl_marks[b])]
                ln_mark = emit_ln_stats(y0, False, ylw)

                ln2_v = sV.reserve(4)

                def _(v, lnm=ln_mark):
                    v.wait_ge(sV.h, lnm)
                    for fc in range(0, NFC, 2):
                        v.tensor_mul(bsc0[:, 0:512],
                                     y0[:, fc * 512:(fc + 1) * 512], rbc[:, :])
                        sV.attach(v.tensor_add(
                            ln2_b[:, fc * 512:(fc + 1) * 512],
                            bsc0[:, 0:512], cbc[:, :]))
                DV.add(_)

                ln2_pl = sPL.reserve(4)

                def _(g, lnm=ln_mark):
                    g.wait_ge(sV.h, lnm)
                    for fc in range(1, NFC, 2):
                        g.tensor_mul(bsc1[:, 0:512],
                                     y0[:, fc * 512:(fc + 1) * 512], rbc[:, :])
                        sPL.attach(g.tensor_add(
                            ln2_b[:, fc * 512:(fc + 1) * 512],
                            bsc1[:, 0:512], cbc[:, :]))
                DP.add(_)
                ln2_ready = [(sV, ln2_v), (sPL, ln2_pl)]

                RT = []
                rt_i = [0]
                SRT = []
                srt_i = [0]

                def rt_drain(k):
                    while k > 0 and rt_i[0] < len(RT):
                        DT.add(RT[rt_i[0]])
                        rt_i[0] += 1
                        k -= 1

                def srt_drain(k):
                    while k > 0 and srt_i[0] < len(SRT):
                        DS.add(SRT[srt_i[0]])
                        srt_i[0] += 1
                        k -= 1

                z_mm = sPR.reserve(1)

                def _(te):
                    for s, n in ln2_ready:
                        te.wait_ge(s.h, n)
                    for ec in range(NFC):
                        inst = te.matmul(
                            bank[4][:, :], wr_b[:, ec * 128:(ec + 1) * 128],
                            ln2_b[:, ec * 512:(ec + 1) * 512],
                            start=(ec == 0), stop=(ec == NFC - 1),
                            skip_group_check=True)
                    sPR.attach(inst)
                RT.append(_)

                z_ev = sV.reserve(1)

                def _(v, m=z_mm, pz=prev_zsq[0]):
                    v.wait_ge(sPR.h, m)
                    if pz:
                        v.wait_ge(sS.h, pz)
                    sV.attach(v.tensor_copy(z_fp[:, :], bank[4][:, :]))
                DV.add(_)

                zsq_m = sS.reserve(1)

                def _(sc, m=z_ev):
                    sc.wait_ge(sV.h, m)
                    sS.attach(sc.activation(sqring[:, 0:512], z_fp[:, :],
                                            AF.Square))
                DS.add(_)

                ssq_m = sPR.reserve(1)

                def _(te, m=zsq_m, pg=prev_gtev3[0]):
                    te.wait_ge(sS.h, m)
                    if pg:
                        te.wait_ge(sV.h, pg)
                    sPR.attach(te.matmul(bank[5][0:1, :], ones_cb[:, :],
                                         sqring[:, 0:512], start=True,
                                         stop=True, skip_group_check=True))
                RT.append(_)

                rn_m = sS.reserve(2)

                def _(sc, m=ssq_m, base=rn_m - 2):
                    sc.wait_ge(sPR.h, m)
                    sS.attach(sc.activation(r0[:, :], bank[5][0:1, :], AF.Ln))
                    sc.wait_ge(sS.h, base + 1)
                    sS.attach(sc.activation(r1[:, :], r0[:, :], AF.Exp,
                                            scale=-0.5))
                DS.add(_)

                rn2_m = sV.reserve(2)

                def _(v, m=rn_m, base=rn2_m - 2):
                    v.wait_ge(sS.h, m)
                    sV.attach(v.tensor_scalar_min(r3[:, :], r1[:, :], 1e12))
                    v.wait_ge(sV.h, base + 1)
                    sV.attach(v.tensor_scalar_mul(r2[:, :], r3[:, :],
                                                  expT[0:1, 0:1]))
                DV.add(_)

                lg_mm = sPR.reserve(1)

                def _(te, m=z_ev):
                    te.wait_ge(sV.h, m)
                    sPR.attach(te.matmul(bank[4][0:16, :], cnT[:, :],
                                         z_fp[:, :], start=True, stop=True,
                                         skip_group_check=True))
                RT.append(_)

                lg_ev = sV.reserve(2)

                def _(v, m=lg_mm, m2=rn2_m, base=lg_ev - 2, mz=zsq_m):
                    v.wait_ge(sPR.h, m)
                    v.wait_ge(sS.h, mz)
                    sV.attach(v.tensor_copy(lg0, bank[4][0:16, :]))
                    v.wait_ge(sV.h, m2)
                    sV.attach(v.tensor_copy(rden_b[:, 0:512], r2[:, :]))
                DV.add(_)

                rnbc_m = sPR.reserve(1)

                def _(te, m=lg_ev, mr=rn_m):
                    te.wait_ge(sV.h, m)
                    te.wait_ge(sS.h, mr)
                    sPR.attach(te.matmul(bank[5][0:16, :], ones_rb[0:1, 0:16],
                                         rden_b[0:1, 0:512], start=True,
                                         stop=True, skip_group_check=True))
                RT.append(_)

                lg1_m = sV.reserve(1)

                def _(v, m=rnbc_m):
                    v.wait_ge(sPR.h, m)
                    sV.attach(v.tensor_mul(lg1, lg0, bank[5][0:16, :]))
                DV.add(_)

                lt_ev = []
                for c in range(NJC):
                    tbk = bank[3 + 2 * (c % 2)]
                    m = sPR.reserve(1)

                    def _(te, c=c, tbk=tbk, lg1_m=lg1_m):
                        te.wait_ge(sV.h, lg1_m)
                        if c >= 2:
                            te.wait_ge(sV.h, lt_ev[c - 2])
                        sPR.attach(te.transpose(tbk[:, c * 16:(c + 1) * 16],
                                                lg1[:, c * 128:(c + 1) * 128],
                                                ident_f[32:48, 32:48]))
                    RT.append(_)

                    mv = sV.reserve(1)

                    def _(v, c=c, tbk=tbk, m=m):
                        v.wait_ge(sPR.h, m)
                        sV.attach(v.tensor_copy(lgtm[:, c * 16:(c + 1) * 16],
                                                tbk[:, c * 16:(c + 1) * 16]))
                    DV.add(_)
                    lt_ev.append(mv)

                sm_exp = []
                g_done = []
                for c in range(NJC):
                    sub_m = sV.reserve(2)

                    def _(v, c=c, base=sub_m - 2):
                        v.wait_ge(sV.h, lt_ev[c])
                        sV.attach(v.reduce_max(gc0[:, :],
                                               lgtm[:, c * 16:(c + 1) * 16],
                                               axis=AX))
                        v.wait_ge(sV.h, base + 1)
                        sV.attach(v.tensor_scalar(
                            gw0[:, :], lgtm[:, c * 16:(c + 1) * 16],
                            gc0[:, :], None, OP.subtract))
                    DV.add(_)

                    me = sSS.reserve(1)

                    def _(sc, c=c, m=sub_m):
                        sc.wait_ge(sV.h, m)
                        sSS.attach(sc.activation(gw1[:, :], gw0[:, :],
                                                 AF.Exp))
                    SRT.append(_)
                    sm_exp.append(me)

                    mg = sV.reserve(9)

                    def _(v, c=c, me=me, base=mg - 9):
                        k = [base]

                        def step(inst):
                            k[0] += 1
                            sV.attach(inst)
                            v.wait_ge(sV.h, k[0])
                        v.wait_ge(sSS.h, me)
                        step(v.reduce_sum(gc1[:, :], gw1[:, :], axis=AX))
                        step(v.reciprocal(gc0[:, :], gc1[:, :]))
                        step(v.tensor_scalar(gw0[:, :], gw1[:, :], gc0[:, :],
                                             None, OP.mult))
                        step(v.reduce_max(gc1[:, :], gw0[:, :], axis=AX))
                        step(v.tensor_scalar(gw1[:, :], gw0[:, :], gc1[:, :],
                                             None, OP.is_equal))
                        step(v.scalar_tensor_tensor(gw2[:, :], gw1[:, :],
                                                    -1e30, gw0[:, :],
                                                    OP.mult, OP.add))
                        step(v.reduce_max(gc1[:, :], gw2[:, :], axis=AX))
                        step(v.tensor_scalar(gw2[:, :], gw0[:, :], gc1[:, :],
                                             None, OP.is_ge))
                        sV.attach(v.tensor_mul(gtm_b[:, c * 16:(c + 1) * 16],
                                               gw0[:, :], gw2[:, :]))
                    DV.add(_)
                    g_done.append(mg)

                gt_ev = []
                for c in range(NJC):
                    tbk = bank[3 + 2 * (c % 2)]
                    m = sPR.reserve(1)

                    def _(te, c=c, tbk=tbk):
                        te.wait_ge(sV.h, g_done[c])
                        if c >= 2:
                            te.wait_ge(sV.h, gt_ev[c - 2])
                        sPR.attach(te.transpose(
                            tbk[:, :].bitcast(BF16)[0:16, c * 128:(c + 1) * 128],
                            gtm_b[:, c * 16:(c + 1) * 16], ident_b[:, :]))
                    RT.append(_)

                    mv = sV.reserve(1)

                    def _(v, c=c, tbk=tbk, m=m):
                        v.wait_ge(sPR.h, m)
                        sV.attach(v.tensor_copy(
                            gfm[:, c * 128:(c + 1) * 128],
                            tbk[:, :].bitcast(BF16)[0:16, c * 128:(c + 1) * 128]))
                    DV.add(_)
                    gt_ev.append(mv)
                gfm_m = sV.n

                gel_m = {}
                o_mm = {}
                G_ev = {}
                ffn_ev = {}

                for h in range(H):
                    if h % 2 == 0:
                        c = h // 2
                        gsem = sPR if c == 0 else sPE
                        m = gsem.reserve(1)

                        def _(te, c=c, gfm_m=gfm_m, gsem=gsem):
                            te.wait_ge(sV.h, gfm_m)
                            if c >= 1:
                                te.wait_ge(sV.h, G_ev[c - 1])
                            gsem.attach(te.matmul(
                                bank[3][:, :],
                                et_b[0:16, c * 128:(c + 1) * 128],
                                gfm[:, :], start=True, stop=True,
                                skip_group_check=True))
                        (RT.append if c == 0 else DT.add)(_)

                        mv = sV.reserve(1)

                        def _(v, c=c, m=m, gsem=gsem):
                            v.wait_ge(gsem.h, m)
                            sV.attach(v.tensor_copy(G_ch[:, :],
                                                    bank[3][:, :]))
                        DV.add(_)
                        G_ev[c] = mv

                    for hc in range(4):
                        gi = h * 4 + hc
                        m = sPE.reserve(1)

                        def _(te, h=h, hc=hc, gi=gi, pg=prev_gel63[0]):
                            bk = bank[gi % 2]
                            if h == 0 and hc == 0:
                                for s, n in ln2_ready:
                                    te.wait_ge(s.h, n)
                            if gi < 2 and pg:
                                te.wait_ge(sS.h, pg)
                            if gi >= 2:
                                te.wait_ge(sS.h, gel_m[gi - 2])
                            if h == 0 and hc == 0:
                                te.wait_ge(dWB.h, 256)
                            for ec in range(NFC):
                                inst = te.matmul(
                                    bk[:, :],
                                    win_b[:, h * 4096 + ec * 512 + hc * 128:
                                          h * 4096 + ec * 512 + hc * 128 + 128],
                                    ln2_b[:, ec * 512:(ec + 1) * 512],
                                    start=(ec == 0), stop=(ec == NFC - 1),
                                    skip_group_check=True)
                            sPE.attach(inst)
                        if h == 0:
                            rt_drain([1, 3, 4, 0][hc])
                        elif h == 1:
                            rt_drain([2, 2, 1, 0][hc])
                        DT.add(_)

                        ms = sS.reserve(1)

                        def _(sc, h=h, hc=hc, gi=gi, m=m):
                            hbuf = hid0 if h % 2 == 0 else hid1
                            sc.wait_ge(sPE.h, m)
                            if h >= 2:
                                sc.wait_ge(sPE.h, o_mm[h - 2])
                            bk = bank[gi % 2]
                            sS.attach(sc.activation(
                                hbuf[:, hc * 512:(hc + 1) * 512], bk[:, :],
                                AF.Gelu))
                        DS.add(_)
                        gel_m[gi] = ms

                    if h == 0:
                        srt_drain(99)
                    if h == 1:
                        rt_drain(99)
                    mo = sPE.reserve(1)

                    def _(te, h=h, pf=prev_ffn7[0]):
                        hbuf = hid0 if h % 2 == 0 else hid1
                        for hc in range(4):
                            te.wait_ge(sS.h, gel_m[h * 4 + hc])
                        if h < 2 and pf:
                            te.wait_ge(sV.h, pf)
                        if h >= 2:
                            te.wait_ge(sV.h, ffn_ev[h // 2 - 1])
                        for hc in range(4):
                            inst = te.matmul(
                                bank[2][(h % 2) * 64:(h % 2) * 64 + 64, :],
                                wout_b[:, h * 256 + hc * 64: h * 256 + hc * 64 + 64],
                                hbuf[:, hc * 512:(hc + 1) * 512],
                                start=(hc == 0), stop=(hc == 3),
                                skip_group_check=True)
                        sPE.attach(inst)
                    DT.add(_)
                    o_mm[h] = mo

                    if h % 2 == 1:
                        c = h // 2
                        mv = sV.reserve(1)

                        def _(v, c=c, mo=mo, pyl=prev_ypl2[0]):
                            v.wait_ge(sPE.h, mo)
                            v.wait_ge(sV.h, G_ev[c])
                            if c == 0 and pyl:
                                v.wait_ge(sPL.h, pyl)
                            sV.attach(v.tensor_mul(
                                ffn_b[:, c * 512:(c + 1) * 512],
                                bank[2][:, :], G_ch[:, :]))
                        DV.add(_)
                        ffn_ev[c] = mv
                ffn_done = sV.n

                if dbg:
                    def _(sy, b=b, fd=ffn_done, gm=gfm_m):
                        sy.wait_ge(sV.h, fd)
                        dW.inc(sy.dma_start(dbg_ffn.ap()[b], ffn_b[:, :]), 16)
                        dW.inc(sy.dma_start(dbg_hid.ap()[b], hid0[:, :]), 16)
                        sy.wait_ge(sV.h, gm)
                        dW.inc(sy.dma_start(dbg_gtm.ap()[b],
                                            lgtm[:, :]), 16)
                        sy.wait_ge(dW.h, dW.n)
                    blk.sync(_)
                ln3_mark = emit_ln_stats(ffn_b, True, [(sV, ffn_done)])
                for fc in range(NFC):
                    if fc % 2 == 0:
                        mleft = sV.reserve(1)

                        def _(v, fc=fc, lnm=ln3_mark):
                            if fc == 0:
                                v.wait_ge(sV.h, lnm)
                            v.tensor_mul(bsc0[:, 0:512],
                                         ffn_b[:, fc * 512:(fc + 1) * 512],
                                         rbc[:, :])
                            v.tensor_add(bsc0[:, 512:1024], bsc0[:, 0:512],
                                         cbc[:, :])
                            sV.attach(v.tensor_add(
                                y0[:, fc * 512:(fc + 1) * 512],
                                y0[:, fc * 512:(fc + 1) * 512],
                                bsc0[:, 512:1024]))
                        DV.add(_)
                    else:
                        mright = sPL.reserve(1)

                        def _(g, fc=fc, lnm=ln3_mark):
                            if fc == 1:
                                g.wait_ge(sV.h, lnm)
                            g.tensor_mul(bsc1[:, 0:512],
                                         ffn_b[:, fc * 512:(fc + 1) * 512],
                                         rbc[:, :])
                            g.tensor_add(bsc1[:, 512:1024], bsc1[:, 0:512],
                                         cbc[:, :])
                            sPL.attach(g.tensor_add(
                                y0[:, fc * 512:(fc + 1) * 512],
                                y0[:, fc * 512:(fc + 1) * 512],
                                bsc1[:, 512:1024]))
                        DP.add(_)

                        def _(sy, fc=fc, b=b, ml=mleft, mr=mright):
                            sy.wait_ge(sV.h, ml)
                            sy.wait_ge(sPL.h, mr)
                            dstt = outT.ap().rearrange("(c p) t -> p c t",
                                                       p=128)
                            dst = dstt[:, fc - 1:fc + 1, b * P:(b + 1) * P]
                            src = y0[:, (fc - 1) * 512:(fc + 1) * 512]
                            dO.inc(sy.dma_start(
                                dst, src.rearrange("p (c t) -> p c t", c=2)),
                                16)
                        blk.sync(_)
                ost_marks.append(dO.n)

                if dbg2:
                    def _(sy, b=b, mo=dO.n):
                        sy.wait_ge(dO.h, mo)
                        dW.inc(sy.dma_start(dbg_ffn2.ap()[b], ffn_b[:, :]),
                               16)
                        dW.inc(sy.dma_start(dbg_g2.ap()[b], gfm), 16)
                        sy.wait_ge(dW.h, dW.n)
                    blk.sync(_)
                if b + 1 < NB:
                    def _(sy, b=b, mv=sV.n, mpl=sPL.n, mo=dO.n):
                        sy.wait_ge(sV.h, mv)
                        sy.wait_ge(sPL.h, mpl)
                        sy.wait_ge(dO.h, mo)
                        dYL.inc(sy.dma_start(y0[:, :], y_scr.ap()[b + 1]),
                                16)
                    blk.sync(_)
                    yl_marks.append(dYL.n)
                prev_gel63[0] = gel_m[4 * H - 1]
                prev_ffn7[0] = ffn_ev[H // 2 - 1]
                prev_gtev3[0] = gt_ev[NJC - 1]
                prev_zsq[0] = zsq_m
                prev_ypl2[0] = sPL.n
                flush(blk)
        if n_b_iters:
            blkB.__exit__(None, None, None)

        with nc.Block() as blk:
            def _(sy):
                if dbg2 and n_b_iters:
                    dW.inc(sy.dma_start(dbg_wb.ap()[0],
                                        win_b[:, 11 * 4096:12 * 4096]), 16)
                    dW.inc(sy.dma_start(dbg_wb.ap()[1],
                                        win_b[:, 12 * 4096:13 * 4096]), 16)
                    sy.wait_ge(dW.h, dW.n)
                sy.wait_ge(dO.h, dO.n)
            blk.sync(_)

        stB.close()
    return nc


def _prep_weights(inputs):
    Wq = inputs["L0_attn_Wq"]
    Wk = inputs["L0_attn_Wk"]
    Wv = inputs["L0_attn_Wv"]
    Wo = inputs["L0_attn_Wo"]

    def wtile(W):
        return np.ascontiguousarray(
            W.T.reshape(8, 128, 1024).transpose(1, 0, 2).reshape(128, 8192)
        ).astype(np.float32)

    wr = inputs["L0_ffn_w_router"]
    wr_h = np.ascontiguousarray(
        wr.T.reshape(8, 128, 128).transpose(1, 0, 2).reshape(128, 1024)
    ).astype(np.float32)
    ex_in = inputs["L0_ffn_input_experts"]
    win_h = np.ascontiguousarray(
        ex_in.reshape(16, 8, 128, 4, 128).transpose(2, 0, 1, 3, 4)
        .reshape(128, 65536)).astype(np.float32)
    ex_out = inputs["L0_ffn_output_experts"]
    wout_h = np.ascontiguousarray(
        ex_out.reshape(16, 4, 128, 64).transpose(2, 0, 1, 3)
        .reshape(128, 4096)).astype(np.float32)
    bo = inputs["L0_attn_bo"]
    bo_h = np.ascontiguousarray(bo.reshape(8, 128).T).astype(np.float32)
    cT = np.ascontiguousarray(
        inputs["L0_ffn_head_centroids"].T).astype(np.float32)
    temp = np.asarray(inputs["L0_ffn_temperature"],
                      dtype=np.float32).reshape(1, 1)
    et = np.zeros((16, 1024), dtype=np.float32)
    for n in range(16):
        et[n, n * 64:(n + 1) * 64] = 1.0
    ident = np.eye(128, dtype=np.float32)
    maskd = np.triu(np.ones((128, 128), dtype=np.float32))
    return {
        "wq_h": wtile(Wq), "wk_h": wtile(Wk), "wv_h": wtile(Wv),
        "wo_h": wtile(Wo), "wr_hh": wr_h, "win_h": win_h, "wout_hh": wout_h,
        "bo_h": bo_h, "cT_h": cT, "temp_h": temp, "et_h": et,
        "ident_h": ident, "maskd_h": maskd,
    }


def kernel(**inputs):
    x = np.asarray(inputs["L0_attn_x"], dtype=np.float32)
    B, T, _ = x.shape
    shared = _prep_weights(inputs)

    nc = build_nc()

    in_maps = []
    xp_cache = {}
    for c in range(8):
        bi, half = c // 2, c % 2
        xs = np.ascontiguousarray(
            x[bi, half * TOK:(half + 1) * TOK, :].T).astype(np.float32)
        if bi not in xp_cache:
            xp_cache[bi] = np.ascontiguousarray(
                x[bi, 0:TOK, :].T).astype(np.float32)
        m = dict(shared)
        m["xT"] = xs
        m["xpT"] = xp_cache[bi]
        m["gm_h"] = np.full((128, 1), 1.0 if half == 1 else 0.0,
                            dtype=np.float32)
        in_maps.append(m)

    res = run_bass_kernel_spmd(nc, in_maps, core_ids=list(range(8)))

    out = np.empty((B, T, E), dtype=np.float32)
    for c in range(8):
        bi, half = c // 2, c % 2
        out[bi, half * TOK:(half + 1) * TOK, :] = res.results[c]["outT"].T
    return out


# version 21
# speedup vs baseline: 1.0294x; 1.0063x over previous
import numpy as np
from contextlib import ExitStack

import concourse.bass as bass
import concourse.mybir as mybir
from concourse.bass_utils import run_bass_kernel_spmd
from concourse.alu_op_type import AluOpType as OP

F32 = mybir.dt.float32
BF16 = mybir.dt.bfloat16
AF = mybir.ActivationFunctionType
AX = mybir.AxisListType.X

TOK = 4096
P = 512
NB = TOK // P
E = 1024
NFC = E // 128
H = 16
D = 64
NJC = P // 128
INV_E = 1.0 / E
LN_EPS = 1e-5
ATT_EPS = 1e-5
INV_SQRT_G = 1.0 / np.sqrt(128.0)


class Sem:
    def __init__(self, nc, ctx, name):
        self.h = ctx.enter_context(nc.semaphore(name))
        self.n = 0

    def reserve(self, k=1):
        self.n += k
        return self.n

    def attach(self, inst, k=1):
        inst.then_inc(self.h, k)

    def inc(self, inst, k=1):
        inst.then_inc(self.h, k)
        self.n += k
        return self.n


class Dfr:

    def __init__(self, name):
        self.name = name
        self.q = []

    def add(self, fn):
        self.q.append(fn)

    def flush(self, blk):
        if not self.q:
            return
        q = self.q
        self.q = []

        def body(eng):
            for f in q:
                f(eng)
        getattr(blk, self.name)(body)


def build_nc(stop_after=None, dbg=False, dbg2=False, a_parts='lkvts'):
    nc = bass.Bass("TRN2", target_bir_lowering=False)

    xT = nc.dram_tensor("xT", [E, TOK], F32, kind="ExternalInput")
    wq_h = nc.dram_tensor("wq_h", [128, 8192], F32, kind="ExternalInput")
    wk_h = nc.dram_tensor("wk_h", [128, 8192], F32, kind="ExternalInput")
    wv_h = nc.dram_tensor("wv_h", [128, 8192], F32, kind="ExternalInput")
    wo_h = nc.dram_tensor("wo_h", [128, 8192], F32, kind="ExternalInput")
    bo_h = nc.dram_tensor("bo_h", [128, 8], F32, kind="ExternalInput")
    wr_hh = nc.dram_tensor("wr_hh", [128, 1024], F32, kind="ExternalInput")
    cT_h = nc.dram_tensor("cT_h", [128, 16], F32, kind="ExternalInput")
    temp_h = nc.dram_tensor("temp_h", [1, 1], F32, kind="ExternalInput")
    win_h = nc.dram_tensor("win_h", [128, 65536], F32, kind="ExternalInput")
    wout_hh = nc.dram_tensor("wout_hh", [128, 4096], F32, kind="ExternalInput")
    et_h = nc.dram_tensor("et_h", [16, 1024], F32, kind="ExternalInput")
    ident_h = nc.dram_tensor("ident_h", [128, 128], F32, kind="ExternalInput")
    maskd_h = nc.dram_tensor("maskd_h", [128, 128], F32, kind="ExternalInput")
    xpT = nc.dram_tensor("xpT", [E, TOK], F32, kind="ExternalInput")
    gm_h = nc.dram_tensor("gm_h", [128, 1], F32, kind="ExternalInput")
    outT = nc.dram_tensor("outT", [E, TOK], F32, kind="ExternalOutput")

    _sk = "ExternalOutput" if dbg else "Internal"
    ln1_scr = nc.dram_tensor("ln1_scr", [NB, 128, 4096], BF16, kind=_sk)
    phk_scr = nc.dram_tensor("phk_scr", [NB, 128, 4096], BF16, kind=_sk)
    v_scr = nc.dram_tensor("v_scr", [NB, 128, 4160], BF16, kind=_sk)
    y_scr = nc.dram_tensor("y_scr", [NB, 128, 4096], F32, kind=_sk)
    wbf_scr = nc.dram_tensor("wbf_scr", [16, 128, 4096], BF16,
                             kind="ExternalOutput" if dbg2 else "Internal")
    if dbg2:
        dbg_ffn2 = nc.dram_tensor("dbg_ffn2", [NB, 128, 4096], BF16,
                                  kind="ExternalOutput")
        dbg_g2 = nc.dram_tensor("dbg_g2", [NB, 16, 512], BF16,
                                kind="ExternalOutput")
        dbg_wb = nc.dram_tensor("dbg_wb", [2, 128, 4096], BF16,
                                kind="ExternalOutput")
    if dbg:
        dbg_ao = nc.dram_tensor("dbg_ao", [NB, 128, 4096], F32,
                                kind="ExternalOutput")
        dbg_u = nc.dram_tensor("dbg_u", [NB, 128, 4096], BF16,
                               kind="ExternalOutput")
        dbg_ffn = nc.dram_tensor("dbg_ffn", [NB, 128, 4096], BF16,
                                 kind="ExternalOutput")
        dbg_gtm = nc.dram_tensor("dbg_gtm", [NB, 128, 64], F32,
                                 kind="ExternalOutput")
        dbg_hid = nc.dram_tensor("dbg_hid", [NB, 128, 2048], BF16,
                                 kind="ExternalOutput")

    ctx = ExitStack()
    with ctx:
        def sb(st, name, shape, dt):
            return st.enter_context(nc.sbuf_tensor(name, shape, dt))

        ident_f = sb(ctx, "ident_f", [128, 128], F32)
        ident_b = sb(ctx, "ident_b", [128, 128], BF16)
        ones_cf = sb(ctx, "ones_cf", [128, 1], F32)
        ones_cb = sb(ctx, "ones_cb", [128, 1], BF16)
        ones_rf = sb(ctx, "ones_rf", [1, 128], F32)
        ones_rb = sb(ctx, "ones_rb", [1, 128], BF16)
        ones_mb = sb(ctx, "ones_mb", [128, 1], BF16)
        cnT = sb(ctx, "cnT", [128, 16], F32)
        expT = sb(ctx, "expT", [1, 1], F32)
        r0 = sb(ctx, "r0", [1, 512], F32)
        r1 = sb(ctx, "r1", [1, 512], F32)
        r2 = sb(ctx, "r2", [1, 512], F32)
        r3 = sb(ctx, "r3", [1, 512], F32)
        rden_b = sb(ctx, "rden_b", [1, 512], BF16)
        rbc = sb(ctx, "rbc", [128, 512], F32)
        cbc = sb(ctx, "cbc", [128, 512], F32)
        sqring = sb(ctx, "sqring", [128, 1024], BF16)
        yring = sb(ctx, "yring", [128, 1024], F32)
        wr_b = sb(ctx, "wr_b", [128, 1024], BF16)
        et_b = sb(ctx, "et_b", [16, 1024], BF16)
        wout_b = sb(ctx, "wout_b", [128, 4096], BF16)
        castr = sb(ctx, "castr", [128, 1024], BF16)

        bank = [ctx.enter_context(nc.psum_tensor(f"bk{i}", [128, 512], F32))
                for i in range(8)]

        sPE = Sem(nc, ctx, "sPE")
        sV = Sem(nc, ctx, "sV")
        sS = Sem(nc, ctx, "sS")
        sPL = Sem(nc, ctx, "sPL")
        sPR = Sem(nc, ctx, "sPR")
        sSS = Sem(nc, ctx, "sSS")
        dX = Sem(nc, ctx, "dX")
        dX2 = Sem(nc, ctx, "dX2")
        dSL = Sem(nc, ctx, "dSL")
        dSC = Sem(nc, ctx, "dSC")
        dY = Sem(nc, ctx, "dY")
        dYL = Sem(nc, ctx, "dYL")
        dW = Sem(nc, ctx, "dW")
        dWS = Sem(nc, ctx, "dWS")
        dWB = Sem(nc, ctx, "dWB")
        dO = Sem(nc, ctx, "dO")

        DT = Dfr("tensor")
        DV = Dfr("vector")
        DS = Dfr("scalar")
        DP = Dfr("gpsimd")

        def flush(blk):
            DT.flush(blk)
            DV.flush(blk)
            DS.flush(blk)
            DP.flush(blk)

        stA = ExitStack()
        maskd = sb(stA, "maskd", [128, 128], F32)
        bo_sb = sb(stA, "bo_sb", [128, 8], F32)
        gm_sb = sb(stA, "gm_sb", [128, 1], F32)
        wq_b = sb(stA, "wq_b", [128, 8192], BF16)
        wo_b = sb(stA, "wo_b", [128, 8192], BF16)
        S_sb = sb(stA, "S_sb", [128, 512], F32)
        z_sb = sb(stA, "z_sb", [128, 8], F32)
        ckS = sb(stA, "ckS", [128, 4096], F32)
        ckz = sb(stA, "ckz", [128, 64], F32)
        Spre_sb = sb(stA, "Spre_sb", [128, 512], F32)
        zpre = sb(stA, "zpre", [128, 8], F32)
        x0 = sb(stA, "x0", [128, 4096], F32)
        ln1_b = sb(stA, "ln1_b", [128, 4096], BF16)
        phk_f = sb(stA, "phk_f", [128, 4096], BF16)
        V0 = sb(stA, "V0", [128, 4160], BF16)
        e1s = sb(stA, "e1s", [128, 1024], F32)
        rs = sb(stA, "rs", [128, 1024], F32)

        stAk = ExitStack()
        wk_b = sb(stAk, "wk_b", [128, 8192], BF16)
        wv_b = sb(stAk, "wv_b", [128, 8192], BF16)
        phk_t = sb(stAk, "phk_t", [128, 4096], BF16)
        wstg = sb(stAk, "wstg", [128, 4096], F32)
        wcast = sb(stAk, "wcast", [128, 4096], BF16)

        wjobs = []
        wjobs.append((wq_h.ap()[:, 0:4096], 4096, wq_b[:, 0:4096], None))
        wjobs.append((wq_h.ap()[:, 4096:8192], 4096, wq_b[:, 4096:8192], None))
        wjobs.append((wo_h.ap()[:, 0:4096], 4096, wo_b[:, 0:4096], None))
        wjobs.append((wo_h.ap()[:, 4096:8192], 4096, wo_b[:, 4096:8192], None))
        wjobs.append((wr_hh.ap()[:, :], 1024, wr_b[:, :], None))
        wjobs.append((wout_hh.ap()[:, :], 4096, wout_b[:, :], None))
        for hh in range(16):
            wjobs.append((win_h.ap()[:, hh * 4096:(hh + 1) * 4096], 4096,
                          None, hh))
        wstate = {"free_pl": 0, "wc_store": 0, "ji": 0}

        def emit_wjobs(blk, kmax):
            for _ in range(kmax):
                if wstate["ji"] >= len(wjobs):
                    return
                src, width, dst, scr = wjobs[wstate["ji"]]
                wstate["ji"] += 1
                fpl = wstate["free_pl"]
                wcs = wstate["wc_store"]

                def _(sy, src=src, width=width, fpl=fpl):
                    if fpl:
                        sy.wait_ge(sPL.h, fpl)
                    dW.inc(sy.dma_start(wstg[:, 0:width], src), 16)
                blk.sync(_)
                ld_mark = dW.n
                cm = sPL.reserve(1)
                if dst is not None:
                    def _(g, dst=dst, width=width, ld=ld_mark):
                        g.wait_ge(dW.h, ld)
                        sPL.attach(g.tensor_copy(dst, wstg[:, 0:width]))
                    DP.add(_)
                    wstate["free_pl"] = cm
                else:
                    def _(g, width=width, ld=ld_mark, wcs=wcs):
                        g.wait_ge(dW.h, ld)
                        if wcs:
                            g.wait_ge(dWS.h, wcs)
                        sPL.attach(g.tensor_copy(wcast[:, 0:width],
                                                 wstg[:, 0:width]))
                    DP.add(_)
                    wstate["free_pl"] = cm

                    def _(sy, scr=scr, cm=cm):
                        sy.wait_ge(sPL.h, cm)
                        dWS.inc(sy.dma_start(wbf_scr.ap()[scr], wcast[:, :]),
                                16)
                    blk.sync(_)
                    wstate["wc_store"] = dWS.n

        lnst = {"b67_free_v": 0, "stat_last": 0}

        def emit_ln_stats(src, srcbf, src_ready):
            free_v = lnst["b67_free_v"]
            stat_last = lnst["stat_last"]
            pool_guard = sPL.n
            sq_m = []
            summ_m = []
            sqm_m = []
            for fc in range(NFC):
                if srcbf:
                    m = sS.reserve(1)

                    def _(sc, fc=fc, stat_last=stat_last):
                        if fc == 0:
                            for s, n in src_ready:
                                sc.wait_ge(s.h, n)
                        if fc < 2 and stat_last:
                            sc.wait_ge(sPE.h, stat_last)
                        if fc >= 2:
                            sc.wait_ge(sPE.h, sqm_m[fc - 2])
                        sS.attach(sc.activation(
                            sqring[:, (fc % 2) * 512:(fc % 2) * 512 + 512],
                            src[:, fc * 512:(fc + 1) * 512], AF.Square))
                    DS.add(_)
                    sq_m.append(m)
                else:
                    m = sS.reserve(2)

                    def _(sc, fc=fc, stat_last=stat_last):
                        if fc == 0:
                            for s, n in src_ready:
                                sc.wait_ge(s.h, n)
                        if fc < 2 and stat_last:
                            sc.wait_ge(sPE.h, stat_last)
                        if fc >= 2:
                            sc.wait_ge(sPE.h, summ_m[fc - 2])
                            sc.wait_ge(sPE.h, sqm_m[fc - 2])
                        sS.attach(sc.copy(
                            castr[:, (fc % 2) * 512:(fc % 2) * 512 + 512],
                            src[:, fc * 512:(fc + 1) * 512]))
                        sS.attach(sc.activation(
                            sqring[:, (fc % 2) * 512:(fc % 2) * 512 + 512],
                            castr[:, (fc % 2) * 512:(fc % 2) * 512 + 512],
                            AF.Square))
                    DS.add(_)
                    sq_m.append(m)

            for fc in range(NFC):
                m1 = sPE.reserve(1)
                m2 = sPE.reserve(1)

                def _(te, fc=fc):
                    if fc == 0 and free_v:
                        te.wait_ge(sV.h, free_v)
                    te.wait_ge(sS.h, sq_m[fc])
                    if srcbf:
                        sPE.attach(te.matmul(bank[6][0:1, :], ones_mb[:, :],
                                             src[:, fc * 512:(fc + 1) * 512],
                                             start=(fc == 0),
                                             stop=(fc == NFC - 1),
                                             skip_group_check=True))
                    else:
                        sPE.attach(te.matmul(
                            bank[6][0:1, :], ones_mb[:, :],
                            castr[:, (fc % 2) * 512:(fc % 2) * 512 + 512],
                            start=(fc == 0), stop=(fc == NFC - 1),
                            skip_group_check=True))
                    sPE.attach(te.matmul(
                        bank[7][0:1, :], ones_mb[:, :],
                        sqring[:, (fc % 2) * 512:(fc % 2) * 512 + 512],
                        start=(fc == 0), stop=(fc == NFC - 1),
                        skip_group_check=True))
                DT.add(_)
                summ_m.append(m1)
                sqm_m.append(m2)
            stat_mark = sPE.n

            sqz = sS.reserve(1)

            def _(sc):
                sc.wait_ge(sPE.h, stat_mark)
                sS.attach(sc.activation(r2[:, :], bank[6][0:1, :],
                                        AF.Square))
            DS.add(_)

            fin = sV.reserve(1)

            def _(v, sqz=sqz):
                v.wait_ge(sS.h, sqz)
                sV.attach(v.scalar_tensor_tensor(r3[:, :], bank[7][0:1, :],
                                                 LN_EPS, r2[:, :],
                                                 OP.add, OP.subtract))
            DV.add(_)

            rstd = sS.reserve(3)

            def _(sc, fin=fin, base=rstd - 3):
                sc.wait_ge(sV.h, fin)
                sS.attach(sc.activation(r1[:, :], r3[:, :], AF.Ln))
                sc.wait_ge(sS.h, base + 1)
                sS.attach(sc.activation(r2[:, :], r1[:, :], AF.Exp,
                                        scale=-0.5))
                sc.wait_ge(sS.h, base + 2)
                sS.attach(sc.copy(sqring[0:1, 0:512], r2[:, :]))
            DS.add(_)

            cneg = sV.reserve(1)

            def _(v, rstd=rstd):
                v.wait_ge(sS.h, rstd - 1)
                sV.attach(v.scalar_tensor_tensor(
                    sqring[0:1, 512:1024], bank[6][0:1, :], -1.0,
                    r2[:, :], OP.mult, OP.mult))
            DV.add(_)

            bc = sPE.reserve(1)

            def _(te, cneg=cneg, rstd=rstd):
                te.wait_ge(sV.h, cneg)
                te.wait_ge(sS.h, rstd)
                te.matmul(bank[6][:, :], ones_rb[0:1, :],
                          sqring[0:1, 0:512], start=True, stop=True,
                          skip_group_check=True)
                sPE.attach(te.matmul(bank[7][:, :], ones_rb[0:1, :],
                                     sqring[0:1, 512:1024], start=True,
                                     stop=True, skip_group_check=True))
            DT.add(_)

            bcev = sV.reserve(1)

            def _(v, bc=bc, pg=pool_guard):
                v.wait_ge(sPE.h, bc)
                if pg:
                    v.wait_ge(sPL.h, pg)
                v.tensor_copy(rbc[:, :], bank[6][:, :])
                sV.attach(v.tensor_copy(cbc[:, :], bank[7][:, :]))
            DV.add(_)
            lnst["b67_free_v"] = bcev
            lnst["stat_last"] = bc
            return bcev

        def emit_elu_proj(w_sb, dst_bf, rhs_bf, rhs_ready, prev_free,
                          mask=None, parts='mse'):
            evac_s = []
            evac_v = []
            mmm = []
            for oc in range(NFC):
                bk = bank[oc % 2]
                s_ = (oc % 2) * 512
                o_ = 512 - s_
                m = sPE.reserve(1)

                def _(te, oc=oc, bk=bk):
                    if oc == 0:
                        for s, n in rhs_ready:
                            te.wait_ge(s.h, n)
                    if oc < 2:
                        for s, n in prev_free:
                            te.wait_ge(s.h, n)
                    if oc >= 2 and 's' in parts:
                        te.wait_ge(sS.h, evac_s[oc - 2])
                    if oc >= 2 and 'e' in parts:
                        te.wait_ge(sV.h, evac_v[oc - 2])
                    for ec in range(NFC):
                        inst = te.matmul(
                            bk[:, :],
                            w_sb[:, ec * 1024 + oc * 128: ec * 1024 + oc * 128 + 128],
                            rhs_bf[:, ec * 512:(ec + 1) * 512],
                            start=(ec == 0), stop=(ec == NFC - 1),
                            skip_group_check=True)
                    sPE.attach(inst)
                DT.add(_)
                mmm.append(m)

                if 's' in parts:
                    ms = sS.reserve(1)

                    def _(sc, oc=oc, bk=bk, s_=s_, m=m):
                        sc.wait_ge(sPE.h, m)
                        if 'e' in parts:
                            if mask is None:
                                if oc >= 2:
                                    sc.wait_ge(sV.h, evac_v[oc - 2])
                            else:
                                if oc >= 1:
                                    sc.wait_ge(sV.h, evac_v[oc - 1])
                        sS.attach(sc.activation(e1s[:, s_:s_ + 512], bk[:, :],
                                                AF.Exp))
                    DS.add(_)
                    evac_s.append(ms)
                else:
                    evac_s.append(0)

                if 'e' in parts:
                    mv = sV.reserve(1)

                    def _(v, oc=oc, bk=bk, s_=s_, o_=o_, m=m, ms=evac_s[-1]):
                        v.wait_ge(sPE.h, m)
                        if ms:
                            v.wait_ge(sS.h, ms)
                        if mask is None:
                            v.tensor_scalar_max(rs[:, s_:s_ + 512], bk[:, :],
                                                0.0)
                            sV.attach(v.scalar_tensor_tensor(
                                dst_bf[:, oc * 512:(oc + 1) * 512],
                                e1s[:, s_:s_ + 512], 1.0, rs[:, s_:s_ + 512],
                                OP.min, OP.add))
                        else:
                            v.tensor_scalar(rs[:, s_:s_ + 512], bk[:, :], 0.0,
                                            mask, OP.max, OP.mult)
                            v.tensor_scalar(rs[:, o_:o_ + 512],
                                            e1s[:, s_:s_ + 512], mask, mask,
                                            OP.mult, OP.min)
                            sV.attach(v.tensor_add(
                                dst_bf[:, oc * 512:(oc + 1) * 512],
                                rs[:, o_:o_ + 512], rs[:, s_:s_ + 512]))
                    DV.add(_)
                    evac_v.append(mv)
                else:
                    evac_v.append(0)
            return evac_v, mmm, evac_s

        with nc.Block() as blk:
            def _(sy):
                dW.inc(sy.dma_start(ident_f[:, :], ident_h.ap()[:, :]), 16)
                dW.inc(sy.dma_start(maskd[:, :], maskd_h.ap()[:, :]), 16)
                dW.inc(sy.dma_start(bo_sb[:, :], bo_h.ap()[:, :]), 16)
                dW.inc(sy.dma_start(gm_sb[:, :], gm_h.ap()[:, :]), 16)
                dW.inc(sy.dma_start(r0[0:1, 0:1], temp_h.ap()[:, :]), 16)
                dW.inc(sy.dma_start(e1s[:, 0:16], cT_h.ap()[:, :]), 16)
                dW.inc(sy.dma_start(wstg[0:16, 0:1024], et_h.ap()[:, :]), 16)
                src = xpT.ap().rearrange("(c p) t -> p c t", p=128)
                dst = x0[:, :].rearrange("p (c t) -> p c t", c=NFC)
                dX.inc(sy.dma_start(dst, src[:, :, 0:P]), 16)
            blk.sync(_)
            w0_ld = dW.n

            def _(v):
                v.memset(ones_cf[:, :], 1.0)
                v.memset(ones_cb[:, :], 1.0)
                v.memset(ones_rf[:, :], 1.0)
                v.memset(ones_rb[:, :], 1.0)
                v.memset(ones_mb[:, :], 1.0 / 1024.0)
                v.memset(S_sb[:, :], 0.0)
                v.memset(z_sb[:, :], 0.0)
                v.memset(Spre_sb[:, :], 0.0)
                v.memset(zpre[:, :], 0.0)
                v0r = V0[:, :].rearrange("p (j h c) -> p j h c", j=4, h=16)
                sV.inc(v.memset(v0r[:, :, :, 64:65], 1.0))
            blk.vector(_)

            def _(g):
                g.wait_ge(dW.h, w0_ld)
                g.tensor_copy(ident_b[:, :], ident_f[:, :])
                sPL.inc(g.tensor_copy(et_b[:, :], wstg[0:16, 0:1024]))
            blk.gpsimd(_)
            wstate["free_pl"] = sPL.n

            def _(sc):
                sc.wait_ge(dW.h, w0_ld)
                sS.inc(sc.activation(r1[0:1, 0:1], r0[0:1, 0:1], AF.Exp))
                sS.inc(sc.activation(rs[:, 0:16], e1s[:, 0:16], AF.Square))
            blk.scalar(_)
            m_expT = sS.n

            def _(v):
                v.wait_ge(sS.h, m_expT)
                sV.inc(v.tensor_scalar_mul(expT[:, :], r1[0:1, 0:1],
                                           INV_SQRT_G))
            blk.vector(_)

            def _(te):
                te.wait_ge(sS.h, m_expT)
                sPE.inc(te.matmul(bank[6][0:1, 0:16], ones_cf[:, :],
                                  rs[:, 0:16], start=True, stop=True,
                                  skip_group_check=True))
            blk.tensor(_)
            m_cssq = sPE.n

            def _(sc):
                sc.wait_ge(sPE.h, m_cssq)
                sS.inc(sc.activation(r1[0:1, 0:16], bank[6][0:1, 0:16],
                                     AF.Ln))
                sc.wait_ge(sS.h, sS.n)
                sS.inc(sc.activation(r2[0:1, 0:16], r1[0:1, 0:16], AF.Exp,
                                     scale=-0.5))
            blk.scalar(_)
            m_crn = sS.n

            def _(v):
                v.wait_ge(sS.h, m_crn)
                sV.inc(v.tensor_scalar_min(r3[0:1, 0:16], r2[0:1, 0:16],
                                           1e12))
            blk.vector(_)
            m_crn2 = sV.n

            def _(te):
                te.wait_ge(sV.h, m_crn2)
                sPE.inc(te.matmul(bank[6][:, 0:16], ones_rf[0:1, :],
                                  r3[0:1, 0:16], start=True, stop=True,
                                  skip_group_check=True))
            blk.tensor(_)
            m_cbc = sPE.n

            def _(v):
                v.wait_ge(sPE.h, m_cbc)
                sV.inc(v.tensor_copy(rs[:, 16:32], bank[6][:, 0:16]))
                v.wait_ge(sV.h, sV.n)
                sV.inc(v.tensor_mul(cnT[:, :], e1s[:, 0:16], rs[:, 16:32]))
            blk.vector(_)
            lnst["b67_free_v"] = sV.n

            wkv_v = 0
            for w_dst, w_src in ((wk_b, wk_h), (wv_b, wv_h)):
                for half in range(2):
                    fpl = wstate["free_pl"]
                    fv = wkv_v

                    def _(sy, w_src=w_src, half=half, fpl=fpl, fv=fv):
                        sy.wait_ge(sPL.h, fpl)
                        if fv:
                            sy.wait_ge(sV.h, fv)
                        dW.inc(sy.dma_start(
                            wstg[:, :],
                            w_src.ap()[:, half * 4096:half * 4096 + 4096]), 16)
                    blk.sync(_)
                    ldm = dW.n

                    def _(g, w_dst=w_dst, half=half, ldm=ldm):
                        g.wait_ge(dW.h, ldm)
                        sPL.inc(g.tensor_copy(
                            w_dst[:, half * 4096:half * 4096 + 2048],
                            wstg[:, 0:2048]))
                    blk.gpsimd(_)
                    wstate["free_pl"] = sPL.n

                    def _(v, w_dst=w_dst, half=half, ldm=ldm):
                        v.wait_ge(dW.h, ldm)
                        sV.inc(v.tensor_copy(
                            w_dst[:, half * 4096 + 2048:half * 4096 + 4096],
                            wstg[:, 2048:4096]))
                    blk.vector(_)
                    wkv_v = sV.n
        wk_ready = sPL.n
        wk_ready_v = wkv_v

        x_marks = {"pe": 0, "s": 0, "v": 0, "pl": 0}
        st_marks = {"ln1": 0, "phk": 0, "v": 0}
        hoist_ln = [None]
        proj01_free = []
        prev_supd = [0]
        prev_trev = [0]
        if stop_after == "W0":
            n_a_iters = 0
        elif isinstance(stop_after, str) and stop_after.startswith("A") and \
                len(stop_after) > 1 and stop_after[1:].isdigit():
            n_a_iters = int(stop_after[1:])
        else:
            n_a_iters = 2 * NB
        if n_a_iters:
            blkA = nc.Block()
            blkA.__enter__()
        for it in range(n_a_iters):
            prefix = it < NB
            b = it % NB
            if True:
                blk = blkA
                xr = [(dX, 16 * (it + 1))]
                ln_mark = emit_ln_stats(x0, False, xr)

                stg = dict(st_marks)
                ln1_v = sV.reserve(4)

                def _(v, stg=stg, ln_mark=ln_mark):
                    v.wait_ge(sV.h, ln_mark)
                    if stg["ln1"]:
                        v.wait_ge(dSC.h, stg["ln1"])
                    for fc in range(0, NFC, 2):
                        v.tensor_mul(e1s[:, 0:512],
                                     x0[:, fc * 512:(fc + 1) * 512], rbc[:, :])
                        sV.attach(v.tensor_add(
                            ln1_b[:, fc * 512:(fc + 1) * 512],
                            e1s[:, 0:512], cbc[:, :]))
                DV.add(_)

                ln1_pl = sPL.reserve(4)

                def _(g, stg=stg, ln_mark=ln_mark):
                    g.wait_ge(sV.h, ln_mark)
                    if stg["ln1"]:
                        g.wait_ge(dSC.h, stg["ln1"])
                    for fc in range(1, NFC, 2):
                        g.tensor_mul(rs[:, 0:512],
                                     x0[:, fc * 512:(fc + 1) * 512], rbc[:, :])
                        sPL.attach(g.tensor_add(
                            ln1_b[:, fc * 512:(fc + 1) * 512],
                            rs[:, 0:512], cbc[:, :]))
                DP.add(_)
                ln1_ready = [(sV, ln1_v), (sPL, ln1_pl)]
                x_marks = {"pe": sPE.n, "s": sS.n, "v": ln1_v, "pl": ln1_pl}

                if it + 1 < 2 * NB:
                    def _(sy, it=it, xm=dict(x_marks)):
                        sy.wait_ge(sPE.h, xm["pe"])
                        sy.wait_ge(sS.h, xm["s"])
                        sy.wait_ge(sV.h, xm["v"])
                        sy.wait_ge(sPL.h, xm["pl"])
                        srct = xpT if (it + 1) < NB else xT
                        nb = (it + 1) % NB
                        src = srct.ap().rearrange("(c p) t -> p c t", p=128)
                        dst = x0[:, :].rearrange("p (c t) -> p c t", c=NFC)
                        dX.inc(sy.dma_start(dst,
                                            src[:, :, nb * P:(nb + 1) * P]),
                               16)
                    blk.sync(_)

                if 'k' not in a_parts:
                    flush(blk)
                    continue
                prev_free = [(sPL, wk_ready), (sV, wk_ready_v)] + \
                    list(proj01_free)
                if st_marks["phk"]:
                    prev_free = prev_free + [(dSC, st_marks["phk"])]
                kp = 'm'
                if '1' not in a_parts:
                    kp += 's'
                if '1' not in a_parts and '2' not in a_parts:
                    kp += 'e'
                phk_marks, _km, phk_es = emit_elu_proj(
                    wk_b, phk_f, ln1_b, ln1_ready, prev_free,
                    mask=(gm_sb[:, :] if (prefix and 'M' not in a_parts)
                          else None), parts=kp)
                if phk_es and phk_es[-1]:
                    proj01_free[:] = [(sS, phk_es[-1]),
                                      (sV, phk_marks[-1])]

                if 'v' not in a_parts:
                    flush(blk)
                    continue
                vevac = []
                vmm = []
                stv = st_marks["v"]
                gi = 0
                for jc in range(NJC):
                    for half in range(2):
                        bk = bank[2 + gi % 2]
                        m = sPE.reserve(1)

                        def _(te, jc=jc, half=half, bk=bk, gi=gi, stv=stv,
                              psu=prev_supd[0]):
                            if gi == 0:
                                for s, n in ln1_ready:
                                    te.wait_ge(s.h, n)
                                te.wait_ge(sPL.h, wk_ready)
                                te.wait_ge(sV.h, wk_ready_v)
                                if stv:
                                    te.wait_ge(dSC.h, stv)
                            if gi < 2 and psu:
                                te.wait_ge(sV.h, psu)
                            if gi >= 2:
                                te.wait_ge(sV.h, vevac[gi - 2])
                            for ec in range(NFC):
                                inst = te.matmul(
                                    bk[:, :],
                                    ln1_b[:, ec * 512 + jc * 128: ec * 512 + jc * 128 + 128],
                                    wv_b[:, ec * 1024 + half * 512: ec * 1024 + half * 512 + 512],
                                    start=(ec == 0), stop=(ec == NFC - 1),
                                    skip_group_check=True)
                            sPE.attach(inst)
                        DT.add(_)
                        vmm.append(m)

                        mv = sV.reserve(1)

                        def _(v, jc=jc, half=half, bk=bk, m=m):
                            v.wait_ge(sPE.h, m)
                            v0r = V0[:, :].rearrange("p (j h c) -> p j h c",
                                                     j=4, h=16)
                            dst = v0r[:, jc, half * 8:half * 8 + 8, 0:64]
                            sV.attach(v.tensor_copy(
                                dst, bk[:, :].rearrange("p (h d) -> p h d",
                                                        h=8)))
                        DV.add(_)
                        vevac.append(mv)
                        gi += 1
                v_done = sV.n

                if 't' not in a_parts:
                    flush(blk)
                    continue
                tr_m = []
                tr_ev = []
                gi = 0
                for jc in range(NJC):
                    for half in range(2):
                        bk = bank[4 + gi % 2]
                        m = sPE.reserve(1)

                        def _(te, jc=jc, half=half, bk=bk, gi=gi,
                              ptr=prev_trev[0]):
                            if gi < 2 and ptr:
                                te.wait_ge(sS.h, ptr)
                            if gi >= 2:
                                te.wait_ge(sS.h, tr_ev[gi - 2])
                            for k in range(4):
                                oc = half * 4 + k
                                te.wait_ge(sV.h, phk_marks[oc])
                                inst = te.transpose(
                                    bk[:, :].bitcast(BF16)[:, k * 128:(k + 1) * 128],
                                    phk_f[:, oc * 512 + jc * 128: oc * 512 + jc * 128 + 128],
                                    ident_b[:, :])
                            sPE.attach(inst)
                        DT.add(_)
                        tr_m.append(m)

                        ms = sS.reserve(1)

                        def _(sc, jc=jc, half=half, bk=bk, gi=gi):
                            sc.wait_ge(sPE.h, tr_m[gi])
                            sS.attach(sc.copy(
                                phk_t[:, jc * 1024 + half * 512: jc * 1024 + half * 512 + 512],
                                bk[:, :].bitcast(BF16)[:, 0:512]))
                        DS.add(_)
                        tr_ev.append(ms)
                        gi += 1
                trdone = sS.n

                if 's' not in a_parts:
                    flush(blk)
                    continue
                ds_mark = sPE.reserve(1)

                def _(te, v_done=v_done, trdone=trdone):
                    te.wait_ge(sV.h, v_done)
                    te.wait_ge(sS.h, trdone)
                    for hh in range(H):
                        for jc in range(NJC):
                            inst = te.matmul(
                                bank[2][(hh % 2) * 64:(hh % 2) * 64 + 64,
                                        (hh // 2) * 64:(hh // 2) * 64 + 64],
                                phk_t[:, jc * 1024 + (hh // 2) * 128 + (hh % 2) * 64:
                                      jc * 1024 + (hh // 2) * 128 + (hh % 2) * 64 + 64],
                                V0[:, jc * 1040 + hh * 65: jc * 1040 + hh * 65 + 64],
                                start=(jc == 0), stop=(jc == NJC - 1),
                                skip_group_check=True)
                    sPE.attach(inst)
                DT.add(_)

                s_upd = sV.reserve(NFC + 1)

                def _(v, b=b, prefix=prefix, ds_mark=ds_mark,
                      base=s_upd - NFC - 1):
                    Sdst = Spre_sb if prefix else S_sb
                    zdst = zpre if prefix else z_sb
                    if not prefix:
                        v.tensor_copy(ckS[:, b * 512:(b + 1) * 512],
                                      S_sb[:, :])
                        v.tensor_copy(ckz[:, b * 8:(b + 1) * 8], z_sb[:, :])
                    for fc in range(NFC):
                        v.wait_ge(sV.h, phk_marks[fc])
                        sV.attach(v.reduce_sum(
                            rs[:, 0:1], phk_f[:, fc * 512:(fc + 1) * 512],
                            axis=AX))
                        v.wait_ge(sV.h, base + fc + 1)
                        v.tensor_add(zdst[:, fc:fc + 1], zdst[:, fc:fc + 1],
                                     rs[:, 0:1])
                    v.wait_ge(sPE.h, ds_mark)
                    sV.attach(v.tensor_add(Sdst[:, :], Sdst[:, :],
                                           bank[2][:, :]))
                DV.add(_)

                if not prefix:
                    def _(sy, b=b, lv=ln1_v, lpl=ln1_pl, pm=phk_marks[-1],
                          vd=v_done):
                        sy.wait_ge(sV.h, lv)
                        sy.wait_ge(sPL.h, lpl)
                        dSC.inc(sy.dma_start(ln1_scr.ap()[b], ln1_b[:, :]),
                                16)
                        sy.wait_ge(sV.h, pm)
                        dSC.inc(sy.dma_start(phk_scr.ap()[b], phk_f[:, :]),
                                16)
                        sy.wait_ge(sV.h, vd)
                        dSC.inc(sy.dma_start(v_scr.ap()[b], V0[:, :]), 16)
                        st_marks["ln1"] = dSC.n
                        st_marks["phk"] = dSC.n
                        st_marks["v"] = dSC.n
                    blk.sync(_)

                emit_wjobs(blk, 2 if it > 0 else 1)
                prev_supd[0] = s_upd
                prev_trev[0] = trdone
                flush(blk)
        if n_a_iters:
            blkA.__exit__(None, None, None)

        spre_mark = sV.n
        with nc.Block() as blk:
            emit_wjobs(blk, 99)

            def _(sy):
                sy.wait_ge(dWS.h, dWS.n)
                sy.wait_ge(dW.h, dW.n)
                src = xT.ap().rearrange("(c p) t -> p c t", p=128)
                dst = x0[:, :].rearrange("p (c t) -> p c t", c=NFC)
                dX.inc(sy.dma_start(dst, src[:, :, 0:P]), 16)
                sy.wait_ge(dSC.h, dSC.n)
                dSL.inc(sy.dma_start(ln1_b[:, :], ln1_scr.ap()[0]), 16)
                dSL.inc(sy.dma_start(phk_f[:, :], phk_scr.ap()[0]), 16)
                dSL.inc(sy.dma_start(V0[:, :], v_scr.ap()[0]), 16)
            blk.sync(_)
            flush(blk)
        wcast_all = sPL.n

        stAk.close()
        stAq = ExitStack()
        x1 = sb(stAq, "x1", [128, 4096], F32)
        phq = sb(stAq, "phq", [128, 4096], BF16)
        A_sb0 = sb(stAq, "A_sb0", [128, 2048], BF16)
        A_sb1 = sb(stAq, "A_sb1", [128, 2048], BF16)
        u_sb = sb(stAq, "u_sb", [128, 4096], BF16)
        rden_c = sb(stAq, "rden_c", [1, 512], BF16)
        yring2 = sb(stAq, "yring2", [128, 1024], F32)
        ao = sb(stAq, "ao", [128, 4096], F32)
        rdbc = sb(stAq, "rdbc", [64, 512], F32)
        S65 = sb(stAq, "S65", [128, 520], BF16)

        aq_x = {0: dX, 1: dX2}
        aq_prev_pairs = [0, 0]
        aq_x_cnt = {0: dX.n, 1: dX2.n}
        yst_marks = []
        prev_u15 = [0]
        prev_u14 = [0]
        prev_rdbc15 = [0]
        prev_rd15 = [0]
        prev_num15 = [0]
        prev_wo = [0]
        prev_ypl = [0]
        abank_hist = {2: [], 3: []}
        if stop_after in ("W0", "A") or (
                isinstance(stop_after, str) and stop_after.startswith("A")
                and len(stop_after) > 1):
            n_q_iters = 0
        else:
            n_q_iters = NB
        if n_q_iters:
            blkQ = nc.Block()
            blkQ.__enter__()
        for b in range(n_q_iters):
            if True:
                blk = blkQ
                xbuf = x0 if b % 2 == 0 else x1
                xw_sem = aq_x[b % 2]
                xw_n = aq_x_cnt[b % 2]

                sl_base = 48 * b + 48

                if b + 1 < NB:
                    nsem = aq_x[(b + 1) % 2]
                    nsem.n += 16
                    aq_x_cnt[(b + 1) % 2] = nsem.n

                    def _(sy, b=b, nsem=nsem, gv=sV.n, gp=sPL.n):
                        if gv:
                            sy.wait_ge(sV.h, gv)
                        if gp:
                            sy.wait_ge(sPL.h, gp)
                        src = xT.ap().rearrange("(c p) t -> p c t", p=128)
                        dstb = (x1 if (b + 1) % 2 else x0)
                        dst = dstb[:, :].rearrange("p (c t) -> p c t", c=NFC)
                        nsem.attach(sy.dma_start(
                            dst, src[:, :, (b + 1) * P:(b + 2) * P]), 16)
                    blk.sync(_)

                s65_mark = sV.reserve(3)

                def _(v, b=b, base=s65_mark - 3, pn=prev_num15[0]):
                    v.wait_ge(sV.h, spre_mark)
                    if pn:
                        v.wait_ge(sPE.h, pn)
                    s65r = S65[:, :].rearrange("p (h c) -> p h c", h=8)
                    sV.attach(v.tensor_add(e1s[:, 0:512],
                                           ckS[:, b * 512:(b + 1) * 512],
                                           Spre_sb[:, :]))
                    v.wait_ge(sV.h, base + 1)
                    v.tensor_copy(s65r[:, :, 0:64],
                                  e1s[:, 0:512].rearrange("p (h d) -> p h d",
                                                          h=8))
                    sV.attach(v.tensor_add(rs[:, 0:8],
                                           ckz[:, b * 8:(b + 1) * 8],
                                           zpre[:, :]))
                    v.wait_ge(sV.h, base + 2)
                    sV.attach(v.tensor_copy(
                        s65r[:, :, 64:65],
                        rs[:, 0:8].rearrange("p (h o) -> p h o", o=1)))
                DV.add(_)

                phq_marks, _qm, phq_es = emit_elu_proj(
                    wq_b, phq, ln1_b, [(dSL, sl_base), (sPL, sPL.n)],
                    [(sV, s65_mark)] + proj01_free)
                proj01_free[:] = [(sS, phq_es[-1]), (sV, phq_marks[-1])]
                qproj_pe = sPE.n

                if b + 1 < NB:
                    def _(sy, qp=qproj_pe, b=b):
                        sy.wait_ge(sPE.h, qp)
                        dSL.inc(sy.dma_start(ln1_b[:, :],
                                             ln1_scr.ap()[b + 1]), 16)
                    blk.sync(_)

                b67g = lnst["b67_free_v"]
                amm = {}
                aev_v = {}
                aev_s = {}
                num_mark = {}
                mda_mark = {}
                bc_mark = {}
                rd_mark = {}
                rdbc_ev = {}
                u_mark = {}

                def emit_mda(g):
                    m = sV.reserve(1)

                    def _(v, g=g, pr=prev_rd15[0]):
                        nbk = bank[4] if g % 2 == 0 else bank[6]
                        v.wait_ge(sPE.h, num_mark[g])
                        if g >= 1:
                            v.wait_ge(sS.h, rd_mark[g - 1])
                        elif pr:
                            v.wait_ge(sS.h, pr)
                        sV.attach(v.tensor_scalar_add(
                            r0[:, :], nbk[64:65, 0:512], ATT_EPS))
                    DV.add(_)
                    mda_mark[g] = m

                def emit_den_s(g):
                    md = sS.reserve(2)

                    def _(sc, g=g, base=md - 2):
                        sc.wait_ge(sV.h, mda_mark[g])
                        sS.attach(sc.activation(r1[:, :], r0[:, :], AF.Ln))
                        sc.wait_ge(sS.h, base + 1)
                        if g >= 2:
                            sc.wait_ge(sPE.h, bc_mark[g - 2])
                        sS.attach(sc.activation(
                            (rden_b if g % 2 == 0 else rden_c)[:, :],
                            r1[:, :], AF.Exp, scale=-1.0))
                    DS.add(_)
                    rd_mark[g] = md

                def emit_stage2a(g):
                    mb = sPE.reserve(1)

                    def _(te, g=g, prb=prev_rdbc15[0]):
                        te.wait_ge(sS.h, rd_mark[g])
                        if g >= 1:
                            te.wait_ge(sS.h, rdbc_ev[g - 1])
                        elif prb:
                            te.wait_ge(sS.h, prb)
                        rdn = rden_b if g % 2 == 0 else rden_c
                        sPE.attach(te.matmul(
                            bank[5][0:64, :], ones_rb[0:1, 0:64],
                            rdn[0:1, :],
                            start=True, stop=True, skip_group_check=True))
                    DT.add(_)
                    bc_mark[g] = mb

                    ms = sS.reserve(1)

                    def _(sc, g=g, mb=mb):
                        sc.wait_ge(sPE.h, mb)
                        if g >= 1:
                            sc.wait_ge(sV.h, u_mark[g - 1])
                        sS.attach(sc.copy(rdbc[:, :], bank[5][0:64, :]))
                    DS.add(_)
                    rdbc_ev[g] = ms

                def emit_stage2b(g):
                    mu = sV.reserve(1)

                    def _(v, g=g, ms=rdbc_ev[g]):
                        nbk = bank[4] if g % 2 == 0 else bank[6]
                        v.wait_ge(sS.h, ms)
                        sV.attach(v.tensor_mul(
                            u_sb[(g % 2) * 64:(g % 2) * 64 + 64,
                                 (g // 2) * 512:(g // 2) * 512 + 512],
                            nbk[0:64, 0:512], rdbc[:, :]))
                    DV.add(_)
                    u_mark[g] = mu

                for h in range(H):
                    if h >= 2:
                        emit_stage2a(h - 2)
                    for jc in range(NJC):
                        bki = 2 + jc % 2
                        hist = abank_hist[bki][-1] if abank_hist[bki] else None
                        m = sPE.reserve(1)

                        def _(te, h=h, jc=jc, bki=bki, hist=hist, b=b):
                            qoff = (h % 2) * 64
                            coff = (h // 2) * 512
                            if h == 0 and jc == 0:
                                te.wait_ge(dSL.h, 48 * b + 48)
                            if hist:
                                te.wait_ge(sV.h, hist[0])
                                te.wait_ge(sS.h, hist[1])
                            te.wait_ge(sV.h, phq_marks[h // 2])
                            sPE.attach(te.matmul(
                                bank[bki][:, jc * 128:512],
                                phk_f[qoff:qoff + 64, coff + jc * 128: coff + jc * 128 + 128],
                                phq[qoff:qoff + 64, coff + jc * 128: coff + 512],
                                start=True, stop=True, skip_group_check=True))
                        DT.add(_)
                        amm[(h, jc)] = m

                        if jc < NJC - 1:
                            ms = sS.reserve(1)

                            def _(sc, h=h, jc=jc, bki=bki, m=m):
                                Ab = A_sb0 if h % 2 == 0 else A_sb1
                                sc.wait_ge(sPE.h, m)
                                if h >= 2:
                                    sc.wait_ge(sPE.h, num_mark[h - 2])
                                sS.attach(sc.copy(
                                    Ab[:, jc * 512 + (jc + 1) * 128: jc * 512 + 512],
                                    bank[bki][:, (jc + 1) * 128:512]))
                            DS.add(_)
                            aev_s[(h, jc)] = ms
                        else:
                            aev_s[(h, jc)] = 0

                        mv = sV.reserve(1)

                        def _(v, h=h, jc=jc, bki=bki, m=m,
                              ms=aev_s[(h, jc)]):
                            Ab = A_sb0 if h % 2 == 0 else A_sb1
                            v.wait_ge(sPE.h, m)
                            if ms:
                                v.wait_ge(sS.h, ms)
                            if h >= 2:
                                v.wait_ge(sPE.h, num_mark[h - 2])
                            sV.attach(v.tensor_mul(
                                Ab[:, jc * 512 + jc * 128: jc * 512 + jc * 128 + 128],
                                bank[bki][:, jc * 128:jc * 128 + 128],
                                maskd[:, :]))
                        DV.add(_)
                        aev_v[(h, jc)] = mv
                        abank_hist[bki].append((aev_v[(h, jc)],
                                                aev_s[(h, jc)]))
                        if jc == 0 and h >= 2:
                            emit_stage2b(h - 2)
                        if jc == 1 and h >= 1:
                            emit_mda(h - 1)

                    mn = sPE.reserve(1)

                    def _(te, h=h, b=b, pu14=prev_u14[0],
                          pu15=prev_u15[0], b67g=b67g):
                        qoff = (h % 2) * 64
                        nbk = bank[4] if h % 2 == 0 else bank[6]
                        Ab = A_sb0 if h % 2 == 0 else A_sb1
                        if h == 0:
                            te.wait_ge(dSL.h, 48 * b + 48)
                            if pu14:
                                te.wait_ge(sV.h, pu14)
                        if h == 1:
                            if pu15:
                                te.wait_ge(sV.h, pu15)
                            if b67g:
                                te.wait_ge(sV.h, b67g)
                        if h >= 2:
                            te.wait_ge(sV.h, u_mark[h - 2])
                        te.wait_ge(sV.h, s65_mark)
                        for jc in range(NJC):
                            te.wait_ge(sV.h, aev_v[(h, jc)])
                            if jc < NJC - 1:
                                te.wait_ge(sS.h, aev_s[(h, jc)])
                            te.matmul(
                                nbk[0:65, jc * 128:512],
                                V0[:, jc * 1040 + h * 65: jc * 1040 + h * 65 + 65],
                                Ab[:, jc * 512 + jc * 128: jc * 512 + 512],
                                start=(jc == 0), stop=False,
                                skip_group_check=True)
                        sPE.attach(te.matmul(
                            nbk[0:65, 0:512],
                            S65[qoff:qoff + 64, (h // 2) * 65:(h // 2) * 65 + 65],
                            phq[qoff:qoff + 64, (h // 2) * 512:(h // 2) * 512 + 512],
                            start=False, stop=True, skip_group_check=True))
                    DT.add(_)
                    num_mark[h] = mn
                    if h >= 1:
                        emit_den_s(h - 1)

                emit_mda(H - 1)
                emit_den_s(H - 1)
                emit_stage2a(H - 2)
                emit_stage2b(H - 2)
                emit_stage2a(H - 1)
                emit_stage2b(H - 1)
                u_last = sV.n
                last_A_pe = sPE.n

                if b + 1 < NB:
                    def _(sy, b=b, lastA=last_A_pe):
                        sy.wait_ge(sPE.h, lastA)
                        dSL.inc(sy.dma_start(phk_f[:, :],
                                             phk_scr.ap()[b + 1]), 16)
                        dSL.inc(sy.dma_start(V0[:, :], v_scr.ap()[b + 1]), 16)
                    blk.sync(_)

                wo_ev = []
                for oc in range(NFC):
                    bk = bank[oc % 2]
                    m = sPE.reserve(1)

                    def _(te, oc=oc, bk=bk, u_last=u_last,
                          pyl=prev_ypl[0]):
                        if oc == 0:
                            te.wait_ge(sV.h, u_last)
                            te.wait_ge(sS.h, rdbc_ev[H - 1])
                            te.wait_ge(sPL.h, wcast_all)
                            if pyl:
                                te.wait_ge(sPL.h, pyl)
                        if oc >= 2:
                            te.wait_ge(sS.h, wo_ev[oc - 2])
                        for fc in range(NFC):
                            inst = te.matmul(
                                bk[:, :],
                                wo_b[:, fc * 1024 + oc * 128: fc * 1024 + oc * 128 + 128],
                                u_sb[:, fc * 512:(fc + 1) * 512],
                                start=(fc == 0), stop=(fc == NFC - 1),
                                skip_group_check=True)
                        sPE.attach(inst)
                    DT.add(_)

                    ms = sS.reserve(1)

                    def _(sc, oc=oc, bk=bk, m=m):
                        sc.wait_ge(sPE.h, m)
                        sS.attach(sc.activation(
                            ao[:, oc * 512:(oc + 1) * 512], bk[:, :],
                            AF.Identity, bias=bo_sb[:, oc:oc + 1]))
                    DS.add(_)
                    wo_ev.append(ms)
                proj01_free[:] = [(sS, wo_ev[-1])]
                ao_ready = [(sS, sS.n)]

                if dbg:
                    def _(sy, b=b, am=sS.n, um=u_last):
                        sy.wait_ge(sS.h, am)
                        dW.inc(sy.dma_start(dbg_ao.ap()[b], ao[:, :]), 16)
                        sy.wait_ge(sV.h, um)
                        dW.inc(sy.dma_start(dbg_u.ap()[b], u_sb[:, :]), 16)
                        sy.wait_ge(dW.h, dW.n)
                    blk.sync(_)

                lnao_mark = emit_ln_stats(ao, False, ao_ready)
                loc_pairs = []
                for fc in range(NFC):
                    p = fc // 2
                    rguard = (loc_pairs[p - 2] if p >= 2 else
                              aq_prev_pairs[p])
                    ring = yring if p % 2 == 0 else yring2
                    if fc % 2 == 0:
                        mleft = sV.reserve(1)

                        def _(v, fc=fc, rg=rguard, lnm=lnao_mark, xw=xw_n,
                              xbuf=xbuf, xs=xw_sem, ring=ring):
                            if fc == 0:
                                v.wait_ge(sV.h, lnm)
                                v.wait_ge(xs.h, xw)
                            if rg:
                                v.wait_ge(dY.h, rg)
                            v.tensor_mul(e1s[:, 0:512],
                                         ao[:, fc * 512:(fc + 1) * 512],
                                         rbc[:, :])
                            v.tensor_add(e1s[:, 512:1024], e1s[:, 0:512],
                                         cbc[:, :])
                            sV.attach(v.tensor_add(
                                ring[:, 0:512], e1s[:, 512:1024],
                                xbuf[:, fc * 512:(fc + 1) * 512]))
                        DV.add(_)
                    else:
                        mright = sPL.reserve(1)

                        def _(g, fc=fc, rg=rguard, lnm=lnao_mark, xw=xw_n,
                              xbuf=xbuf, xs=xw_sem, ring=ring):
                            if fc == 1:
                                g.wait_ge(sV.h, lnm)
                                g.wait_ge(xs.h, xw)
                            if rg:
                                g.wait_ge(dY.h, rg)
                            g.tensor_mul(rs[:, 0:512],
                                         ao[:, fc * 512:(fc + 1) * 512],
                                         rbc[:, :])
                            g.tensor_add(rs[:, 512:1024], rs[:, 0:512],
                                         cbc[:, :])
                            sPL.attach(g.tensor_add(
                                ring[:, 512:1024], rs[:, 512:1024],
                                xbuf[:, fc * 512:(fc + 1) * 512]))
                        DP.add(_)

                        def _(sy, fc=fc, b=b, ml=mleft, mr=mright,
                              ring=ring):
                            sy.wait_ge(sV.h, ml)
                            sy.wait_ge(sPL.h, mr)
                            dst = y_scr.ap()[b].rearrange(
                                "p (c t) -> p c t", c=NFC)[:, fc - 1:fc + 1, :]
                            src = ring[:, :].rearrange("p (c t) -> p c t",
                                                       c=2)
                            dY.inc(sy.dma_start(dst, src), 16)
                        blk.sync(_)
                        loc_pairs.append(dY.n)
                aq_prev_pairs[:] = [loc_pairs[2], loc_pairs[3]]
                yst_marks.append(dY.n)
                prev_u15[0] = u_mark[H - 1]
                prev_u14[0] = u_mark[H - 2]
                prev_rdbc15[0] = rdbc_ev[H - 1]
                prev_rd15[0] = rd_mark[H - 1]
                prev_num15[0] = num_mark[H - 1]
                prev_ypl[0] = sPL.n
                flush(blk)
        if n_q_iters:
            blkQ.__exit__(None, None, None)

        stAq.close()
        stA.close()
        stB = ExitStack()
        win_b = sb(stB, "win_b", [128, 65536], BF16)
        y0 = sb(stB, "y0", [128, 4096], F32)
        ln2_b = sb(stB, "ln2_b", [128, 4096], BF16)
        hid0 = sb(stB, "hid0", [128, 2048], BF16)
        hid1 = sb(stB, "hid1", [128, 2048], BF16)
        G_ch = sb(stB, "G_ch", [128, 512], BF16)
        ffn_b = sb(stB, "ffn_b", [128, 4096], BF16)
        z_fp = sb(stB, "z_fp", [128, 512], F32)
        lgtm = sb(stB, "lgtm", [128, 64], F32)
        gtm_b = sb(stB, "gtm_b", [128, 64], BF16)
        gw0 = sb(stB, "gw0", [128, 16], F32)
        gw1 = sb(stB, "gw1", [128, 16], F32)
        gw2 = sb(stB, "gw2", [128, 16], F32)
        gc0 = sb(stB, "gc0", [128, 1], F32)
        gc1 = sb(stB, "gc1", [128, 1], F32)
        gfm_t = sb(stB, "gfm", [16, 512], BF16)
        gfm = gfm_t[:, :]
        lg0 = z_fp[0:16, :]
        lg1 = z_fp[32:48, :]
        bsc0 = hid0[:, :].bitcast(F32)
        bsc1 = hid1[:, :].bitcast(F32)

        with nc.Block() as blk:
            def _(sy):
                sy.wait_ge(dY.h, dY.n)
                dYL.inc(sy.dma_start(y0[:, :], y_scr.ap()[0]), 16)
                for hh in range(16):
                    dWB.inc(sy.dma_start(
                        win_b[:, hh * 4096:(hh + 1) * 4096],
                        wbf_scr.ap()[hh]), 16)
            blk.sync(_)
        yl_marks = [16]
        ost_marks = []
        wb_hist = []

        if stop_after in ("W0", "A", "AQ") or (
                isinstance(stop_after, str) and stop_after.startswith("A")
                and len(stop_after) > 1):
            n_b_iters = 0
        elif isinstance(stop_after, str) and stop_after.startswith("B"):
            n_b_iters = int(stop_after[1:])
        else:
            n_b_iters = NB
        prev_gel63 = [0]
        prev_ffn7 = [0]
        prev_gtev3 = [0]
        prev_zsq = [0]
        prev_ypl2 = [0]
        if n_b_iters:
            blkB = nc.Block()
            blkB.__enter__()
        for b in range(n_b_iters):
            if True:
                blk = blkB
                ylw = [(dYL, yl_marks[b])]
                ln_mark = emit_ln_stats(y0, False, ylw)

                ln2_v = sV.reserve(4)

                def _(v, lnm=ln_mark):
                    v.wait_ge(sV.h, lnm)
                    for fc in range(0, NFC, 2):
                        v.tensor_mul(bsc0[:, 0:512],
                                     y0[:, fc * 512:(fc + 1) * 512], rbc[:, :])
                        sV.attach(v.tensor_add(
                            ln2_b[:, fc * 512:(fc + 1) * 512],
                            bsc0[:, 0:512], cbc[:, :]))
                DV.add(_)

                ln2_pl = sPL.reserve(4)

                def _(g, lnm=ln_mark):
                    g.wait_ge(sV.h, lnm)
                    for fc in range(1, NFC, 2):
                        g.tensor_mul(bsc1[:, 0:512],
                                     y0[:, fc * 512:(fc + 1) * 512], rbc[:, :])
                        sPL.attach(g.tensor_add(
                            ln2_b[:, fc * 512:(fc + 1) * 512],
                            bsc1[:, 0:512], cbc[:, :]))
                DP.add(_)
                ln2_ready = [(sV, ln2_v), (sPL, ln2_pl)]

                RT = []
                rt_i = [0]
                SRT = []
                srt_i = [0]

                def rt_drain(k):
                    while k > 0 and rt_i[0] < len(RT):
                        DT.add(RT[rt_i[0]])
                        rt_i[0] += 1
                        k -= 1

                def srt_drain(k):
                    while k > 0 and srt_i[0] < len(SRT):
                        DS.add(SRT[srt_i[0]])
                        srt_i[0] += 1
                        k -= 1

                z_mm = sPR.reserve(1)

                def _(te):
                    for s, n in ln2_ready:
                        te.wait_ge(s.h, n)
                    for ec in range(NFC):
                        inst = te.matmul(
                            bank[4][:, :], wr_b[:, ec * 128:(ec + 1) * 128],
                            ln2_b[:, ec * 512:(ec + 1) * 512],
                            start=(ec == 0), stop=(ec == NFC - 1),
                            skip_group_check=True)
                    sPR.attach(inst)
                RT.append(_)

                z_ev = sV.reserve(1)

                def _(v, m=z_mm, pz=prev_zsq[0]):
                    v.wait_ge(sPR.h, m)
                    if pz:
                        v.wait_ge(sS.h, pz)
                    sV.attach(v.tensor_copy(z_fp[:, :], bank[4][:, :]))
                DV.add(_)

                zsq_m = sS.reserve(1)

                def _(sc, m=z_ev):
                    sc.wait_ge(sV.h, m)
                    sS.attach(sc.activation(sqring[:, 0:512], z_fp[:, :],
                                            AF.Square))
                DS.add(_)

                ssq_m = sPR.reserve(1)

                def _(te, m=zsq_m, pg=prev_gtev3[0]):
                    te.wait_ge(sS.h, m)
                    if pg:
                        te.wait_ge(sV.h, pg)
                    sPR.attach(te.matmul(bank[5][0:1, :], ones_cb[:, :],
                                         sqring[:, 0:512], start=True,
                                         stop=True, skip_group_check=True))
                RT.append(_)

                rn_m = sS.reserve(2)

                def _(sc, m=ssq_m, base=rn_m - 2):
                    sc.wait_ge(sPR.h, m)
                    sS.attach(sc.activation(r0[:, :], bank[5][0:1, :], AF.Ln))
                    sc.wait_ge(sS.h, base + 1)
                    sS.attach(sc.activation(r1[:, :], r0[:, :], AF.Exp,
                                            scale=-0.5))
                DS.add(_)

                rn2_m = sV.reserve(2)

                def _(v, m=rn_m, base=rn2_m - 2):
                    v.wait_ge(sS.h, m)
                    sV.attach(v.tensor_scalar_min(r3[:, :], r1[:, :], 1e12))
                    v.wait_ge(sV.h, base + 1)
                    sV.attach(v.tensor_scalar_mul(r2[:, :], r3[:, :],
                                                  expT[0:1, 0:1]))
                DV.add(_)

                lg_mm = sPR.reserve(1)

                def _(te, m=z_ev):
                    te.wait_ge(sV.h, m)
                    sPR.attach(te.matmul(bank[4][0:16, :], cnT[:, :],
                                         z_fp[:, :], start=True, stop=True,
                                         skip_group_check=True))
                RT.append(_)

                lg_ev = sV.reserve(2)

                def _(v, m=lg_mm, m2=rn2_m, base=lg_ev - 2, mz=zsq_m):
                    v.wait_ge(sPR.h, m)
                    v.wait_ge(sS.h, mz)
                    sV.attach(v.tensor_copy(lg0, bank[4][0:16, :]))
                    v.wait_ge(sV.h, m2)
                    sV.attach(v.tensor_copy(rden_b[:, 0:512], r2[:, :]))
                DV.add(_)

                rnbc_m = sPR.reserve(1)

                def _(te, m=lg_ev, mr=rn_m):
                    te.wait_ge(sV.h, m)
                    te.wait_ge(sS.h, mr)
                    sPR.attach(te.matmul(bank[5][0:16, :], ones_rb[0:1, 0:16],
                                         rden_b[0:1, 0:512], start=True,
                                         stop=True, skip_group_check=True))
                RT.append(_)

                lg1_m = sV.reserve(1)

                def _(v, m=rnbc_m):
                    v.wait_ge(sPR.h, m)
                    sV.attach(v.tensor_mul(lg1, lg0, bank[5][0:16, :]))
                DV.add(_)

                lt_ev = []
                for c in range(NJC):
                    tbk = bank[3 + 2 * (c % 2)]
                    m = sPR.reserve(1)

                    def _(te, c=c, tbk=tbk, lg1_m=lg1_m):
                        te.wait_ge(sV.h, lg1_m)
                        if c >= 2:
                            te.wait_ge(sV.h, lt_ev[c - 2])
                        sPR.attach(te.transpose(tbk[:, c * 16:(c + 1) * 16],
                                                lg1[:, c * 128:(c + 1) * 128],
                                                ident_f[32:48, 32:48]))
                    RT.append(_)

                    mv = sV.reserve(1)

                    def _(v, c=c, tbk=tbk, m=m):
                        v.wait_ge(sPR.h, m)
                        sV.attach(v.tensor_copy(lgtm[:, c * 16:(c + 1) * 16],
                                                tbk[:, c * 16:(c + 1) * 16]))
                    DV.add(_)
                    lt_ev.append(mv)

                sm_exp = []
                g_done = []
                for c in range(NJC):
                    sub_m = sV.reserve(2)

                    def _(v, c=c, base=sub_m - 2):
                        v.wait_ge(sV.h, lt_ev[c])
                        sV.attach(v.reduce_max(gc0[:, :],
                                               lgtm[:, c * 16:(c + 1) * 16],
                                               axis=AX))
                        v.wait_ge(sV.h, base + 1)
                        sV.attach(v.tensor_scalar(
                            gw0[:, :], lgtm[:, c * 16:(c + 1) * 16],
                            gc0[:, :], None, OP.subtract))
                    DV.add(_)

                    me = sSS.reserve(1)

                    def _(sc, c=c, m=sub_m):
                        sc.wait_ge(sV.h, m)
                        sSS.attach(sc.activation(gw1[:, :], gw0[:, :],
                                                 AF.Exp))
                    SRT.append(_)
                    sm_exp.append(me)

                    mg = sV.reserve(9)

                    def _(v, c=c, me=me, base=mg - 9):
                        k = [base]

                        def step(inst):
                            k[0] += 1
                            sV.attach(inst)
                            v.wait_ge(sV.h, k[0])
                        v.wait_ge(sSS.h, me)
                        step(v.reduce_sum(gc1[:, :], gw1[:, :], axis=AX))
                        step(v.reciprocal(gc0[:, :], gc1[:, :]))
                        step(v.tensor_scalar(gw0[:, :], gw1[:, :], gc0[:, :],
                                             None, OP.mult))
                        step(v.reduce_max(gc1[:, :], gw0[:, :], axis=AX))
                        step(v.tensor_scalar(gw1[:, :], gw0[:, :], gc1[:, :],
                                             None, OP.is_equal))
                        step(v.scalar_tensor_tensor(gw2[:, :], gw1[:, :],
                                                    -1e30, gw0[:, :],
                                                    OP.mult, OP.add))
                        step(v.reduce_max(gc1[:, :], gw2[:, :], axis=AX))
                        step(v.tensor_scalar(gw2[:, :], gw0[:, :], gc1[:, :],
                                             None, OP.is_ge))
                        sV.attach(v.tensor_mul(gtm_b[:, c * 16:(c + 1) * 16],
                                               gw0[:, :], gw2[:, :]))
                    DV.add(_)
                    g_done.append(mg)

                gt_ev = []
                for c in range(NJC):
                    tbk = bank[3 + 2 * (c % 2)]
                    m = sPR.reserve(1)

                    def _(te, c=c, tbk=tbk):
                        te.wait_ge(sV.h, g_done[c])
                        if c >= 2:
                            te.wait_ge(sV.h, gt_ev[c - 2])
                        sPR.attach(te.transpose(
                            tbk[:, :].bitcast(BF16)[0:16, c * 128:(c + 1) * 128],
                            gtm_b[:, c * 16:(c + 1) * 16], ident_b[:, :]))
                    RT.append(_)

                    mv = sV.reserve(1)

                    def _(v, c=c, tbk=tbk, m=m):
                        v.wait_ge(sPR.h, m)
                        sV.attach(v.tensor_copy(
                            gfm[:, c * 128:(c + 1) * 128],
                            tbk[:, :].bitcast(BF16)[0:16, c * 128:(c + 1) * 128]))
                    DV.add(_)
                    gt_ev.append(mv)
                gfm_m = sV.n

                gel_m = {}
                o_mm = {}
                G_ev = {}
                ffn_ev = {}

                for h in range(H):
                    if h % 2 == 0:
                        c = h // 2
                        gsem = sPR if c == 0 else sPE
                        m = gsem.reserve(1)

                        def _(te, c=c, gfm_m=gfm_m, gsem=gsem):
                            te.wait_ge(sV.h, gfm_m)
                            if c >= 1:
                                te.wait_ge(sV.h, G_ev[c - 1])
                            gsem.attach(te.matmul(
                                bank[3][:, :],
                                et_b[0:16, c * 128:(c + 1) * 128],
                                gfm[:, :], start=True, stop=True,
                                skip_group_check=True))
                        (RT.append if c == 0 else DT.add)(_)

                        mv = sV.reserve(1)

                        def _(v, c=c, m=m, gsem=gsem):
                            v.wait_ge(gsem.h, m)
                            sV.attach(v.tensor_copy(G_ch[:, :],
                                                    bank[3][:, :]))
                        DV.add(_)
                        G_ev[c] = mv

                    for hc in range(4):
                        gi = h * 4 + hc
                        m = sPE.reserve(1)

                        def _(te, h=h, hc=hc, gi=gi, pg=prev_gel63[0]):
                            bk = bank[gi % 2]
                            if h == 0 and hc == 0:
                                for s, n in ln2_ready:
                                    te.wait_ge(s.h, n)
                            if gi < 2 and pg:
                                te.wait_ge(sS.h, pg)
                            if gi >= 2:
                                te.wait_ge(sS.h, gel_m[gi - 2])
                            if h == 0 and hc == 0:
                                te.wait_ge(dWB.h, 256)
                            for ec in range(NFC):
                                inst = te.matmul(
                                    bk[:, :],
                                    win_b[:, h * 4096 + ec * 512 + hc * 128:
                                          h * 4096 + ec * 512 + hc * 128 + 128],
                                    ln2_b[:, ec * 512:(ec + 1) * 512],
                                    start=(ec == 0), stop=(ec == NFC - 1),
                                    skip_group_check=True)
                            sPE.attach(inst)
                        if h == 0:
                            rt_drain([1, 3, 4, 0][hc])
                        elif h == 1:
                            rt_drain([2, 2, 1, 0][hc])
                        DT.add(_)

                        ms = sS.reserve(1)

                        def _(sc, h=h, hc=hc, gi=gi, m=m):
                            hbuf = hid0 if h % 2 == 0 else hid1
                            sc.wait_ge(sPE.h, m)
                            if h >= 2:
                                sc.wait_ge(sPE.h, o_mm[h - 2])
                            bk = bank[gi % 2]
                            sS.attach(sc.activation(
                                hbuf[:, hc * 512:(hc + 1) * 512], bk[:, :],
                                AF.Gelu))
                        DS.add(_)
                        gel_m[gi] = ms

                    if h == 0:
                        srt_drain(99)
                    if h == 1:
                        rt_drain(99)
                    mo = sPE.reserve(1)

                    def _(te, h=h, pf=prev_ffn7[0]):
                        hbuf = hid0 if h % 2 == 0 else hid1
                        for hc in range(4):
                            te.wait_ge(sS.h, gel_m[h * 4 + hc])
                        if h < 2 and pf:
                            te.wait_ge(sV.h, pf)
                        if h >= 2:
                            te.wait_ge(sV.h, ffn_ev[h // 2 - 1])
                        for hc in range(4):
                            inst = te.matmul(
                                bank[2][(h % 2) * 64:(h % 2) * 64 + 64, :],
                                wout_b[:, h * 256 + hc * 64: h * 256 + hc * 64 + 64],
                                hbuf[:, hc * 512:(hc + 1) * 512],
                                start=(hc == 0), stop=(hc == 3),
                                skip_group_check=True)
                        sPE.attach(inst)
                    DT.add(_)
                    o_mm[h] = mo

                    if h % 2 == 1:
                        c = h // 2
                        mv = sV.reserve(1)

                        def _(v, c=c, mo=mo, pyl=prev_ypl2[0]):
                            v.wait_ge(sPE.h, mo)
                            v.wait_ge(sV.h, G_ev[c])
                            if c == 0 and pyl:
                                v.wait_ge(sPL.h, pyl)
                            sV.attach(v.tensor_mul(
                                ffn_b[:, c * 512:(c + 1) * 512],
                                bank[2][:, :], G_ch[:, :]))
                        DV.add(_)
                        ffn_ev[c] = mv
                ffn_done = sV.n

                if dbg:
                    def _(sy, b=b, fd=ffn_done, gm=gfm_m):
                        sy.wait_ge(sV.h, fd)
                        dW.inc(sy.dma_start(dbg_ffn.ap()[b], ffn_b[:, :]), 16)
                        dW.inc(sy.dma_start(dbg_hid.ap()[b], hid0[:, :]), 16)
                        sy.wait_ge(sV.h, gm)
                        dW.inc(sy.dma_start(dbg_gtm.ap()[b],
                                            lgtm[:, :]), 16)
                        sy.wait_ge(dW.h, dW.n)
                    blk.sync(_)
                ln3_mark = emit_ln_stats(ffn_b, True, [(sV, ffn_done)])
                wb_pm = []
                for fc in range(NFC):
                    if fc % 2 == 0:
                        mleft = sV.reserve(1)

                        def _(v, fc=fc, lnm=ln3_mark):
                            if fc == 0:
                                v.wait_ge(sV.h, lnm)
                            v.tensor_mul(bsc0[:, 0:512],
                                         ffn_b[:, fc * 512:(fc + 1) * 512],
                                         rbc[:, :])
                            v.tensor_add(bsc0[:, 512:1024], bsc0[:, 0:512],
                                         cbc[:, :])
                            sV.attach(v.tensor_add(
                                y0[:, fc * 512:(fc + 1) * 512],
                                y0[:, fc * 512:(fc + 1) * 512],
                                bsc0[:, 512:1024]))
                        DV.add(_)
                    else:
                        mright = sPL.reserve(1)

                        def _(g, fc=fc, lnm=ln3_mark):
                            if fc == 1:
                                g.wait_ge(sV.h, lnm)
                            g.tensor_mul(bsc1[:, 0:512],
                                         ffn_b[:, fc * 512:(fc + 1) * 512],
                                         rbc[:, :])
                            g.tensor_add(bsc1[:, 512:1024], bsc1[:, 0:512],
                                         cbc[:, :])
                            sPL.attach(g.tensor_add(
                                y0[:, fc * 512:(fc + 1) * 512],
                                y0[:, fc * 512:(fc + 1) * 512],
                                bsc1[:, 512:1024]))
                        DP.add(_)

                        def _(sy, fc=fc, b=b, ml=mleft, mr=mright):
                            sy.wait_ge(sV.h, ml)
                            sy.wait_ge(sPL.h, mr)
                            dstt = outT.ap().rearrange("(c p) t -> p c t",
                                                       p=128)
                            dst = dstt[:, fc - 1:fc + 1, b * P:(b + 1) * P]
                            src = y0[:, (fc - 1) * 512:(fc + 1) * 512]
                            dO.inc(sy.dma_start(
                                dst, src.rearrange("p (c t) -> p c t", c=2)),
                                16)
                        blk.sync(_)
                        wb_pm.append(dO.n)
                ost_marks.append(dO.n)

                if dbg2:
                    def _(sy, b=b, mo=dO.n):
                        sy.wait_ge(dO.h, mo)
                        dW.inc(sy.dma_start(dbg_ffn2.ap()[b], ffn_b[:, :]),
                               16)
                        dW.inc(sy.dma_start(dbg_g2.ap()[b], gfm), 16)
                        sy.wait_ge(dW.h, dW.n)
                    blk.sync(_)
                if b + 1 < NB:
                    def _(sy, b=b, pm=list(wb_pm)):
                        ysrc = y_scr.ap()[b + 1]
                        for j in range(4):
                            sy.wait_ge(dO.h, pm[j])
                            dYL.inc(sy.dma_start(
                                y0[:, j * 1024:(j + 1) * 1024],
                                ysrc[:, j * 1024:(j + 1) * 1024]), 16)
                    blk.sync(_)
                    yl_marks.append(dYL.n)
                prev_gel63[0] = gel_m[4 * H - 1]
                prev_ffn7[0] = ffn_ev[H // 2 - 1]
                prev_gtev3[0] = gt_ev[NJC - 1]
                prev_zsq[0] = zsq_m
                prev_ypl2[0] = sPL.n
                flush(blk)
        if n_b_iters:
            blkB.__exit__(None, None, None)

        with nc.Block() as blk:
            def _(sy):
                if dbg2 and n_b_iters:
                    dW.inc(sy.dma_start(dbg_wb.ap()[0],
                                        win_b[:, 11 * 4096:12 * 4096]), 16)
                    dW.inc(sy.dma_start(dbg_wb.ap()[1],
                                        win_b[:, 12 * 4096:13 * 4096]), 16)
                    sy.wait_ge(dW.h, dW.n)
                sy.wait_ge(dO.h, dO.n)
            blk.sync(_)

        stB.close()
    return nc


def _prep_weights(inputs):
    Wq = inputs["L0_attn_Wq"]
    Wk = inputs["L0_attn_Wk"]
    Wv = inputs["L0_attn_Wv"]
    Wo = inputs["L0_attn_Wo"]

    def wtile(W):
        return np.ascontiguousarray(
            W.T.reshape(8, 128, 1024).transpose(1, 0, 2).reshape(128, 8192)
        ).astype(np.float32)

    wr = inputs["L0_ffn_w_router"]
    wr_h = np.ascontiguousarray(
        wr.T.reshape(8, 128, 128).transpose(1, 0, 2).reshape(128, 1024)
    ).astype(np.float32)
    ex_in = inputs["L0_ffn_input_experts"]
    win_h = np.ascontiguousarray(
        ex_in.reshape(16, 8, 128, 4, 128).transpose(2, 0, 1, 3, 4)
        .reshape(128, 65536)).astype(np.float32)
    ex_out = inputs["L0_ffn_output_experts"]
    wout_h = np.ascontiguousarray(
        ex_out.reshape(16, 4, 128, 64).transpose(2, 0, 1, 3)
        .reshape(128, 4096)).astype(np.float32)
    bo = inputs["L0_attn_bo"]
    bo_h = np.ascontiguousarray(bo.reshape(8, 128).T).astype(np.float32)
    cT = np.ascontiguousarray(
        inputs["L0_ffn_head_centroids"].T).astype(np.float32)
    temp = np.asarray(inputs["L0_ffn_temperature"],
                      dtype=np.float32).reshape(1, 1)
    et = np.zeros((16, 1024), dtype=np.float32)
    for n in range(16):
        et[n, n * 64:(n + 1) * 64] = 1.0
    ident = np.eye(128, dtype=np.float32)
    maskd = np.triu(np.ones((128, 128), dtype=np.float32))
    return {
        "wq_h": wtile(Wq), "wk_h": wtile(Wk), "wv_h": wtile(Wv),
        "wo_h": wtile(Wo), "wr_hh": wr_h, "win_h": win_h, "wout_hh": wout_h,
        "bo_h": bo_h, "cT_h": cT, "temp_h": temp, "et_h": et,
        "ident_h": ident, "maskd_h": maskd,
    }


def kernel(**inputs):
    x = np.asarray(inputs["L0_attn_x"], dtype=np.float32)
    B, T, _ = x.shape
    shared = _prep_weights(inputs)

    nc = build_nc()

    in_maps = []
    xp_cache = {}
    for c in range(8):
        bi, half = c // 2, c % 2
        xs = np.ascontiguousarray(
            x[bi, half * TOK:(half + 1) * TOK, :].T).astype(np.float32)
        if bi not in xp_cache:
            xp_cache[bi] = np.ascontiguousarray(
                x[bi, 0:TOK, :].T).astype(np.float32)
        m = dict(shared)
        m["xT"] = xs
        m["xpT"] = xp_cache[bi]
        m["gm_h"] = np.full((128, 1), 1.0 if half == 1 else 0.0,
                            dtype=np.float32)
        in_maps.append(m)

    res = run_bass_kernel_spmd(nc, in_maps, core_ids=list(range(8)))

    out = np.empty((B, T, E), dtype=np.float32)
    for c in range(8):
        bi, half = c // 2, c % 2
        out[bi, half * TOK:(half + 1) * TOK, :] = res.results[c]["outT"].T
    return out


# version 22
# speedup vs baseline: 1.0348x; 1.0053x over previous
import numpy as np
from contextlib import ExitStack

import concourse.bass as bass
import concourse.mybir as mybir
from concourse.bass_utils import run_bass_kernel_spmd
from concourse.alu_op_type import AluOpType as OP

F32 = mybir.dt.float32
BF16 = mybir.dt.bfloat16
AF = mybir.ActivationFunctionType
AX = mybir.AxisListType.X

TOK = 4096
P = 512
NB = TOK // P
E = 1024
NFC = E // 128
H = 16
D = 64
NJC = P // 128
INV_E = 1.0 / E
LN_EPS = 1e-5
ATT_EPS = 1e-5
INV_SQRT_G = 1.0 / np.sqrt(128.0)


class Sem:
    def __init__(self, nc, ctx, name):
        self.h = ctx.enter_context(nc.semaphore(name))
        self.n = 0

    def reserve(self, k=1):
        self.n += k
        return self.n

    def attach(self, inst, k=1):
        inst.then_inc(self.h, k)

    def inc(self, inst, k=1):
        inst.then_inc(self.h, k)
        self.n += k
        return self.n


class Dfr:

    def __init__(self, name):
        self.name = name
        self.q = []

    def add(self, fn):
        self.q.append(fn)

    def flush(self, blk):
        if not self.q:
            return
        q = self.q
        self.q = []

        def body(eng):
            for f in q:
                f(eng)
        getattr(blk, self.name)(body)


def build_nc(stop_after=None, dbg=False, dbg2=False, a_parts='lkvts'):
    nc = bass.Bass("TRN2", target_bir_lowering=False)

    xT = nc.dram_tensor("xT", [E, TOK], F32, kind="ExternalInput")
    wq_h = nc.dram_tensor("wq_h", [128, 8192], F32, kind="ExternalInput")
    wk_h = nc.dram_tensor("wk_h", [128, 8192], F32, kind="ExternalInput")
    wv_h = nc.dram_tensor("wv_h", [128, 8192], F32, kind="ExternalInput")
    wo_h = nc.dram_tensor("wo_h", [128, 8192], F32, kind="ExternalInput")
    bo_h = nc.dram_tensor("bo_h", [128, 8], F32, kind="ExternalInput")
    wr_hh = nc.dram_tensor("wr_hh", [128, 1024], F32, kind="ExternalInput")
    cT_h = nc.dram_tensor("cT_h", [128, 16], F32, kind="ExternalInput")
    temp_h = nc.dram_tensor("temp_h", [1, 1], F32, kind="ExternalInput")
    win_h = nc.dram_tensor("win_h", [128, 65536], F32, kind="ExternalInput")
    wout_hh = nc.dram_tensor("wout_hh", [128, 4096], F32, kind="ExternalInput")
    et_h = nc.dram_tensor("et_h", [16, 1024], F32, kind="ExternalInput")
    ident_h = nc.dram_tensor("ident_h", [128, 128], F32, kind="ExternalInput")
    maskd_h = nc.dram_tensor("maskd_h", [128, 128], F32, kind="ExternalInput")
    xpT = nc.dram_tensor("xpT", [E, TOK], F32, kind="ExternalInput")
    gm_h = nc.dram_tensor("gm_h", [128, 1], F32, kind="ExternalInput")
    outT = nc.dram_tensor("outT", [E, TOK], F32, kind="ExternalOutput")

    _sk = "ExternalOutput" if dbg else "Internal"
    ln1_scr = nc.dram_tensor("ln1_scr", [NB, 128, 4096], BF16, kind=_sk)
    phk_scr = nc.dram_tensor("phk_scr", [NB, 128, 4096], BF16, kind=_sk)
    v_scr = nc.dram_tensor("v_scr", [NB, 128, 4160], BF16, kind=_sk)
    y_scr = nc.dram_tensor("y_scr", [NB, 128, 4096], F32, kind=_sk)
    wbf_scr = nc.dram_tensor("wbf_scr", [16, 128, 4096], BF16,
                             kind="ExternalOutput" if dbg2 else "Internal")
    if dbg2:
        dbg_ffn2 = nc.dram_tensor("dbg_ffn2", [NB, 128, 4096], BF16,
                                  kind="ExternalOutput")
        dbg_g2 = nc.dram_tensor("dbg_g2", [NB, 16, 512], BF16,
                                kind="ExternalOutput")
        dbg_wb = nc.dram_tensor("dbg_wb", [2, 128, 4096], BF16,
                                kind="ExternalOutput")
    if dbg:
        dbg_ao = nc.dram_tensor("dbg_ao", [NB, 128, 4096], F32,
                                kind="ExternalOutput")
        dbg_u = nc.dram_tensor("dbg_u", [NB, 128, 4096], BF16,
                               kind="ExternalOutput")
        dbg_ffn = nc.dram_tensor("dbg_ffn", [NB, 128, 4096], BF16,
                                 kind="ExternalOutput")
        dbg_gtm = nc.dram_tensor("dbg_gtm", [NB, 128, 64], F32,
                                 kind="ExternalOutput")
        dbg_hid = nc.dram_tensor("dbg_hid", [NB, 128, 2048], BF16,
                                 kind="ExternalOutput")

    ctx = ExitStack()
    with ctx:
        def sb(st, name, shape, dt):
            return st.enter_context(nc.sbuf_tensor(name, shape, dt))

        ident_f = sb(ctx, "ident_f", [128, 128], F32)
        ident_b = sb(ctx, "ident_b", [128, 128], BF16)
        ones_cf = sb(ctx, "ones_cf", [128, 1], F32)
        ones_cb = sb(ctx, "ones_cb", [128, 1], BF16)
        ones_rf = sb(ctx, "ones_rf", [1, 128], F32)
        ones_rb = sb(ctx, "ones_rb", [1, 128], BF16)
        ones_mb = sb(ctx, "ones_mb", [128, 1], BF16)
        cnT = sb(ctx, "cnT", [128, 16], F32)
        expT = sb(ctx, "expT", [1, 1], F32)
        r0 = sb(ctx, "r0", [1, 512], F32)
        r1 = sb(ctx, "r1", [1, 512], F32)
        r2 = sb(ctx, "r2", [1, 512], F32)
        r3 = sb(ctx, "r3", [1, 512], F32)
        rden_b = sb(ctx, "rden_b", [1, 512], BF16)
        rbc = sb(ctx, "rbc", [128, 512], F32)
        cbc = sb(ctx, "cbc", [128, 512], F32)
        sqring = sb(ctx, "sqring", [128, 1024], BF16)
        yring = sb(ctx, "yring", [128, 1024], F32)
        wr_b = sb(ctx, "wr_b", [128, 1024], BF16)
        et_b = sb(ctx, "et_b", [16, 1024], BF16)
        wout_b = sb(ctx, "wout_b", [128, 4096], BF16)
        castr = sb(ctx, "castr", [128, 1024], BF16)

        bank = [ctx.enter_context(nc.psum_tensor(f"bk{i}", [128, 512], F32))
                for i in range(8)]

        sPE = Sem(nc, ctx, "sPE")
        sV = Sem(nc, ctx, "sV")
        sS = Sem(nc, ctx, "sS")
        sPL = Sem(nc, ctx, "sPL")
        sPR = Sem(nc, ctx, "sPR")
        sSS = Sem(nc, ctx, "sSS")
        dX = Sem(nc, ctx, "dX")
        dX2 = Sem(nc, ctx, "dX2")
        dSL = Sem(nc, ctx, "dSL")
        dSC = Sem(nc, ctx, "dSC")
        dY = Sem(nc, ctx, "dY")
        dYL = Sem(nc, ctx, "dYL")
        dW = Sem(nc, ctx, "dW")
        dWS = Sem(nc, ctx, "dWS")
        dWB = Sem(nc, ctx, "dWB")
        dO = Sem(nc, ctx, "dO")

        DT = Dfr("tensor")
        DV = Dfr("vector")
        DS = Dfr("scalar")
        DP = Dfr("gpsimd")

        def flush(blk):
            DT.flush(blk)
            DV.flush(blk)
            DS.flush(blk)
            DP.flush(blk)

        stA = ExitStack()
        maskd = sb(stA, "maskd", [128, 128], F32)
        bo_sb = sb(stA, "bo_sb", [128, 8], F32)
        gm_sb = sb(stA, "gm_sb", [128, 1], F32)
        wq_b = sb(stA, "wq_b", [128, 8192], BF16)
        wo_b = sb(stA, "wo_b", [128, 8192], BF16)
        S_sb = sb(stA, "S_sb", [128, 512], F32)
        z_sb = sb(stA, "z_sb", [128, 8], F32)
        ckS = sb(stA, "ckS", [128, 4096], F32)
        ckz = sb(stA, "ckz", [128, 64], F32)
        Spre_sb = sb(stA, "Spre_sb", [128, 512], F32)
        zpre = sb(stA, "zpre", [128, 8], F32)
        x0 = sb(stA, "x0", [128, 4096], F32)
        ln1_b = sb(stA, "ln1_b", [128, 4096], BF16)
        phk_f = sb(stA, "phk_f", [128, 4096], BF16)
        V0 = sb(stA, "V0", [128, 4160], BF16)
        e1s = sb(stA, "e1s", [128, 1024], F32)
        rs = sb(stA, "rs", [128, 1024], F32)

        stAk = ExitStack()
        wk_b = sb(stAk, "wk_b", [128, 8192], BF16)
        wv_b = sb(stAk, "wv_b", [128, 8192], BF16)
        phk_t = sb(stAk, "phk_t", [128, 4096], BF16)
        wstg = sb(stAk, "wstg", [128, 4096], F32)
        wcast = sb(stAk, "wcast", [128, 4096], BF16)

        wjobs = []
        wjobs.append((wq_h.ap()[:, 0:4096], 4096, wq_b[:, 0:4096], None))
        wjobs.append((wq_h.ap()[:, 4096:8192], 4096, wq_b[:, 4096:8192], None))
        wjobs.append((wo_h.ap()[:, 0:4096], 4096, wo_b[:, 0:4096], None))
        wjobs.append((wo_h.ap()[:, 4096:8192], 4096, wo_b[:, 4096:8192], None))
        wjobs.append((wr_hh.ap()[:, :], 1024, wr_b[:, :], None))
        wjobs.append((wout_hh.ap()[:, :], 4096, wout_b[:, :], None))
        for hh in range(16):
            wjobs.append((win_h.ap()[:, hh * 4096:(hh + 1) * 4096], 4096,
                          None, hh))
        wstate = {"free_pl": 0, "wc_store": 0, "ji": 0}

        def emit_wjobs(blk, kmax):
            for _ in range(kmax):
                if wstate["ji"] >= len(wjobs):
                    return
                src, width, dst, scr = wjobs[wstate["ji"]]
                wstate["ji"] += 1
                fpl = wstate["free_pl"]
                wcs = wstate["wc_store"]

                def _(sy, src=src, width=width, fpl=fpl):
                    if fpl:
                        sy.wait_ge(sPL.h, fpl)
                    dW.inc(sy.dma_start(wstg[:, 0:width], src), 16)
                blk.sync(_)
                ld_mark = dW.n
                cm = sPL.reserve(1)
                if dst is not None:
                    def _(g, dst=dst, width=width, ld=ld_mark):
                        g.wait_ge(dW.h, ld)
                        sPL.attach(g.tensor_copy(dst, wstg[:, 0:width]))
                    DP.add(_)
                    wstate["free_pl"] = cm
                else:
                    def _(g, width=width, ld=ld_mark, wcs=wcs):
                        g.wait_ge(dW.h, ld)
                        if wcs:
                            g.wait_ge(dWS.h, wcs)
                        sPL.attach(g.tensor_copy(wcast[:, 0:width],
                                                 wstg[:, 0:width]))
                    DP.add(_)
                    wstate["free_pl"] = cm

                    def _(sy, scr=scr, cm=cm):
                        sy.wait_ge(sPL.h, cm)
                        dWS.inc(sy.dma_start(wbf_scr.ap()[scr], wcast[:, :]),
                                16)
                    blk.sync(_)
                    wstate["wc_store"] = dWS.n

        lnst = {"b67_free_v": 0, "stat_last": 0}

        def emit_ln_stats(src, srcbf, src_ready):
            free_v = lnst["b67_free_v"]
            stat_last = lnst["stat_last"]
            pool_guard = sPL.n
            sq_m = []
            summ_m = []
            sqm_m = []
            for fc in range(NFC):
                if srcbf:
                    m = sS.reserve(1)

                    def _(sc, fc=fc, stat_last=stat_last):
                        if fc == 0:
                            for s, n in src_ready:
                                sc.wait_ge(s.h, n)
                        if fc < 2 and stat_last:
                            sc.wait_ge(sPE.h, stat_last)
                        if fc >= 2:
                            sc.wait_ge(sPE.h, sqm_m[fc - 2])
                        sS.attach(sc.activation(
                            sqring[:, (fc % 2) * 512:(fc % 2) * 512 + 512],
                            src[:, fc * 512:(fc + 1) * 512], AF.Square))
                    DS.add(_)
                    sq_m.append(m)
                else:
                    m = sS.reserve(2)

                    def _(sc, fc=fc, stat_last=stat_last):
                        if fc == 0:
                            for s, n in src_ready:
                                sc.wait_ge(s.h, n)
                        if fc < 2 and stat_last:
                            sc.wait_ge(sPE.h, stat_last)
                        if fc >= 2:
                            sc.wait_ge(sPE.h, summ_m[fc - 2])
                            sc.wait_ge(sPE.h, sqm_m[fc - 2])
                        sS.attach(sc.copy(
                            castr[:, (fc % 2) * 512:(fc % 2) * 512 + 512],
                            src[:, fc * 512:(fc + 1) * 512]))
                        sS.attach(sc.activation(
                            sqring[:, (fc % 2) * 512:(fc % 2) * 512 + 512],
                            castr[:, (fc % 2) * 512:(fc % 2) * 512 + 512],
                            AF.Square))
                    DS.add(_)
                    sq_m.append(m)

            for fc in range(NFC):
                m1 = sPE.reserve(1)
                m2 = sPE.reserve(1)

                def _(te, fc=fc):
                    if fc == 0 and free_v:
                        te.wait_ge(sV.h, free_v)
                    te.wait_ge(sS.h, sq_m[fc])
                    if srcbf:
                        sPE.attach(te.matmul(bank[6][0:1, :], ones_mb[:, :],
                                             src[:, fc * 512:(fc + 1) * 512],
                                             start=(fc == 0),
                                             stop=(fc == NFC - 1),
                                             skip_group_check=True))
                    else:
                        sPE.attach(te.matmul(
                            bank[6][0:1, :], ones_mb[:, :],
                            castr[:, (fc % 2) * 512:(fc % 2) * 512 + 512],
                            start=(fc == 0), stop=(fc == NFC - 1),
                            skip_group_check=True))
                    sPE.attach(te.matmul(
                        bank[7][0:1, :], ones_mb[:, :],
                        sqring[:, (fc % 2) * 512:(fc % 2) * 512 + 512],
                        start=(fc == 0), stop=(fc == NFC - 1),
                        skip_group_check=True))
                DT.add(_)
                summ_m.append(m1)
                sqm_m.append(m2)
            stat_mark = sPE.n

            sqz = sS.reserve(1)

            def _(sc):
                sc.wait_ge(sPE.h, stat_mark)
                sS.attach(sc.activation(r2[:, :], bank[6][0:1, :],
                                        AF.Square))
            DS.add(_)

            fin = sV.reserve(1)

            def _(v, sqz=sqz):
                v.wait_ge(sS.h, sqz)
                sV.attach(v.scalar_tensor_tensor(r3[:, :], bank[7][0:1, :],
                                                 LN_EPS, r2[:, :],
                                                 OP.add, OP.subtract))
            DV.add(_)

            rstd = sS.reserve(3)

            def _(sc, fin=fin, base=rstd - 3):
                sc.wait_ge(sV.h, fin)
                sS.attach(sc.activation(r1[:, :], r3[:, :], AF.Ln))
                sc.wait_ge(sS.h, base + 1)
                sS.attach(sc.activation(r2[:, :], r1[:, :], AF.Exp,
                                        scale=-0.5))
                sc.wait_ge(sS.h, base + 2)
                sS.attach(sc.copy(sqring[0:1, 0:512], r2[:, :]))
            DS.add(_)

            cneg = sV.reserve(1)

            def _(v, rstd=rstd):
                v.wait_ge(sS.h, rstd - 1)
                sV.attach(v.scalar_tensor_tensor(
                    sqring[0:1, 512:1024], bank[6][0:1, :], -1.0,
                    r2[:, :], OP.mult, OP.mult))
            DV.add(_)

            bc = sPE.reserve(1)

            def _(te, cneg=cneg, rstd=rstd):
                te.wait_ge(sV.h, cneg)
                te.wait_ge(sS.h, rstd)
                te.matmul(bank[6][:, :], ones_rb[0:1, :],
                          sqring[0:1, 0:512], start=True, stop=True,
                          skip_group_check=True)
                sPE.attach(te.matmul(bank[7][:, :], ones_rb[0:1, :],
                                     sqring[0:1, 512:1024], start=True,
                                     stop=True, skip_group_check=True))
            DT.add(_)

            bcev = sV.reserve(1)

            def _(v, bc=bc, pg=pool_guard):
                v.wait_ge(sPE.h, bc)
                if pg:
                    v.wait_ge(sPL.h, pg)
                v.tensor_copy(rbc[:, :], bank[6][:, :])
                sV.attach(v.tensor_copy(cbc[:, :], bank[7][:, :]))
            DV.add(_)
            lnst["b67_free_v"] = bcev
            lnst["stat_last"] = bc
            return bcev

        def emit_elu_proj(w_sb, dst_bf, rhs_bf, rhs_ready, prev_free,
                          mask=None, parts='mse'):
            evac_s = []
            evac_v = []
            mmm = []
            for oc in range(NFC):
                bk = bank[oc % 2]
                s_ = (oc % 2) * 512
                o_ = 512 - s_
                m = sPE.reserve(1)

                def _(te, oc=oc, bk=bk):
                    if oc == 0:
                        for s, n in rhs_ready:
                            te.wait_ge(s.h, n)
                    if oc < 2:
                        for s, n in prev_free:
                            te.wait_ge(s.h, n)
                    if oc >= 2 and 's' in parts:
                        te.wait_ge(sS.h, evac_s[oc - 2])
                    if oc >= 2 and 'e' in parts:
                        te.wait_ge(sV.h, evac_v[oc - 2])
                    for ec in range(NFC):
                        inst = te.matmul(
                            bk[:, :],
                            w_sb[:, ec * 1024 + oc * 128: ec * 1024 + oc * 128 + 128],
                            rhs_bf[:, ec * 512:(ec + 1) * 512],
                            start=(ec == 0), stop=(ec == NFC - 1),
                            skip_group_check=True)
                    sPE.attach(inst)
                DT.add(_)
                mmm.append(m)

                if 's' in parts:
                    ms = sS.reserve(1)

                    def _(sc, oc=oc, bk=bk, s_=s_, m=m):
                        sc.wait_ge(sPE.h, m)
                        if 'e' in parts:
                            if mask is None:
                                if oc >= 2:
                                    sc.wait_ge(sV.h, evac_v[oc - 2])
                            else:
                                if oc >= 1:
                                    sc.wait_ge(sV.h, evac_v[oc - 1])
                        sS.attach(sc.activation(e1s[:, s_:s_ + 512], bk[:, :],
                                                AF.Exp))
                    DS.add(_)
                    evac_s.append(ms)
                else:
                    evac_s.append(0)

                if 'e' in parts:
                    mv = sV.reserve(1)

                    def _(v, oc=oc, bk=bk, s_=s_, o_=o_, m=m, ms=evac_s[-1]):
                        v.wait_ge(sPE.h, m)
                        if ms:
                            v.wait_ge(sS.h, ms)
                        if mask is None:
                            v.tensor_scalar_max(rs[:, s_:s_ + 512], bk[:, :],
                                                0.0)
                            sV.attach(v.scalar_tensor_tensor(
                                dst_bf[:, oc * 512:(oc + 1) * 512],
                                e1s[:, s_:s_ + 512], 1.0, rs[:, s_:s_ + 512],
                                OP.min, OP.add))
                        else:
                            v.tensor_scalar(rs[:, s_:s_ + 512], bk[:, :], 0.0,
                                            mask, OP.max, OP.mult)
                            v.tensor_scalar(rs[:, o_:o_ + 512],
                                            e1s[:, s_:s_ + 512], mask, mask,
                                            OP.mult, OP.min)
                            sV.attach(v.tensor_add(
                                dst_bf[:, oc * 512:(oc + 1) * 512],
                                rs[:, o_:o_ + 512], rs[:, s_:s_ + 512]))
                    DV.add(_)
                    evac_v.append(mv)
                else:
                    evac_v.append(0)
            return evac_v, mmm, evac_s

        with nc.Block() as blk:
            def _(sy):
                dW.inc(sy.dma_start(ident_f[:, :], ident_h.ap()[:, :]), 16)
                dW.inc(sy.dma_start(maskd[:, :], maskd_h.ap()[:, :]), 16)
                dW.inc(sy.dma_start(bo_sb[:, :], bo_h.ap()[:, :]), 16)
                dW.inc(sy.dma_start(gm_sb[:, :], gm_h.ap()[:, :]), 16)
                dW.inc(sy.dma_start(r0[0:1, 0:1], temp_h.ap()[:, :]), 16)
                dW.inc(sy.dma_start(e1s[:, 0:16], cT_h.ap()[:, :]), 16)
                dW.inc(sy.dma_start(wstg[0:16, 0:1024], et_h.ap()[:, :]), 16)
                src = xpT.ap().rearrange("(c p) t -> p c t", p=128)
                dst = x0[:, :].rearrange("p (c t) -> p c t", c=NFC)
                dX.inc(sy.dma_start(dst, src[:, :, 0:P]), 16)
            blk.sync(_)
            w0_ld = dW.n

            def _(v):
                v.memset(ones_cf[:, :], 1.0)
                v.memset(ones_cb[:, :], 1.0)
                v.memset(ones_rf[:, :], 1.0)
                v.memset(ones_rb[:, :], 1.0)
                v.memset(ones_mb[:, :], 1.0 / 1024.0)
                v.memset(S_sb[:, :], 0.0)
                v.memset(z_sb[:, :], 0.0)
                v.memset(Spre_sb[:, :], 0.0)
                v.memset(zpre[:, :], 0.0)
                v0r = V0[:, :].rearrange("p (j h c) -> p j h c", j=4, h=16)
                sV.inc(v.memset(v0r[:, :, :, 64:65], 1.0))
            blk.vector(_)

            def _(g):
                g.wait_ge(dW.h, w0_ld)
                g.tensor_copy(ident_b[:, :], ident_f[:, :])
                sPL.inc(g.tensor_copy(et_b[:, :], wstg[0:16, 0:1024]))
            blk.gpsimd(_)
            wstate["free_pl"] = sPL.n

            def _(sc):
                sc.wait_ge(dW.h, w0_ld)
                sS.inc(sc.activation(r1[0:1, 0:1], r0[0:1, 0:1], AF.Exp))
                sS.inc(sc.activation(rs[:, 0:16], e1s[:, 0:16], AF.Square))
            blk.scalar(_)
            m_expT = sS.n

            def _(v):
                v.wait_ge(sS.h, m_expT)
                sV.inc(v.tensor_scalar_mul(expT[:, :], r1[0:1, 0:1],
                                           INV_SQRT_G))
            blk.vector(_)

            def _(te):
                te.wait_ge(sS.h, m_expT)
                sPE.inc(te.matmul(bank[6][0:1, 0:16], ones_cf[:, :],
                                  rs[:, 0:16], start=True, stop=True,
                                  skip_group_check=True))
            blk.tensor(_)
            m_cssq = sPE.n

            def _(sc):
                sc.wait_ge(sPE.h, m_cssq)
                sS.inc(sc.activation(r1[0:1, 0:16], bank[6][0:1, 0:16],
                                     AF.Ln))
                sc.wait_ge(sS.h, sS.n)
                sS.inc(sc.activation(r2[0:1, 0:16], r1[0:1, 0:16], AF.Exp,
                                     scale=-0.5))
            blk.scalar(_)
            m_crn = sS.n

            def _(v):
                v.wait_ge(sS.h, m_crn)
                sV.inc(v.tensor_scalar_min(r3[0:1, 0:16], r2[0:1, 0:16],
                                           1e12))
            blk.vector(_)
            m_crn2 = sV.n

            def _(te):
                te.wait_ge(sV.h, m_crn2)
                sPE.inc(te.matmul(bank[6][:, 0:16], ones_rf[0:1, :],
                                  r3[0:1, 0:16], start=True, stop=True,
                                  skip_group_check=True))
            blk.tensor(_)
            m_cbc = sPE.n

            def _(v):
                v.wait_ge(sPE.h, m_cbc)
                sV.inc(v.tensor_copy(rs[:, 16:32], bank[6][:, 0:16]))
                v.wait_ge(sV.h, sV.n)
                sV.inc(v.tensor_mul(cnT[:, :], e1s[:, 0:16], rs[:, 16:32]))
            blk.vector(_)
            lnst["b67_free_v"] = sV.n

            wkv_v = 0
            for w_dst, w_src in ((wk_b, wk_h), (wv_b, wv_h)):
                for half in range(2):
                    fpl = wstate["free_pl"]
                    fv = wkv_v

                    def _(sy, w_src=w_src, half=half, fpl=fpl, fv=fv):
                        sy.wait_ge(sPL.h, fpl)
                        if fv:
                            sy.wait_ge(sV.h, fv)
                        dW.inc(sy.dma_start(
                            wstg[:, :],
                            w_src.ap()[:, half * 4096:half * 4096 + 4096]), 16)
                    blk.sync(_)
                    ldm = dW.n

                    def _(g, w_dst=w_dst, half=half, ldm=ldm):
                        g.wait_ge(dW.h, ldm)
                        sPL.inc(g.tensor_copy(
                            w_dst[:, half * 4096:half * 4096 + 2048],
                            wstg[:, 0:2048]))
                    blk.gpsimd(_)
                    wstate["free_pl"] = sPL.n

                    def _(v, w_dst=w_dst, half=half, ldm=ldm):
                        v.wait_ge(dW.h, ldm)
                        sV.inc(v.tensor_copy(
                            w_dst[:, half * 4096 + 2048:half * 4096 + 4096],
                            wstg[:, 2048:4096]))
                    blk.vector(_)
                    wkv_v = sV.n
        wk_ready = sPL.n
        wk_ready_v = wkv_v

        x_marks = {"pe": 0, "s": 0, "v": 0, "pl": 0}
        st_marks = {"ln1": 0, "phk": 0, "v": 0}
        hoist_ln = [None]
        proj01_free = []
        prev_supd = [0]
        prev_trev = [0]
        if stop_after == "W0":
            n_a_iters = 0
        elif isinstance(stop_after, str) and stop_after.startswith("A") and \
                len(stop_after) > 1 and stop_after[1:].isdigit():
            n_a_iters = int(stop_after[1:])
        else:
            n_a_iters = 2 * NB
        if n_a_iters:
            blkA = nc.Block()
            blkA.__enter__()
        for it in range(n_a_iters):
            prefix = it < NB
            b = it % NB
            if True:
                blk = blkA
                xr = [(dX, 16 * (it + 1))]
                ln_mark = emit_ln_stats(x0, False, xr)

                stg = dict(st_marks)
                ln1_v = sV.reserve(4)

                def _(v, stg=stg, ln_mark=ln_mark):
                    v.wait_ge(sV.h, ln_mark)
                    if stg["ln1"]:
                        v.wait_ge(dSC.h, stg["ln1"])
                    for fc in range(0, NFC, 2):
                        v.tensor_mul(e1s[:, 0:512],
                                     x0[:, fc * 512:(fc + 1) * 512], rbc[:, :])
                        sV.attach(v.tensor_add(
                            ln1_b[:, fc * 512:(fc + 1) * 512],
                            e1s[:, 0:512], cbc[:, :]))
                DV.add(_)

                ln1_pl = sPL.reserve(4)

                def _(g, stg=stg, ln_mark=ln_mark):
                    g.wait_ge(sV.h, ln_mark)
                    if stg["ln1"]:
                        g.wait_ge(dSC.h, stg["ln1"])
                    for fc in range(1, NFC, 2):
                        g.tensor_mul(rs[:, 0:512],
                                     x0[:, fc * 512:(fc + 1) * 512], rbc[:, :])
                        sPL.attach(g.tensor_add(
                            ln1_b[:, fc * 512:(fc + 1) * 512],
                            rs[:, 0:512], cbc[:, :]))
                DP.add(_)
                ln1_ready = [(sV, ln1_v), (sPL, ln1_pl)]
                x_marks = {"pe": sPE.n, "s": sS.n, "v": ln1_v, "pl": ln1_pl}

                if it + 1 < 2 * NB:
                    def _(sy, it=it, xm=dict(x_marks)):
                        sy.wait_ge(sPE.h, xm["pe"])
                        sy.wait_ge(sS.h, xm["s"])
                        sy.wait_ge(sV.h, xm["v"])
                        sy.wait_ge(sPL.h, xm["pl"])
                        srct = xpT if (it + 1) < NB else xT
                        nb = (it + 1) % NB
                        src = srct.ap().rearrange("(c p) t -> p c t", p=128)
                        dst = x0[:, :].rearrange("p (c t) -> p c t", c=NFC)
                        dX.inc(sy.dma_start(dst,
                                            src[:, :, nb * P:(nb + 1) * P]),
                               16)
                    blk.sync(_)

                if 'k' not in a_parts:
                    flush(blk)
                    continue
                prev_free = [(sPL, wk_ready), (sV, wk_ready_v)] + \
                    list(proj01_free)
                if st_marks["phk"]:
                    prev_free = prev_free + [(dSC, st_marks["phk"])]
                kp = 'm'
                if '1' not in a_parts:
                    kp += 's'
                if '1' not in a_parts and '2' not in a_parts:
                    kp += 'e'
                phk_marks, _km, phk_es = emit_elu_proj(
                    wk_b, phk_f, ln1_b, ln1_ready, prev_free,
                    mask=(gm_sb[:, :] if (prefix and 'M' not in a_parts)
                          else None), parts=kp)
                if phk_es and phk_es[-1]:
                    proj01_free[:] = [(sS, phk_es[-1]),
                                      (sV, phk_marks[-1])]

                if 'v' not in a_parts:
                    flush(blk)
                    continue
                vevac = []
                vmm = []
                stv = st_marks["v"]
                gi = 0
                for jc in range(NJC):
                    for half in range(2):
                        bk = bank[2 + gi % 2]
                        m = sPE.reserve(1)

                        def _(te, jc=jc, half=half, bk=bk, gi=gi, stv=stv,
                              psu=prev_supd[0]):
                            if gi == 0:
                                for s, n in ln1_ready:
                                    te.wait_ge(s.h, n)
                                te.wait_ge(sPL.h, wk_ready)
                                te.wait_ge(sV.h, wk_ready_v)
                                if stv:
                                    te.wait_ge(dSC.h, stv)
                            if gi < 2 and psu:
                                te.wait_ge(sV.h, psu)
                            if gi >= 2:
                                te.wait_ge(sV.h, vevac[gi - 2])
                            for ec in range(NFC):
                                inst = te.matmul(
                                    bk[:, :],
                                    ln1_b[:, ec * 512 + jc * 128: ec * 512 + jc * 128 + 128],
                                    wv_b[:, ec * 1024 + half * 512: ec * 1024 + half * 512 + 512],
                                    start=(ec == 0), stop=(ec == NFC - 1),
                                    skip_group_check=True)
                            sPE.attach(inst)
                        DT.add(_)
                        vmm.append(m)

                        mv = sV.reserve(1)

                        def _(v, jc=jc, half=half, bk=bk, m=m):
                            v.wait_ge(sPE.h, m)
                            v0r = V0[:, :].rearrange("p (j h c) -> p j h c",
                                                     j=4, h=16)
                            dst = v0r[:, jc, half * 8:half * 8 + 8, 0:64]
                            sV.attach(v.tensor_copy(
                                dst, bk[:, :].rearrange("p (h d) -> p h d",
                                                        h=8)))
                        DV.add(_)
                        vevac.append(mv)
                        gi += 1
                v_done = sV.n

                if 't' not in a_parts:
                    flush(blk)
                    continue
                tr_m = []
                tr_ev = []
                gi = 0
                for jc in range(NJC):
                    for half in range(2):
                        bk = bank[4 + gi % 2]
                        m = sPE.reserve(1)

                        def _(te, jc=jc, half=half, bk=bk, gi=gi,
                              ptr=prev_trev[0]):
                            if gi < 2 and ptr:
                                te.wait_ge(sS.h, ptr)
                            if gi >= 2:
                                te.wait_ge(sS.h, tr_ev[gi - 2])
                            for k in range(4):
                                oc = half * 4 + k
                                te.wait_ge(sV.h, phk_marks[oc])
                                inst = te.transpose(
                                    bk[:, :].bitcast(BF16)[:, k * 128:(k + 1) * 128],
                                    phk_f[:, oc * 512 + jc * 128: oc * 512 + jc * 128 + 128],
                                    ident_b[:, :])
                            sPE.attach(inst)
                        DT.add(_)
                        tr_m.append(m)

                        ms = sS.reserve(1)

                        def _(sc, jc=jc, half=half, bk=bk, gi=gi):
                            sc.wait_ge(sPE.h, tr_m[gi])
                            sS.attach(sc.copy(
                                phk_t[:, jc * 1024 + half * 512: jc * 1024 + half * 512 + 512],
                                bk[:, :].bitcast(BF16)[:, 0:512]))
                        DS.add(_)
                        tr_ev.append(ms)
                        gi += 1
                trdone = sS.n

                if 's' not in a_parts:
                    flush(blk)
                    continue
                ds_mark = sPE.reserve(1)

                def _(te, v_done=v_done, trdone=trdone):
                    te.wait_ge(sV.h, v_done)
                    te.wait_ge(sS.h, trdone)
                    for hh in range(H):
                        for jc in range(NJC):
                            inst = te.matmul(
                                bank[2][(hh % 2) * 64:(hh % 2) * 64 + 64,
                                        (hh // 2) * 64:(hh // 2) * 64 + 64],
                                phk_t[:, jc * 1024 + (hh // 2) * 128 + (hh % 2) * 64:
                                      jc * 1024 + (hh // 2) * 128 + (hh % 2) * 64 + 64],
                                V0[:, jc * 1040 + hh * 65: jc * 1040 + hh * 65 + 64],
                                start=(jc == 0), stop=(jc == NJC - 1),
                                skip_group_check=True)
                    sPE.attach(inst)
                DT.add(_)

                s_upd = sV.reserve(NFC + 1)

                def _(v, b=b, prefix=prefix, ds_mark=ds_mark,
                      base=s_upd - NFC - 1):
                    Sdst = Spre_sb if prefix else S_sb
                    zdst = zpre if prefix else z_sb
                    if not prefix:
                        v.tensor_copy(ckS[:, b * 512:(b + 1) * 512],
                                      S_sb[:, :])
                        v.tensor_copy(ckz[:, b * 8:(b + 1) * 8], z_sb[:, :])
                    for fc in range(NFC):
                        v.wait_ge(sV.h, phk_marks[fc])
                        sV.attach(v.reduce_sum(
                            rs[:, 0:1], phk_f[:, fc * 512:(fc + 1) * 512],
                            axis=AX))
                        v.wait_ge(sV.h, base + fc + 1)
                        v.tensor_add(zdst[:, fc:fc + 1], zdst[:, fc:fc + 1],
                                     rs[:, 0:1])
                    v.wait_ge(sPE.h, ds_mark)
                    sV.attach(v.tensor_add(Sdst[:, :], Sdst[:, :],
                                           bank[2][:, :]))
                DV.add(_)

                if not prefix:
                    def _(sy, b=b, lv=ln1_v, lpl=ln1_pl, pm=phk_marks[-1],
                          vd=v_done):
                        sy.wait_ge(sV.h, lv)
                        sy.wait_ge(sPL.h, lpl)
                        dSC.inc(sy.dma_start(ln1_scr.ap()[b], ln1_b[:, :]),
                                16)
                        sy.wait_ge(sV.h, pm)
                        dSC.inc(sy.dma_start(phk_scr.ap()[b], phk_f[:, :]),
                                16)
                        sy.wait_ge(sV.h, vd)
                        dSC.inc(sy.dma_start(v_scr.ap()[b], V0[:, :]), 16)
                        st_marks["ln1"] = dSC.n
                        st_marks["phk"] = dSC.n
                        st_marks["v"] = dSC.n
                    blk.sync(_)

                emit_wjobs(blk, 2 if it > 0 else 1)
                prev_supd[0] = s_upd
                prev_trev[0] = trdone
                flush(blk)
        if n_a_iters:
            blkA.__exit__(None, None, None)

        spre_mark = sV.n
        with nc.Block() as blk:
            emit_wjobs(blk, 99)

            def _(sy):
                sy.wait_ge(dWS.h, dWS.n)
                sy.wait_ge(dW.h, dW.n)
                src = xT.ap().rearrange("(c p) t -> p c t", p=128)
                dst = x0[:, :].rearrange("p (c t) -> p c t", c=NFC)
                dX.inc(sy.dma_start(dst, src[:, :, 0:P]), 16)
                sy.wait_ge(dSC.h, dSC.n)
                dSL.inc(sy.dma_start(ln1_b[:, :], ln1_scr.ap()[0]), 16)
                dSL.inc(sy.dma_start(phk_f[:, :], phk_scr.ap()[0]), 16)
                dSL.inc(sy.dma_start(V0[:, :], v_scr.ap()[0]), 16)
            blk.sync(_)
            flush(blk)
        wcast_all = sPL.n

        stAk.close()
        stAq = ExitStack()
        x1 = sb(stAq, "x1", [128, 4096], F32)
        phq = sb(stAq, "phq", [128, 4096], BF16)
        A_sb0 = sb(stAq, "A_sb0", [128, 2048], BF16)
        A_sb1 = sb(stAq, "A_sb1", [128, 2048], BF16)
        u_sb = sb(stAq, "u_sb", [128, 4096], BF16)
        rden_c = sb(stAq, "rden_c", [1, 512], BF16)
        yring2 = sb(stAq, "yring2", [128, 1024], F32)
        ao = sb(stAq, "ao", [128, 4096], F32)
        rdbc = sb(stAq, "rdbc", [64, 512], F32)
        S65 = sb(stAq, "S65", [128, 520], BF16)

        aq_x = {0: dX, 1: dX2}
        aq_prev_pairs = [0, 0]
        aq_x_cnt = {0: dX.n, 1: dX2.n}
        yst_marks = []
        prev_u15 = [0]
        prev_u14 = [0]
        prev_rdbc15 = [0]
        prev_rd15 = [0]
        prev_num15 = [0]
        prev_wo = [0]
        prev_ypl = [0]
        abank_hist = {2: [], 3: []}
        if stop_after in ("W0", "A") or (
                isinstance(stop_after, str) and stop_after.startswith("A")
                and len(stop_after) > 1):
            n_q_iters = 0
        else:
            n_q_iters = NB
        if n_q_iters:
            blkQ = nc.Block()
            blkQ.__enter__()
        for b in range(n_q_iters):
            if True:
                blk = blkQ
                xbuf = x0 if b % 2 == 0 else x1
                xw_sem = aq_x[b % 2]
                xw_n = aq_x_cnt[b % 2]

                sl_base = 48 * b + 48

                if b + 1 < NB:
                    nsem = aq_x[(b + 1) % 2]
                    nsem.n += 16
                    aq_x_cnt[(b + 1) % 2] = nsem.n

                    def _(sy, b=b, nsem=nsem, gv=sV.n, gp=sPL.n):
                        if gv:
                            sy.wait_ge(sV.h, gv)
                        if gp:
                            sy.wait_ge(sPL.h, gp)
                        src = xT.ap().rearrange("(c p) t -> p c t", p=128)
                        dstb = (x1 if (b + 1) % 2 else x0)
                        dst = dstb[:, :].rearrange("p (c t) -> p c t", c=NFC)
                        nsem.attach(sy.dma_start(
                            dst, src[:, :, (b + 1) * P:(b + 2) * P]), 16)
                    blk.sync(_)

                s65_mark = sV.reserve(3)

                def _(v, b=b, base=s65_mark - 3, pn=prev_num15[0]):
                    v.wait_ge(sV.h, spre_mark)
                    if pn:
                        v.wait_ge(sPE.h, pn)
                    s65r = S65[:, :].rearrange("p (h c) -> p h c", h=8)
                    sV.attach(v.tensor_add(e1s[:, 0:512],
                                           ckS[:, b * 512:(b + 1) * 512],
                                           Spre_sb[:, :]))
                    v.wait_ge(sV.h, base + 1)
                    v.tensor_copy(s65r[:, :, 0:64],
                                  e1s[:, 0:512].rearrange("p (h d) -> p h d",
                                                          h=8))
                    sV.attach(v.tensor_add(rs[:, 0:8],
                                           ckz[:, b * 8:(b + 1) * 8],
                                           zpre[:, :]))
                    v.wait_ge(sV.h, base + 2)
                    sV.attach(v.tensor_copy(
                        s65r[:, :, 64:65],
                        rs[:, 0:8].rearrange("p (h o) -> p h o", o=1)))
                DV.add(_)

                phq_marks, _qm, phq_es = emit_elu_proj(
                    wq_b, phq, ln1_b, [(dSL, sl_base), (sPL, sPL.n)],
                    [(sV, s65_mark)] + proj01_free)
                proj01_free[:] = [(sS, phq_es[-1]), (sV, phq_marks[-1])]
                qproj_pe = sPE.n

                if b + 1 < NB:
                    def _(sy, qp=qproj_pe, b=b):
                        sy.wait_ge(sPE.h, qp)
                        dSL.inc(sy.dma_start(ln1_b[:, :],
                                             ln1_scr.ap()[b + 1]), 16)
                    blk.sync(_)

                b67g = lnst["b67_free_v"]
                amm = {}
                aev_v = {}
                aev_s = {}
                num_mark = {}
                mda_mark = {}
                bc_mark = {}
                rd_mark = {}
                rdbc_ev = {}
                u_mark = {}

                def emit_mda(g):
                    m = sV.reserve(1)

                    def _(v, g=g, pr=prev_rd15[0]):
                        nbk = bank[4] if g % 2 == 0 else bank[6]
                        v.wait_ge(sPE.h, num_mark[g])
                        if g >= 1:
                            v.wait_ge(sS.h, rd_mark[g - 1])
                        elif pr:
                            v.wait_ge(sS.h, pr)
                        sV.attach(v.tensor_scalar_add(
                            r0[:, :], nbk[64:65, 0:512], ATT_EPS))
                    DV.add(_)
                    mda_mark[g] = m

                def emit_den_s(g):
                    md = sS.reserve(2)

                    def _(sc, g=g, base=md - 2):
                        sc.wait_ge(sV.h, mda_mark[g])
                        sS.attach(sc.activation(r1[:, :], r0[:, :], AF.Ln))
                        sc.wait_ge(sS.h, base + 1)
                        if g >= 2:
                            sc.wait_ge(sPE.h, bc_mark[g - 2])
                        sS.attach(sc.activation(
                            (rden_b if g % 2 == 0 else rden_c)[:, :],
                            r1[:, :], AF.Exp, scale=-1.0))
                    DS.add(_)
                    rd_mark[g] = md

                def emit_stage2a(g):
                    mb = sPE.reserve(1)

                    def _(te, g=g, prb=prev_rdbc15[0]):
                        te.wait_ge(sS.h, rd_mark[g])
                        if g >= 1:
                            te.wait_ge(sS.h, rdbc_ev[g - 1])
                        elif prb:
                            te.wait_ge(sS.h, prb)
                        rdn = rden_b if g % 2 == 0 else rden_c
                        sPE.attach(te.matmul(
                            bank[5][0:64, :], ones_rb[0:1, 0:64],
                            rdn[0:1, :],
                            start=True, stop=True, skip_group_check=True))
                    DT.add(_)
                    bc_mark[g] = mb

                    ms = sS.reserve(1)

                    def _(sc, g=g, mb=mb):
                        sc.wait_ge(sPE.h, mb)
                        if g >= 1:
                            sc.wait_ge(sV.h, u_mark[g - 1])
                        sS.attach(sc.copy(rdbc[:, :], bank[5][0:64, :]))
                    DS.add(_)
                    rdbc_ev[g] = ms

                def emit_stage2b(g):
                    mu = sV.reserve(1)

                    def _(v, g=g, ms=rdbc_ev[g]):
                        nbk = bank[4] if g % 2 == 0 else bank[6]
                        v.wait_ge(sS.h, ms)
                        sV.attach(v.tensor_mul(
                            u_sb[(g % 2) * 64:(g % 2) * 64 + 64,
                                 (g // 2) * 512:(g // 2) * 512 + 512],
                            nbk[0:64, 0:512], rdbc[:, :]))
                    DV.add(_)
                    u_mark[g] = mu

                for h in range(H):
                    if h >= 2:
                        emit_stage2a(h - 2)
                    for jc in range(NJC):
                        bki = 2 + jc % 2
                        hist = abank_hist[bki][-1] if abank_hist[bki] else None
                        m = sPE.reserve(1)

                        def _(te, h=h, jc=jc, bki=bki, hist=hist, b=b):
                            qoff = (h % 2) * 64
                            coff = (h // 2) * 512
                            if h == 0 and jc == 0:
                                te.wait_ge(dSL.h, 48 * b + 48)
                            if hist:
                                te.wait_ge(sV.h, hist[0])
                                te.wait_ge(sS.h, hist[1])
                            te.wait_ge(sV.h, phq_marks[h // 2])
                            sPE.attach(te.matmul(
                                bank[bki][:, jc * 128:512],
                                phk_f[qoff:qoff + 64, coff + jc * 128: coff + jc * 128 + 128],
                                phq[qoff:qoff + 64, coff + jc * 128: coff + 512],
                                start=True, stop=True, skip_group_check=True))
                        DT.add(_)
                        amm[(h, jc)] = m

                        if jc < NJC - 1:
                            ms = sS.reserve(1)

                            def _(sc, h=h, jc=jc, bki=bki, m=m):
                                Ab = A_sb0 if h % 2 == 0 else A_sb1
                                sc.wait_ge(sPE.h, m)
                                if h >= 2:
                                    sc.wait_ge(sPE.h, num_mark[h - 2])
                                sS.attach(sc.copy(
                                    Ab[:, jc * 512 + (jc + 1) * 128: jc * 512 + 512],
                                    bank[bki][:, (jc + 1) * 128:512]))
                            DS.add(_)
                            aev_s[(h, jc)] = ms
                        else:
                            aev_s[(h, jc)] = 0

                        mv = sV.reserve(1)

                        def _(v, h=h, jc=jc, bki=bki, m=m,
                              ms=aev_s[(h, jc)]):
                            Ab = A_sb0 if h % 2 == 0 else A_sb1
                            v.wait_ge(sPE.h, m)
                            if ms:
                                v.wait_ge(sS.h, ms)
                            if h >= 2:
                                v.wait_ge(sPE.h, num_mark[h - 2])
                            sV.attach(v.tensor_mul(
                                Ab[:, jc * 512 + jc * 128: jc * 512 + jc * 128 + 128],
                                bank[bki][:, jc * 128:jc * 128 + 128],
                                maskd[:, :]))
                        DV.add(_)
                        aev_v[(h, jc)] = mv
                        abank_hist[bki].append((aev_v[(h, jc)],
                                                aev_s[(h, jc)]))
                        if jc == 0 and h >= 2:
                            emit_stage2b(h - 2)
                        if jc == 1 and h >= 1:
                            emit_mda(h - 1)

                    mn = sPE.reserve(1)

                    def _(te, h=h, b=b, pu14=prev_u14[0],
                          pu15=prev_u15[0], b67g=b67g):
                        qoff = (h % 2) * 64
                        nbk = bank[4] if h % 2 == 0 else bank[6]
                        Ab = A_sb0 if h % 2 == 0 else A_sb1
                        if h == 0:
                            te.wait_ge(dSL.h, 48 * b + 48)
                            if pu14:
                                te.wait_ge(sV.h, pu14)
                        if h == 1:
                            if pu15:
                                te.wait_ge(sV.h, pu15)
                            if b67g:
                                te.wait_ge(sV.h, b67g)
                        if h >= 2:
                            te.wait_ge(sV.h, u_mark[h - 2])
                        te.wait_ge(sV.h, s65_mark)
                        for jc in range(NJC):
                            te.wait_ge(sV.h, aev_v[(h, jc)])
                            if jc < NJC - 1:
                                te.wait_ge(sS.h, aev_s[(h, jc)])
                            te.matmul(
                                nbk[0:65, jc * 128:512],
                                V0[:, jc * 1040 + h * 65: jc * 1040 + h * 65 + 65],
                                Ab[:, jc * 512 + jc * 128: jc * 512 + 512],
                                start=(jc == 0), stop=False,
                                skip_group_check=True)
                        sPE.attach(te.matmul(
                            nbk[0:65, 0:512],
                            S65[qoff:qoff + 64, (h // 2) * 65:(h // 2) * 65 + 65],
                            phq[qoff:qoff + 64, (h // 2) * 512:(h // 2) * 512 + 512],
                            start=False, stop=True, skip_group_check=True))
                    DT.add(_)
                    num_mark[h] = mn
                    if h >= 1:
                        emit_den_s(h - 1)

                emit_mda(H - 1)
                emit_den_s(H - 1)
                emit_stage2a(H - 2)
                emit_stage2b(H - 2)
                emit_stage2a(H - 1)
                emit_stage2b(H - 1)
                u_last = sV.n
                last_A_pe = sPE.n

                if b + 1 < NB:
                    def _(sy, b=b, lastA=last_A_pe):
                        sy.wait_ge(sPE.h, lastA)
                        dSL.inc(sy.dma_start(phk_f[:, :],
                                             phk_scr.ap()[b + 1]), 16)
                        dSL.inc(sy.dma_start(V0[:, :], v_scr.ap()[b + 1]), 16)
                    blk.sync(_)

                wo_ev = []
                for oc in range(NFC):
                    bk = bank[oc % 2]
                    m = sPE.reserve(1)

                    def _(te, oc=oc, bk=bk, u_last=u_last,
                          pyl=prev_ypl[0]):
                        if oc == 0:
                            te.wait_ge(sV.h, u_last)
                            te.wait_ge(sS.h, rdbc_ev[H - 1])
                            te.wait_ge(sPL.h, wcast_all)
                            if pyl:
                                te.wait_ge(sPL.h, pyl)
                        if oc >= 2:
                            te.wait_ge(sS.h, wo_ev[oc - 2])
                        for fc in range(NFC):
                            inst = te.matmul(
                                bk[:, :],
                                wo_b[:, fc * 1024 + oc * 128: fc * 1024 + oc * 128 + 128],
                                u_sb[:, fc * 512:(fc + 1) * 512],
                                start=(fc == 0), stop=(fc == NFC - 1),
                                skip_group_check=True)
                        sPE.attach(inst)
                    DT.add(_)

                    ms = sS.reserve(1)

                    def _(sc, oc=oc, bk=bk, m=m):
                        sc.wait_ge(sPE.h, m)
                        sS.attach(sc.activation(
                            ao[:, oc * 512:(oc + 1) * 512], bk[:, :],
                            AF.Identity, bias=bo_sb[:, oc:oc + 1]))
                    DS.add(_)
                    wo_ev.append(ms)
                proj01_free[:] = [(sS, wo_ev[-1])]
                ao_ready = [(sS, sS.n)]

                if dbg:
                    def _(sy, b=b, am=sS.n, um=u_last):
                        sy.wait_ge(sS.h, am)
                        dW.inc(sy.dma_start(dbg_ao.ap()[b], ao[:, :]), 16)
                        sy.wait_ge(sV.h, um)
                        dW.inc(sy.dma_start(dbg_u.ap()[b], u_sb[:, :]), 16)
                        sy.wait_ge(dW.h, dW.n)
                    blk.sync(_)

                lnao_mark = emit_ln_stats(ao, False, ao_ready)
                loc_pairs = []
                for fc in range(NFC):
                    p = fc // 2
                    rguard = (loc_pairs[p - 2] if p >= 2 else
                              aq_prev_pairs[p])
                    ring = yring if p % 2 == 0 else yring2
                    if fc % 2 == 0:
                        mleft = sV.reserve(1)

                        def _(v, fc=fc, rg=rguard, lnm=lnao_mark, xw=xw_n,
                              xbuf=xbuf, xs=xw_sem, ring=ring):
                            if fc == 0:
                                v.wait_ge(sV.h, lnm)
                                v.wait_ge(xs.h, xw)
                            if rg:
                                v.wait_ge(dY.h, rg)
                            v.tensor_mul(e1s[:, 0:512],
                                         ao[:, fc * 512:(fc + 1) * 512],
                                         rbc[:, :])
                            v.tensor_add(e1s[:, 512:1024], e1s[:, 0:512],
                                         cbc[:, :])
                            sV.attach(v.tensor_add(
                                ring[:, 0:512], e1s[:, 512:1024],
                                xbuf[:, fc * 512:(fc + 1) * 512]))
                        DV.add(_)
                    else:
                        mright = sPL.reserve(1)

                        def _(g, fc=fc, rg=rguard, lnm=lnao_mark, xw=xw_n,
                              xbuf=xbuf, xs=xw_sem, ring=ring):
                            if fc == 1:
                                g.wait_ge(sV.h, lnm)
                                g.wait_ge(xs.h, xw)
                            if rg:
                                g.wait_ge(dY.h, rg)
                            g.tensor_mul(rs[:, 0:512],
                                         ao[:, fc * 512:(fc + 1) * 512],
                                         rbc[:, :])
                            g.tensor_add(rs[:, 512:1024], rs[:, 0:512],
                                         cbc[:, :])
                            sPL.attach(g.tensor_add(
                                ring[:, 512:1024], rs[:, 512:1024],
                                xbuf[:, fc * 512:(fc + 1) * 512]))
                        DP.add(_)

                        def _(sy, fc=fc, b=b, ml=mleft, mr=mright,
                              ring=ring):
                            sy.wait_ge(sV.h, ml)
                            sy.wait_ge(sPL.h, mr)
                            dst = y_scr.ap()[b].rearrange(
                                "p (c t) -> p c t", c=NFC)[:, fc - 1:fc + 1, :]
                            src = ring[:, :].rearrange("p (c t) -> p c t",
                                                       c=2)
                            dY.inc(sy.dma_start(dst, src), 16)
                        blk.sync(_)
                        loc_pairs.append(dY.n)
                aq_prev_pairs[:] = [loc_pairs[2], loc_pairs[3]]
                yst_marks.append(dY.n)
                prev_u15[0] = u_mark[H - 1]
                prev_u14[0] = u_mark[H - 2]
                prev_rdbc15[0] = rdbc_ev[H - 1]
                prev_rd15[0] = rd_mark[H - 1]
                prev_num15[0] = num_mark[H - 1]
                prev_ypl[0] = sPL.n
                flush(blk)
        if n_q_iters:
            blkQ.__exit__(None, None, None)

        stAq.close()
        stA.close()
        stB = ExitStack()
        win_b = sb(stB, "win_b", [128, 65536], BF16)
        y0 = sb(stB, "y0", [128, 4096], F32)
        ln2_b = sb(stB, "ln2_b", [128, 4096], BF16)
        hid0 = sb(stB, "hid0", [128, 2048], BF16)
        hid1 = sb(stB, "hid1", [128, 2048], BF16)
        G_ch = sb(stB, "G_ch", [128, 512], BF16)
        ffn_b = sb(stB, "ffn_b", [128, 4096], BF16)
        z_fp = sb(stB, "z_fp", [128, 512], F32)
        lgtm = sb(stB, "lgtm", [128, 64], F32)
        gtm_b = sb(stB, "gtm_b", [128, 64], BF16)
        gw0 = sb(stB, "gw0", [128, 16], F32)
        gw1 = sb(stB, "gw1", [128, 16], F32)
        gw2 = sb(stB, "gw2", [128, 16], F32)
        gc0 = sb(stB, "gc0", [128, 1], F32)
        gc1 = sb(stB, "gc1", [128, 1], F32)
        gfm_t = sb(stB, "gfm", [16, 512], BF16)
        gfm = gfm_t[:, :]
        lg0 = z_fp[0:16, :]
        lg1 = z_fp[32:48, :]
        bsc0 = hid0[:, :].bitcast(F32)
        bsc1 = hid1[:, :].bitcast(F32)

        with nc.Block() as blk:
            def _(sy):
                sy.wait_ge(dY.h, dY.n)
                dYL.inc(sy.dma_start(y0[:, :], y_scr.ap()[0]), 16)
                for hh in range(16):
                    dWB.inc(sy.dma_start(
                        win_b[:, hh * 4096:(hh + 1) * 4096],
                        wbf_scr.ap()[hh]), 16)
            blk.sync(_)
        yl_marks = [16]
        ost_marks = []
        wb_hist = []

        if stop_after in ("W0", "A", "AQ") or (
                isinstance(stop_after, str) and stop_after.startswith("A")
                and len(stop_after) > 1):
            n_b_iters = 0
        elif isinstance(stop_after, str) and stop_after.startswith("B"):
            n_b_iters = int(stop_after[1:])
        else:
            n_b_iters = NB
        prev_gel63 = [0]
        prev_ffn7 = [0]
        prev_gtev3 = [0]
        prev_zsq = [0]
        prev_ypl2 = [0]
        if n_b_iters:
            blkB = nc.Block()
            blkB.__enter__()
        for b in range(n_b_iters):
            if True:
                blk = blkB
                ylw = [(dYL, yl_marks[b])]
                ln_mark = emit_ln_stats(y0, False, ylw)

                ln2_v = sV.reserve(4)

                def _(v, lnm=ln_mark):
                    v.wait_ge(sV.h, lnm)
                    for fc in range(0, NFC, 2):
                        v.tensor_mul(bsc0[:, 0:512],
                                     y0[:, fc * 512:(fc + 1) * 512], rbc[:, :])
                        sV.attach(v.tensor_add(
                            ln2_b[:, fc * 512:(fc + 1) * 512],
                            bsc0[:, 0:512], cbc[:, :]))
                DV.add(_)

                ln2_pl = sPL.reserve(4)

                def _(g, lnm=ln_mark):
                    g.wait_ge(sV.h, lnm)
                    for fc in range(1, NFC, 2):
                        g.tensor_mul(bsc1[:, 0:512],
                                     y0[:, fc * 512:(fc + 1) * 512], rbc[:, :])
                        sPL.attach(g.tensor_add(
                            ln2_b[:, fc * 512:(fc + 1) * 512],
                            bsc1[:, 0:512], cbc[:, :]))
                DP.add(_)
                ln2_ready = [(sV, ln2_v), (sPL, ln2_pl)]

                RT = []
                rt_i = [0]
                SRT = []
                srt_i = [0]

                def rt_drain(k):
                    while k > 0 and rt_i[0] < len(RT):
                        DT.add(RT[rt_i[0]])
                        rt_i[0] += 1
                        k -= 1

                def srt_drain(k):
                    while k > 0 and srt_i[0] < len(SRT):
                        DS.add(SRT[srt_i[0]])
                        srt_i[0] += 1
                        k -= 1

                z_mm = sPR.reserve(1)

                def _(te):
                    for s, n in ln2_ready:
                        te.wait_ge(s.h, n)
                    for ec in range(NFC):
                        inst = te.matmul(
                            bank[4][:, :], wr_b[:, ec * 128:(ec + 1) * 128],
                            ln2_b[:, ec * 512:(ec + 1) * 512],
                            start=(ec == 0), stop=(ec == NFC - 1),
                            skip_group_check=True)
                    sPR.attach(inst)
                RT.append(_)

                z_ev = sV.reserve(1)

                def _(v, m=z_mm, pz=prev_zsq[0]):
                    v.wait_ge(sPR.h, m)
                    if pz:
                        v.wait_ge(sS.h, pz)
                    sV.attach(v.tensor_copy(z_fp[:, :], bank[4][:, :]))
                DV.add(_)

                zsq_m = sS.reserve(1)

                def _(sc, m=z_ev):
                    sc.wait_ge(sV.h, m)
                    sS.attach(sc.activation(sqring[:, 0:512], z_fp[:, :],
                                            AF.Square))
                DS.add(_)

                ssq_m = sPR.reserve(1)

                def _(te, m=zsq_m, pg=prev_gtev3[0]):
                    te.wait_ge(sS.h, m)
                    if pg:
                        te.wait_ge(sV.h, pg)
                    sPR.attach(te.matmul(bank[5][0:1, :], ones_cb[:, :],
                                         sqring[:, 0:512], start=True,
                                         stop=True, skip_group_check=True))
                RT.append(_)

                rn_m = sS.reserve(2)

                def _(sc, m=ssq_m, base=rn_m - 2):
                    sc.wait_ge(sPR.h, m)
                    sS.attach(sc.activation(r0[:, :], bank[5][0:1, :], AF.Ln))
                    sc.wait_ge(sS.h, base + 1)
                    sS.attach(sc.activation(r1[:, :], r0[:, :], AF.Exp,
                                            scale=-0.5))
                DS.add(_)

                rn2_m = sV.reserve(2)

                def _(v, m=rn_m, base=rn2_m - 2):
                    v.wait_ge(sS.h, m)
                    sV.attach(v.tensor_scalar_min(r3[:, :], r1[:, :], 1e12))
                    v.wait_ge(sV.h, base + 1)
                    sV.attach(v.tensor_scalar_mul(r2[:, :], r3[:, :],
                                                  expT[0:1, 0:1]))
                DV.add(_)

                lg_mm = sPR.reserve(1)

                def _(te, m=z_ev):
                    te.wait_ge(sV.h, m)
                    sPR.attach(te.matmul(bank[4][0:16, :], cnT[:, :],
                                         z_fp[:, :], start=True, stop=True,
                                         skip_group_check=True))
                RT.append(_)

                lg_ev = sV.reserve(2)

                def _(v, m=lg_mm, m2=rn2_m, base=lg_ev - 2, mz=zsq_m):
                    v.wait_ge(sPR.h, m)
                    v.wait_ge(sS.h, mz)
                    sV.attach(v.tensor_copy(lg0, bank[4][0:16, :]))
                    v.wait_ge(sV.h, m2)
                    sV.attach(v.tensor_copy(rden_b[:, 0:512], r2[:, :]))
                DV.add(_)

                rnbc_m = sPR.reserve(1)

                def _(te, m=lg_ev, mr=rn_m):
                    te.wait_ge(sV.h, m)
                    te.wait_ge(sS.h, mr)
                    sPR.attach(te.matmul(bank[5][0:16, :], ones_rb[0:1, 0:16],
                                         rden_b[0:1, 0:512], start=True,
                                         stop=True, skip_group_check=True))
                RT.append(_)

                lg1_m = sV.reserve(1)

                def _(v, m=rnbc_m):
                    v.wait_ge(sPR.h, m)
                    sV.attach(v.tensor_mul(lg1, lg0, bank[5][0:16, :]))
                DV.add(_)

                lt_ev = []
                for c in range(NJC):
                    tbk = bank[3 + 2 * (c % 2)]
                    m = sPR.reserve(1)

                    def _(te, c=c, tbk=tbk, lg1_m=lg1_m):
                        te.wait_ge(sV.h, lg1_m)
                        if c >= 2:
                            te.wait_ge(sV.h, lt_ev[c - 2])
                        sPR.attach(te.transpose(tbk[:, c * 16:(c + 1) * 16],
                                                lg1[:, c * 128:(c + 1) * 128],
                                                ident_f[32:48, 32:48]))
                    RT.append(_)

                    mv = sV.reserve(1)

                    def _(v, c=c, tbk=tbk, m=m):
                        v.wait_ge(sPR.h, m)
                        sV.attach(v.tensor_copy(lgtm[:, c * 16:(c + 1) * 16],
                                                tbk[:, c * 16:(c + 1) * 16]))
                    DV.add(_)
                    lt_ev.append(mv)

                sm_exp = []
                g_done = []
                for c in range(NJC):
                    sub_m = sV.reserve(2)

                    def _(v, c=c, base=sub_m - 2):
                        v.wait_ge(sV.h, lt_ev[c])
                        sV.attach(v.reduce_max(gc0[:, :],
                                               lgtm[:, c * 16:(c + 1) * 16],
                                               axis=AX))
                        v.wait_ge(sV.h, base + 1)
                        sV.attach(v.tensor_scalar(
                            gw0[:, :], lgtm[:, c * 16:(c + 1) * 16],
                            gc0[:, :], None, OP.subtract))
                    DV.add(_)

                    me = sSS.reserve(1)

                    def _(sc, c=c, m=sub_m):
                        sc.wait_ge(sV.h, m)
                        sSS.attach(sc.activation(gw1[:, :], gw0[:, :],
                                                 AF.Exp))
                    SRT.append(_)
                    sm_exp.append(me)

                    mg = sV.reserve(9)

                    def _(v, c=c, me=me, base=mg - 9):
                        k = [base]

                        def step(inst):
                            k[0] += 1
                            sV.attach(inst)
                            v.wait_ge(sV.h, k[0])
                        v.wait_ge(sSS.h, me)
                        step(v.reduce_sum(gc1[:, :], gw1[:, :], axis=AX))
                        step(v.reciprocal(gc0[:, :], gc1[:, :]))
                        step(v.tensor_scalar(gw0[:, :], gw1[:, :], gc0[:, :],
                                             None, OP.mult))
                        step(v.reduce_max(gc1[:, :], gw0[:, :], axis=AX))
                        step(v.tensor_scalar(gw1[:, :], gw0[:, :], gc1[:, :],
                                             None, OP.is_equal))
                        step(v.scalar_tensor_tensor(gw2[:, :], gw1[:, :],
                                                    -1e30, gw0[:, :],
                                                    OP.mult, OP.add))
                        step(v.reduce_max(gc1[:, :], gw2[:, :], axis=AX))
                        step(v.tensor_scalar(gw2[:, :], gw0[:, :], gc1[:, :],
                                             None, OP.is_ge))
                        sV.attach(v.tensor_mul(gtm_b[:, c * 16:(c + 1) * 16],
                                               gw0[:, :], gw2[:, :]))
                    DV.add(_)
                    g_done.append(mg)

                gt_ev = []
                for c in range(NJC):
                    tbk = bank[3 + 2 * (c % 2)]
                    m = sPR.reserve(1)

                    def _(te, c=c, tbk=tbk):
                        te.wait_ge(sV.h, g_done[c])
                        if c >= 2:
                            te.wait_ge(sV.h, gt_ev[c - 2])
                        sPR.attach(te.transpose(
                            tbk[:, :].bitcast(BF16)[0:16, c * 128:(c + 1) * 128],
                            gtm_b[:, c * 16:(c + 1) * 16], ident_b[:, :]))
                    RT.append(_)

                    mv = sV.reserve(1)

                    def _(v, c=c, tbk=tbk, m=m):
                        v.wait_ge(sPR.h, m)
                        sV.attach(v.tensor_copy(
                            gfm[:, c * 128:(c + 1) * 128],
                            tbk[:, :].bitcast(BF16)[0:16, c * 128:(c + 1) * 128]))
                    DV.add(_)
                    gt_ev.append(mv)
                gfm_m = sV.n

                gel_m = {}
                o_mm = {}
                G_ev = {}
                ffn_ev = {}

                for h in range(H):
                    if h % 2 == 0:
                        c = h // 2
                        gsem = sPR if c == 0 else sPE
                        m = gsem.reserve(1)

                        def _(te, c=c, gfm_m=gfm_m, gsem=gsem):
                            te.wait_ge(sV.h, gfm_m)
                            if c >= 1:
                                te.wait_ge(sV.h, G_ev[c - 1])
                            gsem.attach(te.matmul(
                                bank[3][:, :],
                                et_b[0:16, c * 128:(c + 1) * 128],
                                gfm[:, :], start=True, stop=True,
                                skip_group_check=True))
                        (RT.append if c == 0 else DT.add)(_)

                        mv = sV.reserve(1)

                        def _(v, c=c, m=m, gsem=gsem):
                            v.wait_ge(gsem.h, m)
                            sV.attach(v.tensor_copy(G_ch[:, :],
                                                    bank[3][:, :]))
                        DV.add(_)
                        G_ev[c] = mv

                    for hc in range(4):
                        gi = h * 4 + hc
                        m = sPE.reserve(1)

                        def _(te, h=h, hc=hc, gi=gi, pg=prev_gel63[0]):
                            bk = bank[gi % 2]
                            if h == 0 and hc == 0:
                                for s, n in ln2_ready:
                                    te.wait_ge(s.h, n)
                            if gi < 2 and pg:
                                te.wait_ge(sS.h, pg)
                            if gi >= 2:
                                te.wait_ge(sS.h, gel_m[gi - 2])
                            if h == 0 and hc == 0:
                                te.wait_ge(dWB.h, 256)
                            for ec in range(NFC):
                                inst = te.matmul(
                                    bk[:, :],
                                    win_b[:, h * 4096 + ec * 512 + hc * 128:
                                          h * 4096 + ec * 512 + hc * 128 + 128],
                                    ln2_b[:, ec * 512:(ec + 1) * 512],
                                    start=(ec == 0), stop=(ec == NFC - 1),
                                    skip_group_check=True)
                            sPE.attach(inst)
                        if h == 0:
                            rt_drain([1, 3, 4, 0][hc])
                        elif h == 1:
                            rt_drain([2, 2, 1, 0][hc])
                        DT.add(_)

                        ms = sS.reserve(1)

                        def _(sc, h=h, hc=hc, gi=gi, m=m):
                            hbuf = hid0 if h % 2 == 0 else hid1
                            sc.wait_ge(sPE.h, m)
                            if h >= 2:
                                sc.wait_ge(sPE.h, o_mm[h - 2])
                            bk = bank[gi % 2]
                            sS.attach(sc.activation(
                                hbuf[:, hc * 512:(hc + 1) * 512], bk[:, :],
                                AF.Gelu))
                        DS.add(_)
                        gel_m[gi] = ms

                    if h == 0:
                        srt_drain(99)
                    if h == 1:
                        rt_drain(99)
                    mo = sPE.reserve(1)

                    def _(te, h=h, pf=prev_ffn7[0]):
                        hbuf = hid0 if h % 2 == 0 else hid1
                        for hc in range(4):
                            te.wait_ge(sS.h, gel_m[h * 4 + hc])
                        if h < 2 and pf:
                            te.wait_ge(sV.h, pf)
                        if h >= 2:
                            te.wait_ge(sV.h, ffn_ev[h // 2 - 1])
                        for hc in range(4):
                            inst = te.matmul(
                                bank[2][(h % 2) * 64:(h % 2) * 64 + 64, :],
                                wout_b[:, h * 256 + hc * 64: h * 256 + hc * 64 + 64],
                                hbuf[:, hc * 512:(hc + 1) * 512],
                                start=(hc == 0), stop=(hc == 3),
                                skip_group_check=True)
                        sPE.attach(inst)
                    DT.add(_)
                    o_mm[h] = mo

                    if h % 2 == 1:
                        c = h // 2
                        mv = sV.reserve(1)

                        def _(v, c=c, mo=mo, pyl=prev_ypl2[0]):
                            v.wait_ge(sPE.h, mo)
                            v.wait_ge(sV.h, G_ev[c])
                            if c == 0 and pyl:
                                v.wait_ge(sPL.h, pyl)
                            sV.attach(v.tensor_mul(
                                ffn_b[:, c * 512:(c + 1) * 512],
                                bank[2][:, :], G_ch[:, :]))
                        DV.add(_)
                        ffn_ev[c] = mv
                ffn_done = sV.n

                if dbg:
                    def _(sy, b=b, fd=ffn_done, gm=gfm_m):
                        sy.wait_ge(sV.h, fd)
                        dW.inc(sy.dma_start(dbg_ffn.ap()[b], ffn_b[:, :]), 16)
                        dW.inc(sy.dma_start(dbg_hid.ap()[b], hid0[:, :]), 16)
                        sy.wait_ge(sV.h, gm)
                        dW.inc(sy.dma_start(dbg_gtm.ap()[b],
                                            lgtm[:, :]), 16)
                        sy.wait_ge(dW.h, dW.n)
                    blk.sync(_)
                ln3_mark = emit_ln_stats(ffn_b, True, [(sV, ffn_done)])
                wb_pm = []
                for fc in range(NFC):
                    if fc % 2 == 0:
                        mleft = sV.reserve(1)

                        def _(v, fc=fc, lnm=ln3_mark):
                            if fc == 0:
                                v.wait_ge(sV.h, lnm)
                            v.tensor_mul(bsc0[:, 0:512],
                                         ffn_b[:, fc * 512:(fc + 1) * 512],
                                         rbc[:, :])
                            v.tensor_add(bsc0[:, 512:1024], bsc0[:, 0:512],
                                         cbc[:, :])
                            sV.attach(v.tensor_add(
                                y0[:, fc * 512:(fc + 1) * 512],
                                y0[:, fc * 512:(fc + 1) * 512],
                                bsc0[:, 512:1024]))
                        DV.add(_)
                    else:
                        mright = sPL.reserve(1)

                        def _(g, fc=fc, lnm=ln3_mark):
                            if fc == 1:
                                g.wait_ge(sV.h, lnm)
                            g.tensor_mul(bsc1[:, 0:512],
                                         ffn_b[:, fc * 512:(fc + 1) * 512],
                                         rbc[:, :])
                            g.tensor_add(bsc1[:, 512:1024], bsc1[:, 0:512],
                                         cbc[:, :])
                            sPL.attach(g.tensor_add(
                                y0[:, fc * 512:(fc + 1) * 512],
                                y0[:, fc * 512:(fc + 1) * 512],
                                bsc1[:, 512:1024]))
                        DP.add(_)

                        def _(sy, fc=fc, b=b, ml=mleft, mr=mright):
                            sy.wait_ge(sV.h, ml)
                            sy.wait_ge(sPL.h, mr)
                            dstt = outT.ap().rearrange("(c p) t -> p c t",
                                                       p=128)
                            dst = dstt[:, fc - 1:fc + 1, b * P:(b + 1) * P]
                            src = y0[:, (fc - 1) * 512:(fc + 1) * 512]
                            dO.inc(sy.dma_start(
                                dst, src.rearrange("p (c t) -> p c t", c=2)),
                                16)
                        blk.sync(_)
                        wb_pm.append(dO.n)
                ost_marks.append(dO.n)

                if dbg2:
                    def _(sy, b=b, mo=dO.n):
                        sy.wait_ge(dO.h, mo)
                        dW.inc(sy.dma_start(dbg_ffn2.ap()[b], ffn_b[:, :]),
                               16)
                        dW.inc(sy.dma_start(dbg_g2.ap()[b], gfm), 16)
                        sy.wait_ge(dW.h, dW.n)
                    blk.sync(_)
                if b + 1 < NB:
                    def _(sy, b=b, mv=sV.n, mpl=sPL.n, pm=list(wb_pm)):
                        ysrc = y_scr.ap()[b + 1]
                        sy.wait_ge(dO.h, pm[1])
                        dYL.inc(sy.dma_start(y0[:, 0:2048],
                                             ysrc[:, 0:2048]), 16)
                        sy.wait_ge(sV.h, mv)
                        sy.wait_ge(sPL.h, mpl)
                        sy.wait_ge(dO.h, pm[3])
                        dYL.inc(sy.dma_start(y0[:, 2048:4096],
                                             ysrc[:, 2048:4096]), 16)
                    blk.sync(_)
                    yl_marks.append(dYL.n)
                prev_gel63[0] = gel_m[4 * H - 1]
                prev_ffn7[0] = ffn_ev[H // 2 - 1]
                prev_gtev3[0] = gt_ev[NJC - 1]
                prev_zsq[0] = zsq_m
                prev_ypl2[0] = sPL.n
                flush(blk)
        if n_b_iters:
            blkB.__exit__(None, None, None)

        with nc.Block() as blk:
            def _(sy):
                if dbg2 and n_b_iters:
                    dW.inc(sy.dma_start(dbg_wb.ap()[0],
                                        win_b[:, 11 * 4096:12 * 4096]), 16)
                    dW.inc(sy.dma_start(dbg_wb.ap()[1],
                                        win_b[:, 12 * 4096:13 * 4096]), 16)
                    sy.wait_ge(dW.h, dW.n)
                sy.wait_ge(dO.h, dO.n)
            blk.sync(_)

        stB.close()
    return nc


def _prep_weights(inputs):
    Wq = inputs["L0_attn_Wq"]
    Wk = inputs["L0_attn_Wk"]
    Wv = inputs["L0_attn_Wv"]
    Wo = inputs["L0_attn_Wo"]

    def wtile(W):
        return np.ascontiguousarray(
            W.T.reshape(8, 128, 1024).transpose(1, 0, 2).reshape(128, 8192)
        ).astype(np.float32)

    wr = inputs["L0_ffn_w_router"]
    wr_h = np.ascontiguousarray(
        wr.T.reshape(8, 128, 128).transpose(1, 0, 2).reshape(128, 1024)
    ).astype(np.float32)
    ex_in = inputs["L0_ffn_input_experts"]
    win_h = np.ascontiguousarray(
        ex_in.reshape(16, 8, 128, 4, 128).transpose(2, 0, 1, 3, 4)
        .reshape(128, 65536)).astype(np.float32)
    ex_out = inputs["L0_ffn_output_experts"]
    wout_h = np.ascontiguousarray(
        ex_out.reshape(16, 4, 128, 64).transpose(2, 0, 1, 3)
        .reshape(128, 4096)).astype(np.float32)
    bo = inputs["L0_attn_bo"]
    bo_h = np.ascontiguousarray(bo.reshape(8, 128).T).astype(np.float32)
    cT = np.ascontiguousarray(
        inputs["L0_ffn_head_centroids"].T).astype(np.float32)
    temp = np.asarray(inputs["L0_ffn_temperature"],
                      dtype=np.float32).reshape(1, 1)
    et = np.zeros((16, 1024), dtype=np.float32)
    for n in range(16):
        et[n, n * 64:(n + 1) * 64] = 1.0
    ident = np.eye(128, dtype=np.float32)
    maskd = np.triu(np.ones((128, 128), dtype=np.float32))
    return {
        "wq_h": wtile(Wq), "wk_h": wtile(Wk), "wv_h": wtile(Wv),
        "wo_h": wtile(Wo), "wr_hh": wr_h, "win_h": win_h, "wout_hh": wout_h,
        "bo_h": bo_h, "cT_h": cT, "temp_h": temp, "et_h": et,
        "ident_h": ident, "maskd_h": maskd,
    }


def kernel(**inputs):
    x = np.asarray(inputs["L0_attn_x"], dtype=np.float32)
    B, T, _ = x.shape
    shared = _prep_weights(inputs)

    nc = build_nc()

    in_maps = []
    xp_cache = {}
    for c in range(8):
        bi, half = c // 2, c % 2
        xs = np.ascontiguousarray(
            x[bi, half * TOK:(half + 1) * TOK, :].T).astype(np.float32)
        if bi not in xp_cache:
            xp_cache[bi] = np.ascontiguousarray(
                x[bi, 0:TOK, :].T).astype(np.float32)
        m = dict(shared)
        m["xT"] = xs
        m["xpT"] = xp_cache[bi]
        m["gm_h"] = np.full((128, 1), 1.0 if half == 1 else 0.0,
                            dtype=np.float32)
        in_maps.append(m)

    res = run_bass_kernel_spmd(nc, in_maps, core_ids=list(range(8)))

    out = np.empty((B, T, E), dtype=np.float32)
    for c in range(8):
        bi, half = c // 2, c % 2
        out[bi, half * TOK:(half + 1) * TOK, :] = res.results[c]["outT"].T
    return out
